# version 23
# baseline (speedup 1.0000x reference)
import numpy as np
import ml_dtypes
import concourse.bass as bass
import concourse.mybir as mybir
from concourse.bass_utils import run_bass_kernel_spmd

F32 = mybir.dt.float32
BF16 = mybir.dt.bfloat16
Alu = mybir.AluOpType
Act = mybir.ActivationFunctionType
AxX = mybir.AxisListType.X
NPBF = ml_dtypes.bfloat16

FMAX = 3.4028235e38
M23 = 1.5 * 2.0 ** 23
M52 = 1.5 * 2.0 ** 52
P64 = 2.0 ** 64

B, S, D, N, FF = 8, 2048, 1024, 64, 4096
DC = D // 128
G = 4
GW = S // G


class Sched:
    STREAMS = ("pe", "dve", "act", "pool",
               "din0", "din1", "din2", "din3", "din4", "din5", "din6", "din7",
               "dout0", "dout1")

    def __init__(self):
        self.cnt = {s: 0 for s in self.STREAMS}
        self.ops = []
        self.access = {}

    def _deps(self, stream, rds, wrs):
        need = {}
        for (arena, lo, hi, is_w) in [(a, l, h, False) for (a, l, h) in rds] + \
                                     [(a, l, h, True) for (a, l, h) in wrs]:
            lst = self.access.get(arena)
            if not lst:
                continue
            for (l2, h2, s2, c2, w2) in lst:
                if l2 < hi and lo < h2 and (w2 or is_w) and not (s2 == stream == "pe"):
                    if need.get(s2, 0) < c2:
                        need[s2] = c2
        return need

    def marker(self, name):
        self.ops.append(("MARK", name, None))

    def truncate(self, name):
        idx = None
        for i, t in enumerate(self.ops):
            if t[0] == "MARK" and t[1] == name:
                idx = i
                break
        assert idx is not None, name
        self.ops = self.ops[:idx]
        self.cnt = {s: 0 for s in self.STREAMS}
        for rec in self.ops:
            if rec[0] != "MARK":
                self.cnt[rec[0]] += rec[3]
        self.access = {}

    def op(self, stream, fn, reads=(), writes=(), inc=1, extra=(), kind="dma"):
        need = self._deps(stream, reads, writes)
        for s2, c2 in extra:
            if need.get(s2, 0) < c2:
                need[s2] = c2
        c_after = self.cnt[stream] + inc
        self.cnt[stream] = c_after
        self.ops.append((stream, fn, need, inc, kind))
        for (a, lo, hi) in reads:
            self.access.setdefault(a, []).append((lo, hi, stream, c_after, False))
        for (a, lo, hi) in writes:
            lst = self.access.setdefault(a, [])
            lst[:] = [e for e in lst if not (lo <= e[0] and e[1] <= hi)]
            lst.append((lo, hi, stream, c_after, True))

    def emit(self, blk, sems):
        watermark = {}

        waited = set()
        for rec in self.ops:
            if rec[0] == "MARK":
                continue
            for dep, c in rec[2].items():
                waited.add((dep, c))

        def section(eng_obj, streams, tag):
            run = {s: 0 for s in self.STREAMS}
            for rec in self.ops:
                if rec[0] == "MARK":
                    continue
                (stream, fn, need, inc, kind) = rec
                if stream in self.STREAMS:
                    run[stream] += inc
                if stream not in streams:
                    continue
                for dep, c in sorted(need.items()):
                    key = (tag, dep)
                    if watermark.get(key, 0) < c:
                        watermark[key] = c
                        eng_obj.wait_ge(sems[dep], c)
                if kind == "dma":
                    fn(eng_obj, sems[stream])
                else:
                    instr = fn(eng_obj)
                    if kind == "compute_recip" or (
                            (stream, run[stream]) in waited and stream in ("dve", "act")):
                        eng_obj.drain().then_inc(sems[stream], inc)
                    else:
                        instr.then_inc(sems[stream], inc)
            if "dout0" in streams:
                fincnt = {s: 0 for s in self.cnt}
                for rec in self.ops:
                    if rec[0] != "MARK":
                        fincnt[rec[0]] += rec[3]
                for s_ in ("dout0", "dout1", "din0", "din1", "din2", "din3",
                           "din4", "din5", "din6", "din7"):
                    eng_obj.wait_ge(sems[s_], fincnt[s_])

        @blk.sync
        def _(e):
            section(e, ("din0", "din1", "din2", "din3", "din4", "din5", "din6", "din7"),
                    "sync")

        @blk.vector
        def _(e):
            section(e, ("dve",), "dve")

        @blk.scalar
        def _(e):
            section(e, ("act",), "act")

        @blk.gpsimd
        def _(e):
            section(e, ("pool", "dout0", "dout1"), "gp")

        @blk.tensor
        def _(e):
            section(e, ("pe",), "pe")


class Reg:

    def __init__(self, pool_t, pool_name, lo, n, p0=0, p1=128):
        self.t, self.name, self.lo, self.n, self.p0, self.p1 = pool_t, pool_name, lo, n, p0, p1

    def ap(self, c0=0, c1=None, p0=None, p1=None):
        c1 = self.n if c1 is None else c1
        p0 = self.p0 if p0 is None else p0
        p1 = self.p1 if p1 is None else p1
        return self.t[p0:p1, self.lo + c0:self.lo + c1]

    def key(self, c0=0, c1=None):
        c1 = self.n if c1 is None else c1
        return (self.name, self.lo + c0, self.lo + c1)

    def sub(self, off, n):
        return Reg(self.t, self.name, self.lo + off, n, self.p0, self.p1)


def build_core(debug=(), stop=None):
    nc = bass.Bass("TRN2", target_bir_lowering=False)
    ins, outs = {}, {}

    def I(name, shape, dt=F32):
        ins[name] = nc.dram_tensor(name, shape, dt, kind="ExternalInput").ap()
        return ins[name]

    def O(name, shape, dt=F32):
        outs[name] = nc.dram_tensor(name, shape, dt, kind="ExternalOutput").ap()
        return outs[name]

    xT = I("xT", [D, S])
    vpk = I("vpack", [128, 74])
    w1t = I("w1t", [D, FF])
    w2t = I("w2t", [FF, D])
    wxt = I("wxt", [2 * D, D])
    wdtt = I("wdtt", [D, D])
    wft = I("wft", [D, D])
    kwt = I("kwt", [128, DC * N])
    vwt = I("vwt", [128, DC * N])
    owt = I("owt", [N, D])
    v_td = I("v_td", [N])
    v_tf = I("v_tf", [N])
    v_tsg = I("v_tsg", [D])
    v_lnw = I("v_lnw", [D])
    v_lnb = I("v_lnb", [D])
    v_b1 = I("v_b1", [FF])
    v_b2 = I("v_b2", [D])
    v_bx = I("v_bx", [2 * D])
    v_bdt = I("v_bdt", [D])
    v_alog = I("v_alog", [D])
    c_ident = I("c_ident", [128, 128])
    c_oneh = I("c_oneh", [16, S])
    c_ones_bf = I("c_ones_bf", [128, 1], BF16)
    c_ones_f = I("c_ones_f", [128, 1])
    c_onesbc = I("c_onesbc", [1, 128])
    x1sp = nc.dram_tensor("x1spill", [D, S], F32, kind="Internal").ap()

    outT = O("outT", [D, S])
    for dbgspec in debug:
        dnm, dsh = dbgspec[0], dbgspec[1]
        ddt = dbgspec[2] if len(dbgspec) > 2 else F32
        O("dbg_" + dnm, dsh, ddt)

    st = Sched()

    import contextlib
    with contextlib.ExitStack() as EX:
        E = EX.enter_context
        PF_COLS = 21296
        PB_COLS = 63400
        pf_t = E(nc.sbuf_tensor("poolf", [128, PF_COLS], F32))
        pb_t = E(nc.sbuf_tensor("poolb", [128, PB_COLS], BF16))
        psA_t = E(nc.psum_tensor("psA", [128, 2048], F32))
        psB_t = E(nc.psum_tensor("psB", [128, 1024], F32))
        psC_t = E(nc.psum_tensor("psC", [128, 512], F32))
        psT_t = E(nc.psum_tensor("psT", [128, 512], F32))
        sems = {s: E(nc.semaphore("sem_" + s)) for s in Sched.STREAMS}
        blk = E(nc.Block())

        def kA(c0, c1):
            return ("psA", c0, c1)

        def kB(c0, c1):
            return ("psB", c0, c1)

        kC = ("psC", 0, 512)
        kT = ("psT", 0, 512)

        def F_(lo, n, p0=0, p1=128):
            assert lo + n <= PF_COLS
            return Reg(pf_t, "pf", lo, n, p0, p1)

        def B_(lo, n, p0=0, p1=128):
            assert lo + n <= PB_COLS
            return Reg(pb_t, "pb", lo, n, p0, p1)

        xn = B_(0, S * DC)
        ro = B_(16384, S * DC)
        warena = B_(32768, 16384)
        PBH = 49152
        zb = B_(PBH, GW * DC)
        tb2b = B_(PBH + 4096, 2 * GW)
        hq = B_(PBH, GW * 8)
        dump5 = B_(PBH, 2048)
        wxs = B_(PBH, 2048)
        sprow = B_(PBH + 2048, S)
        bxrow = B_(PBH + 4096, 2 * D)
        bdrow = B_(PBH + 6144, D)
        PBS = 56320
        states_bf = B_(PBS, S, p1=64)
        kp_bf = B_(PBS + 2048, N * DC)
        vpw_bf = B_(PBS + 2560, N * DC)
        owt_bf = B_(PBS + 3072, D, p1=64)
        onesb_bf = B_(PBS + 4096, 8)
        sqbf = B_(PBS + 4104, 2048)
        dump1 = ro.sub(0, 2048)
        assert PBS + 5128 <= PB_COLS

        def slot(c, g):
            return (g * DC + c) * GW

        def ro_r(c, g):
            return ro.sub(slot(c, g), GW)

        idreg = F_(0, 128)
        onesbcreg = F_(128, 128)
        VEC = 256
        vp_ = [VEC]

        def vecreg(k=1):
            o = vp_[0]
            vp_[0] += k
            assert vp_[0] <= VEC + 384
            return F_(o, k)

        flag = vecreg(); invflag = vecreg()
        gsig = vecreg(DC); gsig1m = vecreg(DC)
        lnw_v = vecreg(DC); lnb_v = vecreg(DC)
        aneg = vecreg(DC)
        b1_v = vecreg(32)
        b2_v = vecreg(DC)
        tdv = vecreg(); tfv = vecreg()
        etf = vecreg(); negetf = vecreg(); ekb = vecreg()
        wv = vecreg()
        bkv = vecreg(); bvv = vecreg()
        tsg_r = vecreg(DC); alog_r = vecreg(DC)
        ones_f = vecreg()
        scl = {}
        for nm in ("w1", "w2", "wx", "wdt", "wf"):
            scl[nm] = dict(sum=vecreg(), r=vecreg(), negr=vecreg(), scale=vecreg())
        tmpv = vecreg(8)
        c_eps = vecreg(); c_mhalf = vecreg(); c_mone = vecreg()
        s0v = vecreg(); sIv = vecreg()
        pw2 = vecreg(); pw4 = vecreg(); pw8 = vecreg(); pwt = vecreg()
        S_v = vecreg(); F_v = vecreg(); T_v = vecreg(); nT_v = vecreg()
        U_v = vecreg(); nU_v = vecreg(); G_v = vecreg()
        Mb_v = vecreg(); nMb_v = vecreg(); nMb1_v = vecreg()

        PH = 1024
        wstage = F_(PH, 2048)
        srA = F_(PH + 2048, S)
        srB = F_(PH + 4096, S)
        srC = F_(PH + 6144, S)
        free8192 = F_(PH + 8192, 1024)
        tbuf = F_(PH + 9216, S)
        tbuf2 = F_(PH + 11264, 512)
        kvf = F_(PH + 11776, S, p1=64)
        wdec = F_(PH + 13824, S, p1=64)
        stf = F_(PH + 15872, S, p1=64)
        ekbuf = F_(PH + 17920, GW, p1=64)
        fold = F_(PH + 11776, S)
        xf1 = F_(PH + 13824, S)
        amaxt = F_(PH + 15872, 16)
        sptk = F_(PH + 15888, 16)
        rhot = F_(PH + 15904, 16)
        onehreg = F_(PH + 15920, S, p1=16)
        v16sp = F_(PH + 17968, 128, p1=16)
        v16rh = F_(PH + 18096, 128, p1=16)
        spbc = F_(PH + 11776, S)
        wsxr = F_(PH + 18224, 2048)
        wsdr = F_(PH + 15920, 2048)
        assert PH + 18224 + 2048 <= PF_COLS
        Bs = [F_(PH + 2048 + i * S, S) for i in range(5)]
        zerot = F_(PH + 12288, S)
        tern7a = F_(PH + 14336, 512)
        tern7b = F_(PH + 14848, 512)

        din_rot = [0]
        dout_rot = [0]

        def dma_in(dst_ap, src_ap, wkey, rkeys=(), slow=False):
            s_ = "din%d" % (din_rot[0] % 4)
            din_rot[0] += 1
            prev = st.cnt[s_]
            def fn(e, sem, d=dst_ap, s=src_ap, sl=slow):
                e.dma_start(out=d, in_=s, allow_slow_non_contiguous=sl).then_inc(sem, 16)
            st.op(s_, fn, reads=rkeys, writes=[wkey], inc=16, extra=[(s_, prev)])

        def dma_io(dst_ap, src_ap, rkeys, wkeys):
            s_ = "dout%d" % (dout_rot[0] % 2)
            dout_rot[0] += 1
            prev = st.cnt[s_]
            def fn(e, sem, d=dst_ap, s=src_ap):
                e.dma_start(out=d, in_=s).then_inc(sem, 16)
            st.op(s_, fn, reads=rkeys, writes=wkeys, inc=16, extra=[(s_, prev)])

        def dma_out(dst_ap, src_ap, rkey, hbmkey=None):
            dma_io(dst_ap, src_ap, [rkey], [hbmkey] if hbmkey else [])

        def dma_rd(dst_ap, src_ap, wkey, hbmkey=None):
            dma_io(dst_ap, src_ap, [hbmkey] if hbmkey else [], [wkey])

        def vop(fn, reads=(), writes=()):
            st.op("dve", fn, reads, writes, kind="compute")

        def aop(fn, reads=(), writes=()):
            st.op("act", fn, reads, writes, kind="compute")

        def pop_(fn, reads=(), writes=()):
            def f2(e, sem, fn=fn):
                fn(e).then_inc(sem, 1)
            st.op("pe", f2, reads, writes, kind="dma")

        def top_(fn, reads=(), writes=()):
            def f2(e, sem, fn=fn):
                fn(e)
                e.drain().then_inc(sem, 1)
            st.op("pe", f2, reads, writes, kind="dma")

        def rop(out_ap, in_ap, reads, writes):
            def fn(e, o=out_ap, i=in_ap):
                e.drain()
                r = e.reciprocal(o, i)
                return r
            st.op("dve", fn, reads, writes, kind="compute_recip")

        def ppv(reg, c=0, p0=None, p1=None):
            p0 = reg.p0 if p0 is None else p0
            p1 = reg.p1 if p1 is None else p1
            return reg.t[p0:p1, reg.lo + c:reg.lo + c + 1]

        def r0(reg, a=0, b=None):
            b = reg.n if b is None else b
            return reg.t[0:1, reg.lo + a:reg.lo + b]

        dma_in(idreg.ap(), c_ident, idreg.key())
        dma_in(r0(onesbcreg), c_onesbc, onesbcreg.key())
        dma_in(onesb_bf.ap(0, 1), c_ones_bf, onesb_bf.key())
        dma_in(ones_f.ap(), c_ones_f, ones_f.key())

        def load_pp(reg, hbm, p=128):
            dma_in(reg.ap(p0=0, p1=p), hbm.rearrange("(c p) -> p c", p=p), reg.key(), slow=True)

        dma_in(tbuf.ap(0, 74), vpk, tbuf.key(0, 74))
        vp_off = 0
        for reg, w in ((tsg_r, DC), (lnw_v, DC), (lnb_v, DC), (alog_r, DC),
                       (b2_v, DC), (b1_v, 32), (tdv, 1), (tfv, 1)):
            vop(lambda e, reg=reg, o=vp_off, w=w: e.tensor_copy(reg.ap(), tbuf.ap(o, o + w)),
                [tbuf.key(vp_off, vp_off + w)], [reg.key()])
            vp_off += w

        vop(lambda e: e.memset(c_eps.ap(), 1e-5), writes=[c_eps.key()])
        vop(lambda e: e.memset(c_mhalf.ap(), -0.5), writes=[c_mhalf.key()])
        vop(lambda e: e.memset(c_mone.ap(), -1.0), writes=[c_mone.key()])
        vop(lambda e: e.memset(flag.ap(), 2.0 ** -40), writes=[flag.key()])
        vop(lambda e: e.tensor_scalar(flag.ap(), flag.ap(), 1.0, -1.0, Alu.add, Alu.add),
            [flag.key()], [flag.key()])
        vop(lambda e: e.tensor_scalar(flag.ap(), flag.ap(), 2.0 ** 40, None, Alu.mult),
            [flag.key()], [flag.key()])
        vop(lambda e: e.tensor_scalar(invflag.ap(), flag.ap(), 1.0, -1.0, Alu.subtract, Alu.mult),
            [flag.key()], [invflag.key()])
        vop(lambda e: e.tensor_scalar(s0v.ap(), flag.ap(), 1.8446744073709552e19, 1.0,
                                      Alu.mult, Alu.add), [flag.key()], [s0v.key()])
        vop(lambda e: e.tensor_mul(pw2.ap(), s0v.ap(), s0v.ap()), [s0v.key()], [pw2.key()])
        vop(lambda e: e.tensor_mul(pw4.ap(), pw2.ap(), pw2.ap()), [pw2.key()], [pw4.key()])
        vop(lambda e: e.tensor_mul(pw8.ap(), pw4.ap(), pw4.ap()), [pw4.key()], [pw8.key()])
        vop(lambda e: e.tensor_scalar(S_v.ap(), pw8.ap(), 2.0 ** 64, None, Alu.mult),
            [pw8.key()], [S_v.key()])
        vop(lambda e: e.tensor_scalar(G_v.ap(), pw8.ap(), 2.0 ** 52, None, Alu.mult),
            [pw8.key()], [G_v.key()])
        vop(lambda e: e.tensor_mul(pwt.ap(), pw4.ap(), pw2.ap()), [pw4.key(), pw2.key()], [pwt.key()])
        vop(lambda e: e.tensor_scalar(T_v.ap(), pwt.ap(), float((2 - 2.0 ** -23) * 2.0 ** 63),
                                      None, Alu.mult), [pwt.key()], [T_v.key()])
        vop(lambda e: e.tensor_scalar(nT_v.ap(), T_v.ap(), -1.0, None, Alu.mult),
            [T_v.key()], [nT_v.key()])
        vop(lambda e: e.tensor_mul(pwt.ap(), pw8.ap(), pwt.ap()), [pw8.key(), pwt.key()], [pwt.key()])
        vop(lambda e: e.tensor_scalar(U_v.ap(), pwt.ap(), float((2 - 2.0 ** -23) * 2.0 ** 115),
                                      None, Alu.mult), [pwt.key()], [U_v.key()])
        vop(lambda e: e.tensor_scalar(nU_v.ap(), U_v.ap(), -1.0, None, Alu.mult),
            [U_v.key()], [nU_v.key()])
        rop(sIv.ap(), s0v.ap(), [s0v.key()], [sIv.key()])
        vop(lambda e: e.tensor_mul(pwt.ap(), sIv.ap(), sIv.ap()), [sIv.key()], [pwt.key()])
        vop(lambda e: e.tensor_mul(pw2.ap(), pwt.ap(), pwt.ap()), [pwt.key()], [pw2.key()])
        vop(lambda e: e.tensor_mul(pw4.ap(), pw2.ap(), pw2.ap()), [pw2.key()], [pw4.key()])
        vop(lambda e: e.tensor_mul(pw4.ap(), pw4.ap(), pw2.ap()), [pw2.key(), pw4.key()], [pw4.key()])
        vop(lambda e: e.tensor_mul(pw4.ap(), pw4.ap(), sIv.ap()), [sIv.key(), pw4.key()], [pw4.key()])
        vop(lambda e: e.tensor_scalar(F_v.ap(), pw4.ap(), 2.0 ** -125, None, Alu.mult),
            [pw4.key()], [F_v.key()])
        vop(lambda e: e.tensor_scalar(Mb_v.ap(), flag.ap(), float(np.float32(1.5 * 2.0 ** 52 - M23)),
                                      M23, Alu.mult, Alu.add), [flag.key()], [Mb_v.key()])
        vop(lambda e: e.tensor_scalar(nMb_v.ap(), Mb_v.ap(), -1.0, None, Alu.mult),
            [Mb_v.key()], [nMb_v.key()])
        vop(lambda e: e.tensor_scalar(nMb1_v.ap(), Mb_v.ap(), -1.0, 1.0, Alu.mult, Alu.add),
            [Mb_v.key()], [nMb1_v.key()])
        aop(lambda e: e.activation(gsig.ap(), tsg_r.ap(), Act.Sigmoid), [tsg_r.key()], [gsig.key()])
        vop(lambda e: e.tensor_scalar(gsig1m.ap(), gsig.ap(), 1.0, -1.0, Alu.subtract, Alu.mult),
            [gsig.key()], [gsig1m.key()])
        aop(lambda e: e.activation(tmpv.ap(0, DC), alog_r.ap(), Act.Abs), [alog_r.key()], [tmpv.key(0, DC)])
        aop(lambda e: e.activation(tmpv.ap(0, DC), tmpv.ap(0, DC), Act.Exp, scale=-1.0),
            [tmpv.key(0, DC)], [tmpv.key(0, DC)])
        aop(lambda e: e.activation(tmpv.ap(0, DC), tmpv.ap(0, DC), Act.Ln, bias=1.0),
            [tmpv.key(0, DC)], [tmpv.key(0, DC)])
        vop(lambda e: e.scalar_tensor_tensor(aneg.ap(), alog_r.ap(), 0.0, tmpv.ap(0, DC), Alu.max, Alu.add),
            [alog_r.key(), tmpv.key(0, DC)], [aneg.key()])
        vop(lambda e: e.tensor_scalar(aneg.ap(), aneg.ap(), -1.0, None, Alu.mult),
            [aneg.key()], [aneg.key()])
        aop(lambda e: e.activation(ppv(wv, p0=0, p1=64), ppv(tdv, p0=0, p1=64), Act.Exp),
            [tdv.key()], [wv.key()])
        aop(lambda e: e.activation(ppv(etf, p0=0, p1=64), ppv(tfv, p0=0, p1=64), Act.Exp),
            [tfv.key()], [etf.key()])
        vop(lambda e: e.tensor_scalar(ppv(negetf, p0=0, p1=64), ppv(etf, p0=0, p1=64), -1.0, None, Alu.mult),
            [etf.key()], [negetf.key()])

        snap_i = [0]

        def snap_vec(tag):
            if not any(d[0] == "snap" for d in debug):
                return
            i = snap_i[0]
            snap_i[0] += 1
            vop(lambda e, i=i: e.tensor_copy(srB.ap(0, 640), pf_t[:, 0:640]),
                [("pf", 0, 640)], [srB.key(0, 640)])
            dma_out(outs["dbg_snap"][:, i * 640:(i + 1) * 640], srB.ap(0, 640), srB.key(0, 640))

        snap_vec("after_aneg")
        kvstg = F_(PH + 2048, 2 * DC * N)
        dma_in(kvstg.ap(0, DC * N), kwt, kvstg.key(0, DC * N))
        dma_in(kvstg.ap(DC * N, 2 * DC * N), vwt, kvstg.key(DC * N, 2 * DC * N))
        for c in range(DC):
            ck = kvstg.key(c * N, (c + 1) * N)
            vop(lambda e, c=c: e.tensor_scalar(kp_bf.ap(c * N, (c + 1) * N),
                                               kvstg.ap(c * N, (c + 1) * N),
                                               ppv(lnw_v, c), None, Alu.mult),
                [ck, lnw_v.key()], [kp_bf.key(c * N, (c + 1) * N)])
            pop_(lambda e, c=c: e.matmul(psC_t[0:64, 0:1], kvstg.ap(c * N, (c + 1) * N),
                                         ppv(lnb_v, c), start=(c == 0), stop=(c == DC - 1)),
                 [ck, lnb_v.key()], [kC])
        vop(lambda e: e.tensor_copy(ppv(bkv, p0=0, p1=64), psC_t[0:64, 0:1]), [kC], [bkv.key()])
        for c in range(DC):
            ck = kvstg.key(DC * N + c * N, DC * N + (c + 1) * N)
            vop(lambda e, c=c: e.tensor_scalar(vpw_bf.ap(c * N, (c + 1) * N),
                                               kvstg.ap(DC * N + c * N, DC * N + (c + 1) * N),
                                               ppv(lnw_v, c), None, Alu.mult),
                [ck, lnw_v.key()], [vpw_bf.key(c * N, (c + 1) * N)])
            pop_(lambda e, c=c: e.matmul(psT_t[0:64, 0:1],
                                         kvstg.ap(DC * N + c * N, DC * N + (c + 1) * N),
                                         ppv(lnb_v, c), start=(c == 0), stop=(c == DC - 1)),
                 [ck, lnb_v.key()], [kT])
        vop(lambda e: e.tensor_copy(ppv(bvv, p0=0, p1=64), psT_t[0:64, 0:1]), [kT], [bvv.key()])
        vop(lambda e: e.tensor_mul(ppv(ekb, p0=0, p1=64), ppv(negetf, p0=0, p1=64),
                                   ppv(bkv, p0=0, p1=64)), [negetf.key(), bkv.key()], [ekb.key()])
        owtf = F_(PH + 2048 + N, D)
        dma_in(owtf.ap(p1=64), owt, owtf.key())
        vop(lambda e: e.tensor_copy(owt_bf.ap(), owtf.ap(p1=64)), [owtf.key()], [owt_bf.key()])

        absstg = Reg(pf_t, "pf", PH + 13824, 2048)

        def stream_abs_gen(hbm, rows, cols, sumreg, piece, dumpreg, first, stgreg=None):
            stgreg = absstg if stgreg is None else stgreg
            nchunk, npiece = rows // 128, cols // piece
            idx = 0
            for c in range(nchunk):
                for p in range(npiece):
                    stg = stgreg.sub((idx % 2) * piece, piece)
                    dma_in(stg.ap(), hbm[c * 128:(c + 1) * 128, p * piece:(p + 1) * piece], stg.key())
                    aop(lambda e, stg=stg: e.activation(dumpreg.ap(0, stg.n), stg.ap(), Act.Abs,
                                                        accum_out=tmpv.ap(6, 7)),
                        [stg.key()], [dumpreg.key(0, stg.n), tmpv.key(6, 7)])
                    if idx == 0 and first:
                        vop(lambda e: e.tensor_copy(sumreg.ap(), tmpv.ap(6, 7)),
                            [tmpv.key(6, 7)], [sumreg.key()])
                    else:
                        vop(lambda e: e.tensor_add(sumreg.ap(), sumreg.ap(), tmpv.ap(6, 7)),
                            [tmpv.key(6, 7), sumreg.key()], [sumreg.key()])
                    idx += 1
                    yield

        def stream_abs(hbm, rows, cols, sumreg, piece, dumpreg):
            for _ in stream_abs_gen(hbm, rows, cols, sumreg, piece, dumpreg, True):
                pass

        pump_gen = [None]

        def pump(n=1):
            g_ = pump_gen[0]
            if g_ is None:
                return
            for _ in range(n):
                if next(g_, "END") == "END":
                    pump_gen[0] = None
                    return

        def finish_scale(nm, numel, eps, mode):
            d = scl[nm]
            pop_(lambda e: e.matmul(psC_t[0:1, 0:1], d["sum"].ap(), ones_f.ap(),
                                    start=True, stop=True), [d["sum"].key(), ones_f.key()], [kC])
            vop(lambda e: e.tensor_copy(tmpv.ap(7, 8, 0, 1), psC_t[0:1, 0:1]), [kC], [tmpv.key(7, 8)])
            pop_(lambda e: e.matmul(psC_t[:, 0:1], r0(onesbcreg), tmpv.ap(7, 8, 0, 1),
                                    start=True, stop=True), [tmpv.key(7, 8), onesbcreg.key()], [kC])
            op1 = Alu.max if mode == "max" else Alu.add
            vop(lambda e: e.tensor_scalar(d["scale"].ap(), psC_t[:, 0:1], 1.0 / numel, eps,
                                          Alu.mult, op1), [kC], [d["scale"].key()])
            rop(d["r"].ap(), d["scale"].ap(), [d["scale"].key()], [d["r"].key()])
            vop(lambda e: e.tensor_scalar(d["negr"].ap(), d["r"].ap(), -1.0, None, Alu.mult),
                [d["r"].key()], [d["negr"].key()])

        def ternarize(dst_reg, dst_off, stg, off, ncols, rv, negrv, sa, sb):
            wk = stg.key(off, off + ncols)
            dk = dst_reg.key(dst_off, dst_off + ncols)
            aop(lambda e: e.activation(sa.ap(0, ncols), stg.ap(off, off + ncols), Act.Relu,
                                       scale=rv.ap(), bias=Mb_v.ap()),
                [wk, rv.key(), Mb_v.key()], [sa.key(0, ncols)])
            aop(lambda e: e.activation(sa.ap(0, ncols), sa.ap(0, ncols), Act.Relu,
                                       bias=nMb1_v.ap()),
                [sa.key(0, ncols), nMb1_v.key()], [sa.key(0, ncols)])
            vop(lambda e: e.tensor_scalar(dst_reg.ap(dst_off, dst_off + ncols),
                                          sa.ap(0, ncols), 2.0, -1.0, Alu.min, Alu.add),
                [sa.key(0, ncols)], [dk])

        snap_vec("after_p0")
        import itertools
        pump_gen[0] = itertools.chain(
            stream_abs_gen(w1t, D, FF, scl["w1"]["sum"], 512, dump1, True, stgreg=free8192),
            stream_abs_gen(w2t, FF, D, scl["w2"]["sum"], 512, dump1, True, stgreg=free8192))
        st.marker("p1")

        if any(d[0] == "vec0" for d in debug):
            vop(lambda e: e.tensor_copy(srA.ap(0, 640), pf_t[:, 0:640]), [("pf", 0, 640)], [srA.key(0, 640)])
            dma_out(outs["dbg_vec0"], srA.ap(0, 640), srA.key(0, 640))
        def st_sq(par):
            return (sqbf, par * GW) if par < 2 else (hq, (par - 2) * GW)

        def st_cs(par):
            return (sqbf, 1024 + par * GW) if par < 2 else (hq, 2048 + (par - 2) * GW)

        def stats_piece(c, g, par, sap, sk, bfap, bfk, alt=False):
            sqreg, sq0 = st_sq(par)
            sq1 = sq0 + GW
            aop(lambda e, s=sap, a=sq0, b=sq1, r=sqreg: e.activation(r.ap(a, b), s,
                                                                     Act.Square),
                [sk], [sqreg.key(sq0, sq1)])
            if bfap is None:
                csreg, cs0 = st_cs(par)
                cs1 = cs0 + GW
                vop(lambda e, s=sap, a=cs0, b=cs1, r=csreg: e.tensor_copy(r.ap(a, b), s),
                    [sk], [csreg.key(cs0, cs1)])
                bfap, bfk = csreg.ap(cs0, cs1), csreg.key(cs0, cs1)
            pop_(lambda e, c=c, g=g, x=bfap: e.matmul(
                psA_t[0:1, g * 512:(g + 1) * 512], onesb_bf.ap(0, 1), x,
                start=(c == 0), stop=(c == DC - 1)),
                [bfk, onesb_bf.key()], [kA(g * 512, (g + 1) * 512)])
            if alt:
                tgt, tk = psA_t[32:33, g * 512:(g + 1) * 512], kA(g * 512, (g + 1) * 512)
            elif g < 2:
                tgt, tk = psB_t[0:1, g * 512:(g + 1) * 512], kB(g * 512, (g + 1) * 512)
            elif g == 2:
                tgt, tk = psC_t[0:1, :], kC
            else:
                tgt, tk = psT_t[0:1, :], kT
            pop_(lambda e, c=c, t=tgt, a=sq0, b=sq1, r=sqreg: e.matmul(
                t, onesb_bf.ap(0, 1), r.ap(a, b),
                start=(c == 0), stop=(c == DC - 1)),
                [sqreg.key(sq0, sq1), onesb_bf.key()], [tk])

        def stats_final(alt=False):
            vop(lambda e: e.tensor_copy(r0(srA), psA_t[0:1, :]), [kA(0, 2048)], [srA.key()])
            if alt:
                vop(lambda e: e.tensor_copy(r0(srB), psA_t[32:33, :]), [kA(0, 2048)], [srB.key()])
            else:
                vop(lambda e: e.tensor_copy(r0(srB, 0, 1024), psB_t[0:1, :]), [kB(0, 1024)],
                    [srB.key(0, 1024)])
                vop(lambda e: e.tensor_copy(r0(srB, 1024, 1536), psC_t[0:1, :]), [kC],
                    [srB.key(1024, 1536)])
                vop(lambda e: e.tensor_copy(r0(srB, 1536, 2048), psT_t[0:1, :]), [kT],
                    [srB.key(1536, 2048)])
            vop(lambda e: e.tensor_scalar(r0(srA), r0(srA), 1.0 / D, None, Alu.mult),
                [srA.key()], [srA.key()])
            vop(lambda e: e.tensor_scalar(r0(srB), r0(srB), 1.0 / D, None, Alu.mult),
                [srB.key()], [srB.key()])
            vop(lambda e: e.tensor_mul(r0(srC), r0(srA), r0(srA)), [srA.key()], [srC.key()])
            vop(lambda e: e.tensor_sub(r0(srB), r0(srB), r0(srC)), [srB.key(), srC.key()], [srB.key()])
            aop(lambda e: e.activation(r0(srB), r0(srB), Act.Ln, bias=ppv(c_eps, 0, 0, 1)),
                [srB.key(), c_eps.key()], [srB.key()])
            aop(lambda e: e.activation(r0(srB), r0(srB), Act.Exp, scale=ppv(c_mhalf, 0, 0, 1)),
                [srB.key(), c_mhalf.key()], [srB.key()])
            vop(lambda e: e.tensor_mul(r0(srC), r0(srA), r0(srB)), [srA.key(), srB.key()], [srC.key()])
            vop(lambda e: e.tensor_scalar(r0(srC), r0(srC), -1.0, None, Alu.mult),
                [srC.key()], [srC.key()])

        def stats_from(get_src, nslots=2):
            for g in range(G):
                for c in range(DC):
                    pump(1)
                    par = (g * DC + c) % nslots
                    sap, sk, bfap, bfk = get_src(c, g, par)
                    stats_piece(c, g, par, sap, sk, bfap, bfk)
            stats_final()

        def bcast_AB(g):
            pop_(lambda e, g=g: e.matmul(psC_t[:, :], r0(onesbcreg), r0(srB, g * GW, (g + 1) * GW),
                                         start=True, stop=True), [srB.key(), onesbcreg.key()], [kC])
            pop_(lambda e, g=g: e.matmul(psT_t[:, :], r0(onesbcreg), r0(srC, g * GW, (g + 1) * GW),
                                         start=True, stop=True), [srC.key(), onesbcreg.key()], [kT])
            return psC_t[:, :], kC, psT_t[:, :], kT

        def xpiece(c, g, par):
            stg = wstage.sub(par * 512, GW)
            dma_in(stg.ap(), xT[c * 128:(c + 1) * 128, g * GW:(g + 1) * GW], stg.key())
            return stg

        pi = [0]
        _xpl = [(c, g) for g in range(G) for c in range(DC)]
        _xis = [0]

        def x_prefetch(upto):
            while _xis[0] < min(upto, len(_xpl)):
                i = _xis[0]
                c, g = _xpl[i]
                stg = wstage.sub((i % 4) * 512, GW)
                dma_in(stg.ap(), xT[c * 128:(c + 1) * 128, g * GW:(g + 1) * GW], stg.key())
                _xis[0] += 1

        def xnext():
            i = pi[0]
            x_prefetch(i + 4)
            stg = wstage.sub((i % 4) * 512, GW)
            pi[0] += 1
            return stg

        def ln1_src(c, g, par):
            stg = xnext()
            a, b = c * S + g * GW, c * S + (g + 1) * GW
            vop(lambda e, s=stg, a=a, b=b: e.tensor_copy(xn.ap(a, b), s.ap()),
                [stg.key()], [xn.key(a, b)])
            return (xn.ap(a, b), xn.key(a, b), xn.ap(a, b), xn.key(a, b))

        stats_from(ln1_src, nslots=4)
        for g in range(G):
            aap, ak, bap, bk = bcast_AB(g)
            for c in range(DC):
                pump(1)
                dk = xn.key(c * S + g * GW, c * S + (g + 1) * GW)
                tb0 = (c % 4) * 512
                vop(lambda e, c=c, g=g, a=aap, tb0=tb0: e.tensor_mul(
                    tbuf.ap(tb0, tb0 + GW), a, xn.ap(c * S + g * GW, c * S + (g + 1) * GW)),
                    [dk, ak], [tbuf.key(tb0, tb0 + GW)])
                vop(lambda e, c=c, g=g, b=bap, tb0=tb0: e.tensor_add(
                    xn.ap(c * S + g * GW, c * S + (g + 1) * GW), b, tbuf.ap(tb0, tb0 + GW)),
                    [tbuf.key(tb0, tb0 + GW), bk], [dk])

        if any(d[0] == "xn" for d in debug):
            for c in range(DC):
                vop(lambda e, c=c: e.tensor_copy(tbuf.ap(), xn.ap(c * S, (c + 1) * S)),
                    [xn.key(c * S, (c + 1) * S)], [tbuf.key()])
                dma_out(outs["dbg_xn"][c * 128:(c + 1) * 128, :], tbuf.ap(), tbuf.key())

        def mix_group(c, g):
            src_off = (g * GW + 1024) % S
            xck = xn.key(c * S + src_off, c * S + src_off + GW)
            xnk = xn.key(c * S + g * GW, c * S + (g + 1) * GW)
            dap, dk = tbuf.ap(0, GW), tbuf.key(0, GW)
            vop(lambda e, c=c, x=xn.ap(c * S + src_off, c * S + src_off + GW), d=dap:
                e.tensor_scalar(d, x, ppv(gsig, c), None, Alu.mult), [xck, gsig.key()], [dk])
            vop(lambda e, c=c, x=xn.ap(c * S + g * GW, c * S + (g + 1) * GW), d=dap:
                e.scalar_tensor_tensor(d, x, ppv(gsig1m, c), d, Alu.mult, Alu.add),
                [xnk, gsig1m.key(), dk], [dk])
            return dap, dk

        def shift_src(c, g, par):
            dap, dk = mix_group(c, g)
            t0, t1 = par * GW, (par + 1) * GW
            vop(lambda e, s=dap, a=t0, b=t1: e.tensor_copy(tb2b.ap(a, b), s),
                [dk], [tb2b.key(t0, t1)])
            return (dap, dk, tb2b.ap(t0, t1), tb2b.key(t0, t1))

        stats_from(shift_src)
        vop(lambda e: e.tensor_copy(wdec.ap(), ppv(wv, p0=0, p1=64).broadcast_to([64, S])),
            [wv.key()], [wdec.key()])
        for g in range(G):
            aap, ak, bap, bk = bcast_AB(g)
            for c in range(DC):
                pump(1)
                dap, dk = mix_group(c, g)
                vop(lambda e, d=dap, a=aap: e.tensor_mul(tbuf2.ap(0, GW), a, d),
                    [dk, ak], [tbuf2.key()])
                zk = zb.key(c * GW, (c + 1) * GW)
                vop(lambda e, c=c, b=bap: e.tensor_add(zb.ap(c * GW, (c + 1) * GW), b, tbuf2.ap(0, GW)),
                    [tbuf2.key(), bk], [zk])
            for c in range(DC):
                pop_(lambda e, c=c: e.matmul(psC_t[0:64, :], kp_bf.ap(c * N, (c + 1) * N),
                                             zb.ap(c * GW, (c + 1) * GW),
                                             start=(c == 0), stop=(c == DC - 1)),
                     [zb.key(c * GW, (c + 1) * GW), kp_bf.key(c * N, (c + 1) * N)], [kC])
            for c in range(DC):
                pop_(lambda e, c=c: e.matmul(psT_t[0:64, :], vpw_bf.ap(c * N, (c + 1) * N),
                                             zb.ap(c * GW, (c + 1) * GW),
                                             start=(c == 0), stop=(c == DC - 1)),
                     [zb.key(c * GW, (c + 1) * GW), vpw_bf.key(c * N, (c + 1) * N)], [kT])
            aop(lambda e: e.activation(ekbuf.ap(), psC_t[0:64, :], Act.Exp,
                                       bias=ppv(ekb, p0=0, p1=64), scale=ppv(negetf, p0=0, p1=64)),
                [kC, ekb.key(), negetf.key()], [ekbuf.key()])
            vop(lambda e, g=g: e.scalar_tensor_tensor(
                kvf.ap(g * GW, (g + 1) * GW), psT_t[0:64, :], ppv(bvv, p0=0, p1=64),
                ekbuf.ap(), Alu.add, Alu.mult),
                [kT, bvv.key(), ekbuf.key()], [kvf.key(g * GW, (g + 1) * GW)])
        vop(lambda e: e.tensor_tensor_scan(stf.ap(), wdec.ap(), kvf.ap(), 0.0, Alu.mult, Alu.add),
            [wdec.key(), kvf.key()], [stf.key()])
        vop(lambda e: e.tensor_copy(states_bf.ap(), stf.ap()), [stf.key()], [states_bf.key()])
        if any(d[0] == "states" for d in debug):
            dma_out(outs["dbg_states"], stf.ap(), stf.key())

        pump(200)
        assert pump_gen[0] is None, "W1/W2 abs pump must be drained here (frees free8192)" 
        finish_scale("w1", D * FF, 1e-8, "max")
        finish_scale("w2", D * FF, 1e-8, "max")
        pump_gen[0] = itertools.chain(
            stream_abs_gen(wxt, 2 * D, D, scl["wx"]["sum"], 1024, sqbf, True),
            stream_abs_gen(wdtt, D, D, scl["wdt"]["sum"], 1024, sqbf, True),
            stream_abs_gen(wft, D, D, scl["wf"]["sum"], 1024, sqbf, True))
        QE = FF // 8

        def tern_thunks(e8):
            buf = e8 % 2
            w1q8 = warena.sub(buf * 8192, 4096)
            w2q8 = warena.sub(buf * 8192 + 4096, 4096)
            th = []
            for c in range(DC):
                def t1(c=c, buf=buf, w1q8=w1q8, e8=e8):
                    stg = wstage.sub((c % 2) * 1024, 512)
                    dma_in(stg.ap(0, 512), w1t[c * 128:(c + 1) * 128, e8 * QE:(e8 + 1) * QE],
                           stg.key(0, 512))
                    ternarize(w1q8, c * 512, stg, 0, 512, scl["w1"]["r"], scl["w1"]["negr"],
                              free8192.sub((c % 2) * 512, 512), tbuf2)
                th.append(t1)
            for j4 in range(4):
                def t2(j4=j4, buf=buf, w2q8=w2q8, e8=e8):
                    fc = e8 * 4 + j4
                    stg = wstage.sub((j4 % 2) * 1024, 1024)
                    dma_in(stg.ap(), w2t[fc * 128:(fc + 1) * 128, :], stg.key())
                    ternarize(w2q8, j4 * 1024, stg, 0, 1024, scl["w2"]["r"], scl["w2"]["negr"],
                              tbuf.sub((j4 % 2) * 1024, 1024), tbuf2)
                th.append(t2)
            return th

        for th in tern_thunks(0):
            th()
        for e8 in range(8):
            buf = e8 % 2
            w1q8 = warena.sub(buf * 8192, 4096)
            w2q8 = warena.sub(buf * 8192 + 4096, 4096)
            h8 = hq.sub(buf * 2048, 2048)
            nxt = tern_thunks(e8 + 1) if e8 + 1 < 8 else []
            for g in range(G):
                for th in nxt[g * 3:(g + 1) * 3]:
                    th()
                pump(1)
                for j in range(4):
                    for c in range(DC):
                        pop_(lambda e, j=j, c=c, g=g, w1q8=w1q8: e.matmul(
                            psA_t[:, j * 512:(j + 1) * 512],
                            w1q8.ap(c * 512 + j * 128, c * 512 + (j + 1) * 128),
                            xn.ap(c * S + g * GW, c * S + (g + 1) * GW),
                            start=(c == 0), stop=(c == DC - 1)),
                            [w1q8.key(c * 512 + j * 128, c * 512 + (j + 1) * 128),
                             xn.key(c * S + g * GW, c * S + (g + 1) * GW)],
                            [kA(j * 512, (j + 1) * 512)])
                for j in range(4):
                    aop(lambda e, j=j, e8=e8, h8=h8: e.activation(
                        h8.ap(j * GW, (j + 1) * GW), psA_t[:, j * 512:(j + 1) * 512],
                        Act.Gelu, bias=ppv(b1_v, e8 * 4 + j)),
                        [kA(j * 512, (j + 1) * 512), b1_v.key()],
                        [h8.key(j * GW, (j + 1) * GW)])
                for c in range(DC):
                    pbk = (c % 2) * 512
                    for j in range(4):
                        pop_(lambda e, c=c, j=j, pbk=pbk, w2q8=w2q8, h8=h8: e.matmul(
                            psB_t[:, pbk:pbk + 512],
                            w2q8.ap(j * 1024 + c * 128, j * 1024 + (c + 1) * 128),
                            h8.ap(j * GW, (j + 1) * GW),
                            start=(j == 0), stop=(j == 3)),
                            [w2q8.key(j * 1024 + c * 128, j * 1024 + (c + 1) * 128),
                             h8.key(j * GW, (j + 1) * GW)], [kB(pbk, pbk + 512)])
                    oa = ro_r(c, g)
                    if e8 == 0:
                        vop(lambda e, c=c, pbk=pbk, oa=oa: e.tensor_scalar(
                            oa.ap(), psB_t[:, pbk:pbk + 512], ppv(b2_v, c), None, Alu.add),
                            [kB(pbk, pbk + 512), b2_v.key()], [oa.key()])
                    else:
                        vop(lambda e, pbk=pbk, oa=oa: e.tensor_add(
                            oa.ap(), psB_t[:, pbk:pbk + 512], oa.ap()),
                            [kB(pbk, pbk + 512), oa.key()], [oa.key()])
        pump(200)
        if any(d[0] == "ffn" for d in debug):
            for g in range(G):
                for c in range(DC):
                    oa = ro_r(c, g)
                    dma_out(outs["dbg_ffn"][c * 128:(c + 1) * 128, g * GW:(g + 1) * GW],
                            oa.ap(), oa.key())
        ep_xslots = [wstage.sub(p * 512, GW) for p in range(4)] + \
                    [free8192.sub(p * 512, GW) for p in range(2)]
        ep_silu = [F_(PH + 11776, GW), F_(PH + 12288, GW)]
        ep_cg = [(g, c) for g in range(G) for c in range(DC)]
        p5slots = [srA.sub(0, 1024), srA.sub(1024, 1024), srB.sub(0, 1024), srB.sub(1024, 1024)]
        p5src = [(wdtt, c) for c in range(DC)] + [(wft, c) for c in range(DC)]

        def _p5dma(i):
            hbm, c = p5src[i]
            stg = p5slots[i % 4]
            dma_in(stg.ap(), hbm[c * 128:(c + 1) * 128, :], stg.key())
        p5dma = [lambda i=i: _p5dma(i) for i in range(len(p5src))]

        def ep_xdma(i):
            g, c = ep_cg[i]
            stg = ep_xslots[i % 6]
            dma_in(stg.ap(), xT[c * 128:(c + 1) * 128, g * GW:(g + 1) * GW], stg.key())

        for i in range(6):
            ep_xdma(i)
        for i, (g, c) in enumerate(ep_cg):
            psX, kX = (psC_t, kC) if i % 2 == 0 else (psT_t, kT)
            pop_(lambda e, c=c, g=g, p=psX: e.matmul(
                p[:, :], owt_bf.ap(c * 128, (c + 1) * 128),
                states_bf.ap(g * GW, (g + 1) * GW), start=True, stop=True),
                [owt_bf.key(c * 128, (c + 1) * 128), states_bf.key(g * GW, (g + 1) * GW)], [kX])
            oa = ro_r(c, g)
            sil = ep_silu[i % 2]
            aop(lambda e, oa=oa, sil=sil: e.activation(sil.ap(), oa.ap(), Act.Silu),
                [oa.key()], [sil.key()])
            stg = ep_xslots[i % 6]
            tb0 = (i % 4) * 512
            vop(lambda e, p=psX, sil=sil, tb0=tb0: e.tensor_mul(tbuf.ap(tb0, tb0 + GW),
                                                                p[:, :], sil.ap()),
                [kX, sil.key()], [tbuf.key(tb0, tb0 + GW)])
            vop(lambda e, s=stg, tb0=tb0: e.tensor_add(tbuf.ap(tb0, tb0 + GW),
                                                       tbuf.ap(tb0, tb0 + GW), s.ap()),
                [tbuf.key(tb0, tb0 + GW), stg.key()], [tbuf.key(tb0, tb0 + GW)])
            if i + 6 < len(ep_cg):
                ep_xdma(i + 6)
            dma_out(x1sp[c * 128:(c + 1) * 128, g * GW:(g + 1) * GW],
                    tbuf.ap(tb0, tb0 + GW), tbuf.key(tb0, tb0 + GW),
                    hbmkey=("x1sp", c * S + g * GW, c * S + (g + 1) * GW))
            vop(lambda e, oa=oa, tb0=tb0: e.tensor_copy(oa.ap(), tbuf.ap(tb0, tb0 + GW)),
                [tbuf.key(tb0, tb0 + GW)], [oa.key()])
            if i % 2 == 0 and i // 2 < len(p5dma):
                p5dma[i // 2]()
            stats_piece(c, g, i % 4, oa.ap(), oa.key(), oa.ap(), oa.key(), alt=True)
            if any(d[0] == "x1" for d in debug):
                dma_out(outs["dbg_x1"][c * 128:(c + 1) * 128, g * GW:(g + 1) * GW],
                        tbuf.ap(tb0, tb0 + GW), tbuf.key(tb0, tb0 + GW))

        if any(d[0] == "w1q" for d in debug):
            dma_out(outs["dbg_w1q"], warena.ap(0, 8192), warena.key(0, 8192))
            dma_out(outs["dbg_w2q"], warena.ap(8192, 16384), warena.key(8192, 16384))
        if any(d[0] == "hq" for d in debug):
            dma_out(outs["dbg_hq"], hq.ap(), hq.key())
        if any(d[0] == "vec1" for d in debug):
            vop(lambda e: e.tensor_copy(srA.ap(0, 640), pf_t[:, 0:640]), [("pf", 0, 640)], [srA.key(0, 640)])
            dma_out(outs["dbg_vec1"], srA.ap(0, 640), srA.key(0, 640))
        if any(d[0] == "pn" for d in debug):
            dma_out(outs["dbg_pn"][:, 0:512], tbuf.ap(0, 512), tbuf.key(0, 512))
            dma_out(outs["dbg_pn"][:, 512:1024], tbuf2.ap(0, 512), tbuf2.key(0, 512))
            for i, nm in enumerate(("w1", "w2")):
                for j, f in enumerate(("scale", "r", "negr")):
                    vop(lambda e, i=i, j=j, nm=nm, f=f: e.tensor_copy(
                        tbuf.ap(600 + i * 3 + j, 601 + i * 3 + j), scl[nm][f].ap()),
                        [scl[nm][f].key()], [tbuf.key(600 + i * 3 + j, 601 + i * 3 + j)])
            dma_out(outs["dbg_pn"][:, 1024:1030], tbuf.ap(600, 606), tbuf.key(600, 606))
        if any(d[0] == "sums" for d in debug):
            for i, nm in enumerate(("w1", "w2", "wx", "wdt", "wf")):
                vop(lambda e, i=i, nm=nm: e.tensor_copy(tbuf2.ap(i, i + 1), scl[nm]["sum"].ap()),
                    [scl[nm]["sum"].key()], [tbuf2.key(i, i + 1)])
            dma_out(outs["dbg_sums"], tbuf2.ap(0, 5), tbuf2.key(0, 5))
        finish_scale("wx", 2 * D * D, 1e-5, "add")
        finish_scale("wdt", D * D, 1e-5, "add")
        finish_scale("wf", D * D, 1e-5, "max")
        wdtq = warena.sub(0, 8192)
        wfq = warena.sub(8192, 8192)
        p5sa = [free8192.sub(0, 512), free8192.sub(512, 512), tern7a, tern7b]
        for i in range(len(p5src)):
            stg = p5slots[i % 4]
            dstq = wdtq if i < DC else wfq
            nm = "wdt" if i < DC else "wf"
            c = p5src[i][1]
            for pc in range(2):
                ternarize(dstq, c * 1024 + pc * 512, stg, pc * 512, 512,
                          scl[nm]["r"], scl[nm]["negr"], p5sa[(i * 2 + pc) % 4], tbuf2)

        assert pump_gen[0] is None, "L1 abs pump must drain before P6 uses xf1 region"
        stats_final(alt=True)
        for g in range(G):
            aap, ak, bap, bk = bcast_AB(g)
            for c in range(DC):
                oa = ro_r(c, g)
                xk = xf1.key(g * GW, (g + 1) * GW)
                vop(lambda e, oa=oa, g=g, a=aap: e.tensor_mul(xf1.ap(g * GW, (g + 1) * GW), a, oa.ap()),
                    [oa.key(), ak], [xk])
                vop(lambda e, g=g, b=bap: e.tensor_add(xf1.ap(g * GW, (g + 1) * GW),
                                                       b, xf1.ap(g * GW, (g + 1) * GW)),
                    [xk, bk], [xk])
                vop(lambda e, c=c, g=g: e.tensor_copy(xn.ap(c * S + g * GW, c * S + (g + 1) * GW),
                                                      xf1.ap(g * GW, (g + 1) * GW)),
                    [xk], [xn.key(c * S + g * GW, c * S + (g + 1) * GW)])
                vop(lambda e, g=g: e.scalar_tensor_tensor(xf1.ap(g * GW, (g + 1) * GW),
                                                          xf1.ap(g * GW, (g + 1) * GW), -1.0,
                                                          xf1.ap(g * GW, (g + 1) * GW),
                                                          Alu.mult, Alu.max), [xk], [xk])
                if c == 0:
                    vop(lambda e, g=g: e.tensor_copy(fold.ap(g * GW, (g + 1) * GW),
                                                     xf1.ap(g * GW, (g + 1) * GW)),
                        [xk], [fold.key(g * GW, (g + 1) * GW)])
                else:
                    vop(lambda e, g=g: e.tensor_max(fold.ap(g * GW, (g + 1) * GW),
                                                    fold.ap(g * GW, (g + 1) * GW),
                                                    xf1.ap(g * GW, (g + 1) * GW)),
                        [xk, fold.key(g * GW, (g + 1) * GW)], [fold.key(g * GW, (g + 1) * GW)])
        for bI in range(16):
            top_(lambda e, b_=bI: e.transpose(psT_t[:, 0:128],
                                              fold.ap(b_ * 128, (b_ + 1) * 128), idreg.ap()),
                 [fold.key(bI * 128, (bI + 1) * 128), idreg.key()], [kT])
            vop(lambda e, b_=bI: e.tensor_reduce(amaxt.ap(b_, b_ + 1), psT_t[:, 0:128], AxX, Alu.max),
                [kT], [amaxt.key(bI, bI + 1)])
        vop(lambda e: e.tensor_scalar(amaxt.ap(), amaxt.ap(), 1e-5, None, Alu.max),
            [amaxt.key()], [amaxt.key()])
        rop(sptk.ap(), amaxt.ap(), [amaxt.key()], [sptk.key()])
        vop(lambda e: e.tensor_scalar(sptk.ap(), sptk.ap(), 127.0, None, Alu.mult),
            [sptk.key()], [sptk.key()])
        vop(lambda e: e.tensor_scalar(rhot.ap(), amaxt.ap(), 1.0 / 127.0, None, Alu.mult),
            [amaxt.key()], [rhot.key()])
        if any(d[0] == "fold" for d in debug):
            dma_out(outs["dbg_fold"], fold.ap(), fold.key())
            dma_out(outs["dbg_amax"], amaxt.ap(), amaxt.key())
        dma_in(onehreg.ap(), c_oneh, onehreg.key())
        top_(lambda e: e.transpose(psT_t[0:16, 0:128], sptk.ap(), idreg.ap()),
             [sptk.key(), idreg.key()], [kT])
        vop(lambda e: e.tensor_copy(v16sp.ap(), psT_t[0:16, 0:128]), [kT], [v16sp.key()])
        top_(lambda e: e.transpose(psT_t[0:16, 0:128], rhot.ap(), idreg.ap()),
             [rhot.key(), idreg.key()], [kT])
        vop(lambda e: e.tensor_copy(v16rh.ap(), psT_t[0:16, 0:128]), [kT], [v16rh.key()])
        for bI in range(16):
            pop_(lambda e, b_=bI: e.matmul(psA_t[:, b_ * 128:(b_ + 1) * 128],
                                           onehreg.ap(b_ * 128, (b_ + 1) * 128), v16sp.ap(),
                                           start=True, stop=True),
                 [onehreg.key(), v16sp.key()], [kA(bI * 128, (bI + 1) * 128)])
        aop(lambda e: e.activation(spbc.ap(), psA_t[:, :], Act.Copy), [kA(0, 2048)], [spbc.key()])
        for bI in range(16):
            pop_(lambda e, b_=bI: e.matmul(psA_t[:, b_ * 128:(b_ + 1) * 128],
                                           onehreg.ap(b_ * 128, (b_ + 1) * 128), v16rh.ap(),
                                           start=True, stop=True),
                 [onehreg.key(), v16rh.key()], [kA(bI * 128, (bI + 1) * 128)])
        vop(lambda e: e.tensor_scalar(wsxr.ap(), psA_t[:, :], scl["wx"]["scale"].ap(), None, Alu.mult),
            [kA(0, 2048), scl["wx"]["scale"].key()], [wsxr.key()])
        if any(d[0] == "spbc" for d in debug):
            dma_out(outs["dbg_spbc"], pf_t[:, spbc.lo:spbc.lo + S], spbc.key())
        vop(lambda e: e.tensor_copy(sprow.ap(p1=1), r0(spbc, 0, S)), [spbc.key()], [sprow.key()])
        dma_in(r0(tbuf, 0, 2 * D), v_bx.unsqueeze(0), tbuf.key())
        vop(lambda e: e.tensor_scalar(bxrow.ap(p1=1), r0(tbuf, 0, 2 * D),
                                      ppv(scl["wx"]["r"], 0, 0, 1), None, Alu.mult),
            [tbuf.key(), scl["wx"]["r"].key()], [bxrow.key()])
        dma_in(r0(tbuf, 0, D), v_bdt.unsqueeze(0), tbuf.key(0, D))
        vop(lambda e: e.tensor_scalar(bdrow.ap(p1=1), r0(tbuf, 0, D),
                                      ppv(scl["wdt"]["r"], 0, 0, 1), None, Alu.mult),
            [tbuf.key(0, D), scl["wdt"]["r"].key()], [bdrow.key()])
        for g in range(G):
            aap, ak, bap, bk = bcast_AB(g)
            for c in range(DC):
                oa = ro_r(c, g)
                xk = xf1.key(0, GW)
                vop(lambda e, oa=oa, a=aap: e.tensor_mul(xf1.ap(0, GW), a, oa.ap()),
                    [oa.key(), ak], [xk])
                vop(lambda e, b=bap: e.tensor_add(xf1.ap(0, GW), b, xf1.ap(0, GW)),
                    [xk, bk], [xk])
                vop(lambda e, g=g: e.tensor_mul(xf1.ap(0, GW), xf1.ap(0, GW),
                                                spbc.ap(g * GW, (g + 1) * GW)),
                    [xk, spbc.key()], [xk])
                vop(lambda e: e.tensor_scalar(xf1.ap(0, GW), xf1.ap(0, GW), Mb_v.ap(), nMb_v.ap(),
                                              Alu.add, Alu.add),
                    [xk, Mb_v.key(), nMb_v.key()], [xk])
                vop(lambda e, oa=oa: e.tensor_scalar(oa.ap(), xf1.ap(0, GW), -128.0, 127.0,
                                                     Alu.max, Alu.min), [xk], [oa.key()])
        if any(d[0] == "rq" for d in debug):
            for c in range(DC):
                for g in range(G):
                    oa = ro_r(c, g)
                    dma_out(outs["dbg_rq"][c * 128:(c + 1) * 128, g * GW:(g + 1) * GW], oa.ap(), oa.key())
        for bI in range(16):
            pop_(lambda e, b_=bI: e.matmul(psA_t[:, b_ * 128:(b_ + 1) * 128],
                                           onehreg.ap(b_ * 128, (b_ + 1) * 128), v16rh.ap(),
                                           start=True, stop=True),
                 [onehreg.key(), v16rh.key()], [kA(bI * 128, (bI + 1) * 128)])
        vop(lambda e: e.tensor_scalar(wsdr.ap(), psA_t[:, :], scl["wdt"]["scale"].ap(), None, Alu.mult),
            [kA(0, 2048), scl["wdt"]["scale"].key()], [wsdr.key()])

        vop(lambda e: e.memset(zerot.ap(), 0.0), writes=[zerot.key()])
        b1r, b2r, b3r, b4r, b5r = Bs

        def wx_dma(oc, par):
            stg = wstage.sub(par * 1024, 1024)
            for q in range(4):
                dma_in(stg.ap(q * 256, (q + 1) * 256),
                       wxt[oc * 128:(oc + 1) * 128, q * 256:(q + 1) * 256],
                       stg.key(q * 256, (q + 1) * 256))

        def wx_tern(oc, par):
            stg = wstage.sub(par * 1024, 1024)
            wq = wxs.sub(par * 1024, 1024)
            for pc in range(2):
                ternarize(wq, pc * 512, stg, pc * 512, 512,
                          scl["wx"]["r"], scl["wx"]["negr"], (tern7a, tern7b)[pc], tbuf2)

        def l1_mm(c):
            for g in range(G):
                for dc2 in range(DC):
                    pop_(lambda e, c=c, g=g, dc2=dc2: e.matmul(
                        psA_t[:, g * 512:(g + 1) * 512],
                        wdtq.ap(dc2 * 1024 + c * 128, dc2 * 1024 + (c + 1) * 128),
                        ro_r(dc2, g).ap(), start=(dc2 == 0), stop=False),
                        [wdtq.key(dc2 * 1024 + c * 128, dc2 * 1024 + (c + 1) * 128),
                         ro_r(dc2, g).key()], [kA(g * 512, (g + 1) * 512)])
                pop_(lambda e, c=c, g=g: e.matmul(
                    psA_t[:, g * 512:(g + 1) * 512], bdrow.ap(c * 128, (c + 1) * 128, 0, 1),
                    sprow.ap(g * GW, (g + 1) * GW, 0, 1), start=False, stop=True),
                    [bdrow.key(c * 128, (c + 1) * 128), sprow.key(g * GW, (g + 1) * GW)],
                    [kA(g * 512, (g + 1) * 512)])
                vop(lambda e, g=g: e.tensor_mul(b1r.ap(g * 512, (g + 1) * 512),
                                                psA_t[:, g * 512:(g + 1) * 512],
                                                wsdr.ap(g * 512, (g + 1) * 512)),
                    [kA(g * 512, (g + 1) * 512), wsdr.key()], [b1r.key(g * 512, (g + 1) * 512)])
            for oc, dst, par in ((c, b2r, 0), (c + 8, b3r, 1)):
                wq = wxs.sub(par * 1024, 1024)
                for g in range(G):
                    for dc2 in range(DC):
                        pop_(lambda e, g=g, dc2=dc2, wq=wq: e.matmul(
                            psA_t[:, g * 512:(g + 1) * 512], wq.ap(dc2 * 128, (dc2 + 1) * 128),
                            ro_r(dc2, g).ap(), start=(dc2 == 0), stop=False),
                            [wq.key(dc2 * 128, (dc2 + 1) * 128), ro_r(dc2, g).key()],
                            [kA(g * 512, (g + 1) * 512)])
                    pop_(lambda e, oc=oc, g=g: e.matmul(
                        psA_t[:, g * 512:(g + 1) * 512], bxrow.ap(oc * 128, (oc + 1) * 128, 0, 1),
                        sprow.ap(g * GW, (g + 1) * GW, 0, 1), start=False, stop=True),
                        [bxrow.key(oc * 128, (oc + 1) * 128), sprow.key(g * GW, (g + 1) * GW)],
                        [kA(g * 512, (g + 1) * 512)])
                    vop(lambda e, dst=dst, g=g: e.tensor_mul(dst.ap(g * 512, (g + 1) * 512),
                                                             psA_t[:, g * 512:(g + 1) * 512],
                                                             wsxr.ap(g * 512, (g + 1) * 512)),
                        [kA(g * 512, (g + 1) * 512), wsxr.key()],
                        [dst.key(g * 512, (g + 1) * 512)])
                    vop(lambda e, dst=dst, g=g: e.tensor_scalar(dst.ap(g * 512, (g + 1) * 512),
                                                                dst.ap(g * 512, (g + 1) * 512),
                                                                5.0, -5.0, Alu.min, Alu.max),
                        [dst.key(g * 512, (g + 1) * 512)], [dst.key(g * 512, (g + 1) * 512)])
            if any(d[0] == "l1t" for d in debug) and c == 0:
                dma_out(outs["dbg_l1t"][:, 0:S], b1r.ap(), b1r.key())
                dma_out(outs["dbg_l1t"][:, S:2 * S], b2r.ap(), b2r.key())
                dma_out(outs["dbg_l1t"][:, 2 * S:3 * S], b3r.ap(), b3r.key())
        def l1_chain(c, tern_thunk=None):
            for fn, scale, bias in ((Act.Abs, None, None), (Act.Exp, -1.0, None),
                                    (Act.Ln, None, 1.0)):
                for g in range(G):
                    s_, src_ = ((b1r, b5r) if fn == Act.Abs else (b5r, b5r))
                    aop(lambda e, fn=fn, scale=scale, bias=bias, g=g, s_=s_:
                        e.activation(b5r.ap(g * 512, (g + 1) * 512),
                                     s_.ap(g * 512, (g + 1) * 512) if fn != Act.Abs
                                     else b1r.ap(g * 512, (g + 1) * 512), fn,
                                     **({} if scale is None else {"scale": scale}),
                                     **({} if bias is None else {"bias": bias})),
                        [(b1r if fn == Act.Abs else b5r).key(g * 512, (g + 1) * 512)],
                        [b5r.key(g * 512, (g + 1) * 512)])
            if tern_thunk is not None:
                tern_thunk()
            vop(lambda e: e.scalar_tensor_tensor(b1r.ap(), b1r.ap(), 0.0, b5r.ap(),
                                                 Alu.max, Alu.add), [b1r.key(), b5r.key()], [b1r.key()])
            vop(lambda e: e.tensor_scalar(b1r.ap(), b1r.ap(), 0.001, 1.0, Alu.max, Alu.min),
                [b1r.key()], [b1r.key()])
            vop(lambda e, c=c: e.tensor_scalar(b4r.ap(), b1r.ap(), ppv(aneg, c), -10.0,
                                               Alu.mult, Alu.max), [b1r.key(), aneg.key()], [b4r.key()])
            aop(lambda e: e.activation(b4r.ap(), b4r.ap(), Act.Exp), [b4r.key()], [b4r.key()])
            aop(lambda e: e.activation(b3r.ap(), b3r.ap(), Act.Tanh), [b3r.key()], [b3r.key()])
            vop(lambda e: e.tensor_mul(b2r.ap(), b1r.ap(), b2r.ap()), [b1r.key(), b2r.key()], [b2r.key()])
            vop(lambda e: e.tensor_tensor_scan(b4r.ap(), b4r.ap(), zerot.ap(), 1.0,
                                               Alu.mult, Alu.add), [b4r.key(), zerot.key()], [b4r.key()])
            if any(d[0] == "l1t" for d in debug) and c == 0:
                dma_out(outs["dbg_l1t"][:, 3 * S:4 * S], b4r.ap(), b4r.key())
            vop(lambda e: e.tensor_copy(b5r.ap(0, 1), S_v.ap()), [S_v.key()], [b5r.key(0, 1)])
            vop(lambda e: e.tensor_scalar(b5r.ap(1, S), b4r.ap(0, S - 1), S_v.ap(), F_v.ap(),
                                          Alu.mult, Alu.max),
                [b4r.key(), S_v.key(), F_v.key()], [b5r.key(1, S)])
            rop(b5r.ap(), b5r.ap(), [b5r.key()], [b5r.key()])
            vop(lambda e: e.tensor_mul(b2r.ap(), b2r.ap(), b5r.ap()), [b2r.key(), b5r.key()], [b2r.key()])
            vop(lambda e: e.tensor_scalar(b5r.ap(), b2r.ap(), T_v.ap(), nT_v.ap(), Alu.min, Alu.max),
                [b2r.key(), T_v.key(), nT_v.key()], [b5r.key()])
            vop(lambda e: e.tensor_tensor(b1r.ap(), b5r.ap(), b2r.ap(), Alu.is_equal),
                [b5r.key(), b2r.key()], [b1r.key()])
            vop(lambda e: e.tensor_scalar(b2r.ap(), b5r.ap(), G_v.ap(), None, Alu.mult),
                [b5r.key(), G_v.key()], [b2r.key()])
            vop(lambda e: e.tensor_tensor_scan(b2r.ap(), b2r.ap(), zerot.ap(), 0.0,
                                               Alu.add, Alu.add), [b2r.key(), zerot.key()], [b2r.key()])
            vop(lambda e: e.scalar_tensor_tensor(b5r.ap(), b2r.ap(), -1.0, b2r.ap(),
                                                 Alu.mult, Alu.max), [b2r.key()], [b5r.key()])
            vop(lambda e: e.tensor_scalar(b5r.ap(), b5r.ap(), U_v.ap(), None, Alu.is_le),
                [b5r.key(), U_v.key()], [b5r.key()])
            vop(lambda e: e.tensor_tensor_scan(b1r.ap(), b1r.ap(), b5r.ap(), 4096.0,
                                               Alu.mult, Alu.mult), [b1r.key(), b5r.key()], [b1r.key()])
            vop(lambda e: e.tensor_mul(b2r.ap(), b4r.ap(), b2r.ap()), [b4r.key(), b2r.key()], [b2r.key()])
            vop(lambda e: e.tensor_scalar(b2r.ap(), b2r.ap(), U_v.ap(), nU_v.ap(), Alu.min, Alu.max),
                [b2r.key(), U_v.key(), nU_v.key()], [b2r.key()])
            vop(lambda e: e.tensor_mul(b2r.ap(), b2r.ap(), b3r.ap()), [b2r.key(), b3r.key()], [b2r.key()])
            vop(lambda e: e.tensor_mul(b2r.ap(), b2r.ap(), b1r.ap()), [b2r.key(), b1r.key()], [b2r.key()])
            if any(d[0] == "l1t" for d in debug) and c == 0:
                dma_out(outs["dbg_l1t"][:, 4 * S:5 * S], b2r.ap(), b2r.key())
            g1sl = [(psC_t[:, :], kC), (psT_t[:, :], kT),
                    (free8192.ap(0, GW), free8192.key(0, GW)),
                    (free8192.ap(GW, 2 * GW), free8192.key(GW, 2 * GW))]
            for g in range(G):
                pbk = (g % 2) * 512
                for dc2 in range(DC):
                    pop_(lambda e, c=c, g=g, dc2=dc2, pbk=pbk: e.matmul(
                        psB_t[:, pbk:pbk + 512],
                        wfq.ap(dc2 * 1024 + c * 128, dc2 * 1024 + (c + 1) * 128),
                        xn.ap(dc2 * S + g * GW, dc2 * S + (g + 1) * GW),
                        start=(dc2 == 0), stop=(dc2 == DC - 1)),
                        [wfq.key(dc2 * 1024 + c * 128, dc2 * 1024 + (c + 1) * 128),
                         xn.key(dc2 * S + g * GW, dc2 * S + (g + 1) * GW)], [kB(pbk, pbk + 512)])
                gap, gk = g1sl[g]
                aop(lambda e, g=g, pbk=pbk, gap=gap: e.activation(
                    gap, psB_t[:, pbk:pbk + 512], Act.Silu,
                    scale=scl["wf"]["scale"].ap()),
                    [kB(pbk, pbk + 512), scl["wf"]["scale"].key()], [gk])
            for g in range(G):
                gap, gk = g1sl[g]
                vop(lambda e, g=g, gap=gap: e.tensor_mul(b2r.ap(g * GW, (g + 1) * GW),
                                                         b2r.ap(g * GW, (g + 1) * GW), gap),
                    [b2r.key(g * GW, (g + 1) * GW), gk], [b2r.key(g * GW, (g + 1) * GW)])
            if any(d[0] == "l1t" for d in debug) and c == 0:
                for g in range(G):
                    gap, gk = g1sl[g]
                    vop(lambda e, g=g, gap=gap: e.tensor_copy(b4r.ap(g * GW, (g + 1) * GW), gap),
                        [gk], [b4r.key(g * GW, (g + 1) * GW)])
                dma_out(outs["dbg_l1t"][:, 5 * S:6 * S], b4r.ap(), b4r.key())
            dma_rd(b5r.ap(), x1sp[c * 128:(c + 1) * 128, :], b5r.key(),
                   hbmkey=("x1sp", c * S, (c + 1) * S))
            vop(lambda e: e.tensor_add(b2r.ap(), b2r.ap(), b5r.ap()), [b2r.key(), b5r.key()], [b2r.key()])
            if any(d[0] == "y0" for d in debug) and c == 0:
                dma_out(outs["dbg_y0"], b2r.ap(), b2r.key())
            dma_out(outT[c * 128:(c + 1) * 128, :], b2r.ap(), b2r.key())

        wx_dma(0, 0)
        wx_dma(8, 1)
        wx_tern(0, 0)
        wx_tern(8, 1)
        wx_dma(1, 0)
        wx_dma(9, 1)
        for c in range(DC):
            l1_mm(c)
            def _tern(c=c):
                if c + 1 < DC:
                    wx_tern(c + 1, 0)
                    wx_tern(c + 9, 1)
                    if c + 2 < DC:
                        wx_dma(c + 2, 0)
                        wx_dma(c + 10, 1)
            l1_chain(c, tern_thunk=_tern)

        if stop is not None:
            st.truncate(stop)
        if any(d[0] == "vsync" for d in debug):
            allw = [(s_, st.cnt[s_]) for s_ in Sched.STREAMS]
            st.op("din0", (lambda e, sem: e.dma_start(out=outs["dbg_vsync"], in_=pf_t[:, 0:640])
                           .then_inc(sem, 16)), inc=16, extra=allw)
            st.op("dout0", (lambda e, sem: e.dma_start(out=outs["dbg_vgp"], in_=pf_t[:, 0:640])
                            .then_inc(sem, 16)), inc=16, extra=allw)
        st.emit(blk, sems)
    return nc, ins, outs


def _host_layout(inputs):
    f32 = np.float32
    common = dict(
        w1t=np.ascontiguousarray(np.asarray(inputs["L0_ffn_W1"], f32).T),
        w2t=np.ascontiguousarray(np.asarray(inputs["L0_ffn_W2"], f32).T),
        wxt=np.ascontiguousarray(np.asarray(inputs["L1_attn_W_x"], f32).T.reshape(8, 128, 16, 128).transpose(2, 1, 0, 3).reshape(2048, 1024)),
        wdtt=np.ascontiguousarray(np.asarray(inputs["L1_attn_W_dt"], f32).T),
        wft=np.ascontiguousarray(np.asarray(inputs["L1_ffn_weight"], f32).T),
        kwt=np.ascontiguousarray(np.asarray(inputs["L0_attn_key_w"], f32).T
                                 .reshape(8, 128, -1).transpose(1, 0, 2).reshape(128, -1)),
        vwt=np.ascontiguousarray(np.asarray(inputs["L0_attn_value_w"], f32).T
                                 .reshape(8, 128, -1).transpose(1, 0, 2).reshape(128, -1)),
        owt=np.ascontiguousarray(np.asarray(inputs["L0_attn_output_w"], f32).T),
        v_td=np.asarray(inputs["L0_attn_time_decay"], f32),
        v_tf=np.asarray(inputs["L0_attn_time_first"], f32),
        v_tsg=np.asarray(inputs["L0_attn_time_shift_gates"], f32),
        v_lnw=np.asarray(inputs["L0_attn_ln_w"], f32),
        v_lnb=np.asarray(inputs["L0_attn_ln_b"], f32),
        v_b1=np.asarray(inputs["L0_ffn_b1"], f32),
        v_b2=np.asarray(inputs["L0_ffn_b2"], f32),
        v_bx=np.asarray(inputs["L1_attn_b_x"], f32),
        v_bdt=np.asarray(inputs["L1_attn_b_dt"], f32),
        v_alog=np.asarray(inputs["L1_attn_A_log"], f32),
        vpack=np.concatenate(
            [np.asarray(inputs[k], f32).reshape(-1, 128).T for k in
             ("L0_attn_time_shift_gates", "L0_attn_ln_w", "L0_attn_ln_b",
              "L1_attn_A_log", "L0_ffn_b2", "L0_ffn_b1")] +
            [np.pad(np.asarray(inputs[k], f32).reshape(-1, 64).T, ((0, 64), (0, 0)))
             for k in ("L0_attn_time_decay", "L0_attn_time_first")], axis=1),
        c_ident=np.eye(128, dtype=f32),
        c_ones_bf=np.ones((128, 1), NPBF),
        c_ones_f=np.ones((128, 1), f32),
        c_onesbc=np.ones((1, 128), f32),
    )
    oneh = np.zeros((16, S), f32)
    for i in range(16):
        oneh[i, i * 128:(i + 1) * 128] = 1.0
    common["c_oneh"] = oneh
    x = np.asarray(inputs["L0_attn_x"], f32)
    in_maps = []
    for c in range(B):
        m = dict(common)
        m["xT"] = np.ascontiguousarray(x[c].T)
        in_maps.append(m)
    return in_maps


def kernel(**inputs):
    in_maps = _host_layout(inputs)
    nc, ins, outs = build_core()
    res = run_bass_kernel_spmd(nc, in_maps, core_ids=list(range(B)))
    out = np.stack([np.asarray(r["outT"]).T for r in res.results], axis=0)
    return np.ascontiguousarray(out, dtype=np.float32)


# version 24
# speedup vs baseline: 1.0032x; 1.0032x over previous
import numpy as np
import ml_dtypes
import concourse.bass as bass
import concourse.mybir as mybir
from concourse.bass_utils import run_bass_kernel_spmd

F32 = mybir.dt.float32
BF16 = mybir.dt.bfloat16
Alu = mybir.AluOpType
Act = mybir.ActivationFunctionType
AxX = mybir.AxisListType.X
NPBF = ml_dtypes.bfloat16

FMAX = 3.4028235e38
M23 = 1.5 * 2.0 ** 23
M52 = 1.5 * 2.0 ** 52
P64 = 2.0 ** 64

B, S, D, N, FF = 8, 2048, 1024, 64, 4096
DC = D // 128
G = 4
GW = S // G


class Sched:
    STREAMS = ("pe", "dve", "act", "pool",
               "din0", "din1", "din2", "din3", "din4", "din5", "din6", "din7",
               "dout0", "dout1")

    def __init__(self):
        self.cnt = {s: 0 for s in self.STREAMS}
        self.ops = []
        self.access = {}

    def _deps(self, stream, rds, wrs):
        need = {}
        for (arena, lo, hi, is_w) in [(a, l, h, False) for (a, l, h) in rds] + \
                                     [(a, l, h, True) for (a, l, h) in wrs]:
            lst = self.access.get(arena)
            if not lst:
                continue
            for (l2, h2, s2, c2, w2) in lst:
                if l2 < hi and lo < h2 and (w2 or is_w) and not (s2 == stream == "pe"):
                    if need.get(s2, 0) < c2:
                        need[s2] = c2
        return need

    def marker(self, name):
        self.ops.append(("MARK", name, None))

    def truncate(self, name):
        idx = None
        for i, t in enumerate(self.ops):
            if t[0] == "MARK" and t[1] == name:
                idx = i
                break
        assert idx is not None, name
        self.ops = self.ops[:idx]
        self.cnt = {s: 0 for s in self.STREAMS}
        for rec in self.ops:
            if rec[0] != "MARK":
                self.cnt[rec[0]] += rec[3]
        self.access = {}

    def op(self, stream, fn, reads=(), writes=(), inc=1, extra=(), kind="dma"):
        need = self._deps(stream, reads, writes)
        for s2, c2 in extra:
            if need.get(s2, 0) < c2:
                need[s2] = c2
        c_after = self.cnt[stream] + inc
        self.cnt[stream] = c_after
        self.ops.append((stream, fn, need, inc, kind))
        for (a, lo, hi) in reads:
            self.access.setdefault(a, []).append((lo, hi, stream, c_after, False))
        for (a, lo, hi) in writes:
            lst = self.access.setdefault(a, [])
            lst[:] = [e for e in lst if not (lo <= e[0] and e[1] <= hi)]
            lst.append((lo, hi, stream, c_after, True))

    def emit(self, blk, sems):
        watermark = {}

        waited = set()
        for rec in self.ops:
            if rec[0] == "MARK":
                continue
            for dep, c in rec[2].items():
                waited.add((dep, c))

        def section(eng_obj, streams, tag):
            run = {s: 0 for s in self.STREAMS}
            for rec in self.ops:
                if rec[0] == "MARK":
                    continue
                (stream, fn, need, inc, kind) = rec
                if stream in self.STREAMS:
                    run[stream] += inc
                if stream not in streams:
                    continue
                for dep, c in sorted(need.items()):
                    key = (tag, dep)
                    if watermark.get(key, 0) < c:
                        watermark[key] = c
                        eng_obj.wait_ge(sems[dep], c)
                if kind == "dma":
                    fn(eng_obj, sems[stream])
                else:
                    instr = fn(eng_obj)
                    if kind == "compute_recip" or (
                            (stream, run[stream]) in waited and stream in ("dve", "act")):
                        eng_obj.drain().then_inc(sems[stream], inc)
                    else:
                        instr.then_inc(sems[stream], inc)
            if "dout0" in streams:
                fincnt = {s: 0 for s in self.cnt}
                for rec in self.ops:
                    if rec[0] != "MARK":
                        fincnt[rec[0]] += rec[3]
                for s_ in ("dout0", "dout1", "din0", "din1", "din2", "din3",
                           "din4", "din5", "din6", "din7"):
                    eng_obj.wait_ge(sems[s_], fincnt[s_])

        @blk.sync
        def _(e):
            section(e, ("din0", "din1", "din2", "din3", "din4", "din5", "din6", "din7"),
                    "sync")

        @blk.vector
        def _(e):
            section(e, ("dve",), "dve")

        @blk.scalar
        def _(e):
            section(e, ("act",), "act")

        @blk.gpsimd
        def _(e):
            section(e, ("pool", "dout0", "dout1"), "gp")

        @blk.tensor
        def _(e):
            section(e, ("pe",), "pe")


class Reg:

    def __init__(self, pool_t, pool_name, lo, n, p0=0, p1=128):
        self.t, self.name, self.lo, self.n, self.p0, self.p1 = pool_t, pool_name, lo, n, p0, p1

    def ap(self, c0=0, c1=None, p0=None, p1=None):
        c1 = self.n if c1 is None else c1
        p0 = self.p0 if p0 is None else p0
        p1 = self.p1 if p1 is None else p1
        return self.t[p0:p1, self.lo + c0:self.lo + c1]

    def key(self, c0=0, c1=None):
        c1 = self.n if c1 is None else c1
        return (self.name, self.lo + c0, self.lo + c1)

    def sub(self, off, n):
        return Reg(self.t, self.name, self.lo + off, n, self.p0, self.p1)


def build_core(debug=(), stop=None):
    nc = bass.Bass("TRN2", target_bir_lowering=False)
    ins, outs = {}, {}

    def I(name, shape, dt=F32):
        ins[name] = nc.dram_tensor(name, shape, dt, kind="ExternalInput").ap()
        return ins[name]

    def O(name, shape, dt=F32):
        outs[name] = nc.dram_tensor(name, shape, dt, kind="ExternalOutput").ap()
        return outs[name]

    xT = I("xT", [D, S])
    vpk = I("vpack", [128, 74])
    w1t = I("w1t", [D, FF])
    w2t = I("w2t", [FF, D])
    wxt = I("wxt", [2 * D, D])
    wdtt = I("wdtt", [D, D])
    wft = I("wft", [D, D])
    kwt = I("kwt", [128, DC * N])
    vwt = I("vwt", [128, DC * N])
    owt = I("owt", [N, D])
    v_td = I("v_td", [N])
    v_tf = I("v_tf", [N])
    v_tsg = I("v_tsg", [D])
    v_lnw = I("v_lnw", [D])
    v_lnb = I("v_lnb", [D])
    v_b1 = I("v_b1", [FF])
    v_b2 = I("v_b2", [D])
    v_bx = I("v_bx", [2 * D])
    v_bdt = I("v_bdt", [D])
    v_alog = I("v_alog", [D])
    c_ident = I("c_ident", [128, 128])
    c_oneh = I("c_oneh", [16, S])
    c_ones_bf = I("c_ones_bf", [128, 1], BF16)
    c_ones_f = I("c_ones_f", [128, 1])
    c_onesbc = I("c_onesbc", [1, 128])
    x1sp = nc.dram_tensor("x1spill", [D, S], F32, kind="Internal").ap()

    outT = O("outT", [D, S])
    for dbgspec in debug:
        dnm, dsh = dbgspec[0], dbgspec[1]
        ddt = dbgspec[2] if len(dbgspec) > 2 else F32
        O("dbg_" + dnm, dsh, ddt)

    st = Sched()

    import contextlib
    with contextlib.ExitStack() as EX:
        E = EX.enter_context
        PF_COLS = 21296
        PB_COLS = 63400
        pf_t = E(nc.sbuf_tensor("poolf", [128, PF_COLS], F32))
        pb_t = E(nc.sbuf_tensor("poolb", [128, PB_COLS], BF16))
        psA_t = E(nc.psum_tensor("psA", [128, 2048], F32))
        psB_t = E(nc.psum_tensor("psB", [128, 1024], F32))
        psC_t = E(nc.psum_tensor("psC", [128, 512], F32))
        psT_t = E(nc.psum_tensor("psT", [128, 512], F32))
        sems = {s: E(nc.semaphore("sem_" + s)) for s in Sched.STREAMS}
        blk = E(nc.Block())

        def kA(c0, c1):
            return ("psA", c0, c1)

        def kB(c0, c1):
            return ("psB", c0, c1)

        kC = ("psC", 0, 512)
        kT = ("psT", 0, 512)

        def F_(lo, n, p0=0, p1=128):
            assert lo + n <= PF_COLS
            return Reg(pf_t, "pf", lo, n, p0, p1)

        def B_(lo, n, p0=0, p1=128):
            assert lo + n <= PB_COLS
            return Reg(pb_t, "pb", lo, n, p0, p1)

        xn = B_(0, S * DC)
        ro = B_(16384, S * DC)
        warena = B_(32768, 16384)
        PBH = 49152
        zb = B_(PBH, GW * DC)
        tb2b = B_(PBH + 4096, 2 * GW)
        hq = B_(PBH, GW * 8)
        dump5 = B_(PBH, 2048)
        wxs = B_(PBH, 2048)
        sprow = B_(PBH + 2048, S)
        bxrow = B_(PBH + 4096, 2 * D)
        bdrow = B_(PBH + 6144, D)
        PBS = 56320
        states_bf = B_(PBS, S, p1=64)
        kp_bf = B_(PBS + 2048, N * DC)
        vpw_bf = B_(PBS + 2560, N * DC)
        owt_bf = B_(PBS + 3072, D, p1=64)
        onesb_bf = B_(PBS + 4096, 8)
        sqbf = B_(PBS + 4104, 2048)
        dump1 = ro.sub(0, 2048)
        assert PBS + 5128 <= PB_COLS

        def slot(c, g):
            return (g * DC + c) * GW

        def ro_r(c, g):
            return ro.sub(slot(c, g), GW)

        idreg = F_(0, 128)
        onesbcreg = F_(128, 128)
        VEC = 256
        vp_ = [VEC]

        def vecreg(k=1):
            o = vp_[0]
            vp_[0] += k
            assert vp_[0] <= VEC + 384
            return F_(o, k)

        flag = vecreg(); invflag = vecreg()
        gsig = vecreg(DC); gsig1m = vecreg(DC)
        lnw_v = vecreg(DC); lnb_v = vecreg(DC)
        aneg = vecreg(DC)
        b1_v = vecreg(32)
        b2_v = vecreg(DC)
        tdv = vecreg(); tfv = vecreg()
        etf = vecreg(); negetf = vecreg(); ekb = vecreg()
        wv = vecreg()
        bkv = vecreg(); bvv = vecreg()
        tsg_r = vecreg(DC); alog_r = vecreg(DC)
        ones_f = vecreg()
        scl = {}
        for nm in ("w1", "w2", "wx", "wdt", "wf"):
            scl[nm] = dict(sum=vecreg(), r=vecreg(), negr=vecreg(), scale=vecreg())
        tmpv = vecreg(8)
        c_eps = vecreg(); c_mhalf = vecreg(); c_mone = vecreg()
        s0v = vecreg(); sIv = vecreg()
        pw2 = vecreg(); pw4 = vecreg(); pw8 = vecreg(); pwt = vecreg()
        S_v = vecreg(); F_v = vecreg(); T_v = vecreg(); nT_v = vecreg()
        U_v = vecreg(); nU_v = vecreg(); G_v = vecreg()
        Mb_v = vecreg(); nMb_v = vecreg(); nMb1_v = vecreg()

        PH = 1024
        wstage = F_(PH, 2048)
        srA = F_(PH + 2048, S)
        srB = F_(PH + 4096, S)
        srC = F_(PH + 6144, S)
        free8192 = F_(PH + 8192, 1024)
        tbuf = F_(PH + 9216, S)
        tbuf2 = F_(PH + 11264, 512)
        kvf = F_(PH + 11776, S, p1=64)
        wdec = F_(PH + 13824, S, p1=64)
        stf = F_(PH + 15872, S, p1=64)
        ekbuf = F_(PH + 17920, GW, p1=64)
        fold = F_(PH + 11776, S)
        xf1 = F_(PH + 13824, S)
        amaxt = F_(PH + 15872, 16)
        sptk = F_(PH + 15888, 16)
        rhot = F_(PH + 15904, 16)
        onehreg = F_(PH + 15920, S, p1=16)
        v16sp = F_(PH + 17968, 128, p1=16)
        v16rh = F_(PH + 18096, 128, p1=16)
        spbc = F_(PH + 11776, S)
        wsxr = F_(PH + 18224, 2048)
        wsdr = F_(PH + 15920, 2048)
        assert PH + 18224 + 2048 <= PF_COLS
        Bs = [F_(PH + 2048 + i * S, S) for i in range(5)]
        zerot = F_(PH + 12288, S)
        tern7a = F_(PH + 14336, 512)
        tern7b = F_(PH + 14848, 512)

        din_rot = [0]
        dout_rot = [0]

        def dma_in(dst_ap, src_ap, wkey, rkeys=(), slow=False):
            s_ = "din%d" % (din_rot[0] % 4)
            din_rot[0] += 1
            prev = st.cnt[s_]
            def fn(e, sem, d=dst_ap, s=src_ap, sl=slow):
                e.dma_start(out=d, in_=s, allow_slow_non_contiguous=sl).then_inc(sem, 16)
            st.op(s_, fn, reads=rkeys, writes=[wkey], inc=16, extra=[(s_, prev)])

        def dma_io(dst_ap, src_ap, rkeys, wkeys):
            s_ = "dout%d" % (dout_rot[0] % 2)
            dout_rot[0] += 1
            prev = st.cnt[s_]
            def fn(e, sem, d=dst_ap, s=src_ap):
                e.dma_start(out=d, in_=s).then_inc(sem, 16)
            st.op(s_, fn, reads=rkeys, writes=wkeys, inc=16, extra=[(s_, prev)])

        def dma_out(dst_ap, src_ap, rkey, hbmkey=None):
            dma_io(dst_ap, src_ap, [rkey], [hbmkey] if hbmkey else [])

        def dma_rd(dst_ap, src_ap, wkey, hbmkey=None):
            dma_io(dst_ap, src_ap, [hbmkey] if hbmkey else [], [wkey])

        def vop(fn, reads=(), writes=()):
            st.op("dve", fn, reads, writes, kind="compute")

        def aop(fn, reads=(), writes=()):
            st.op("act", fn, reads, writes, kind="compute")

        def pop_(fn, reads=(), writes=()):
            def f2(e, sem, fn=fn):
                fn(e).then_inc(sem, 1)
            st.op("pe", f2, reads, writes, kind="dma")

        def top_(fn, reads=(), writes=()):
            def f2(e, sem, fn=fn):
                fn(e)
                e.drain().then_inc(sem, 1)
            st.op("pe", f2, reads, writes, kind="dma")

        def rop(out_ap, in_ap, reads, writes):
            def fn(e, o=out_ap, i=in_ap):
                e.drain()
                r = e.reciprocal(o, i)
                return r
            st.op("dve", fn, reads, writes, kind="compute_recip")

        def ppv(reg, c=0, p0=None, p1=None):
            p0 = reg.p0 if p0 is None else p0
            p1 = reg.p1 if p1 is None else p1
            return reg.t[p0:p1, reg.lo + c:reg.lo + c + 1]

        def r0(reg, a=0, b=None):
            b = reg.n if b is None else b
            return reg.t[0:1, reg.lo + a:reg.lo + b]

        dma_in(idreg.ap(), c_ident, idreg.key())
        dma_in(r0(onesbcreg), c_onesbc, onesbcreg.key())
        dma_in(onesb_bf.ap(0, 1), c_ones_bf, onesb_bf.key())
        dma_in(ones_f.ap(), c_ones_f, ones_f.key())

        def load_pp(reg, hbm, p=128):
            dma_in(reg.ap(p0=0, p1=p), hbm.rearrange("(c p) -> p c", p=p), reg.key(), slow=True)

        dma_in(tbuf.ap(0, 74), vpk, tbuf.key(0, 74))
        vp_off = 0
        for reg, w in ((tsg_r, DC), (lnw_v, DC), (lnb_v, DC), (alog_r, DC),
                       (b2_v, DC), (b1_v, 32), (tdv, 1), (tfv, 1)):
            vop(lambda e, reg=reg, o=vp_off, w=w: e.tensor_copy(reg.ap(), tbuf.ap(o, o + w)),
                [tbuf.key(vp_off, vp_off + w)], [reg.key()])
            vp_off += w

        vop(lambda e: e.memset(c_eps.ap(), 1e-5), writes=[c_eps.key()])
        vop(lambda e: e.memset(c_mhalf.ap(), -0.5), writes=[c_mhalf.key()])
        vop(lambda e: e.memset(c_mone.ap(), -1.0), writes=[c_mone.key()])
        vop(lambda e: e.memset(flag.ap(), 2.0 ** -40), writes=[flag.key()])
        vop(lambda e: e.tensor_scalar(flag.ap(), flag.ap(), 1.0, -1.0, Alu.add, Alu.add),
            [flag.key()], [flag.key()])
        vop(lambda e: e.tensor_scalar(flag.ap(), flag.ap(), 2.0 ** 40, None, Alu.mult),
            [flag.key()], [flag.key()])
        vop(lambda e: e.tensor_scalar(invflag.ap(), flag.ap(), 1.0, -1.0, Alu.subtract, Alu.mult),
            [flag.key()], [invflag.key()])
        vop(lambda e: e.tensor_scalar(s0v.ap(), flag.ap(), 1.8446744073709552e19, 1.0,
                                      Alu.mult, Alu.add), [flag.key()], [s0v.key()])
        vop(lambda e: e.tensor_mul(pw2.ap(), s0v.ap(), s0v.ap()), [s0v.key()], [pw2.key()])
        vop(lambda e: e.tensor_mul(pw4.ap(), pw2.ap(), pw2.ap()), [pw2.key()], [pw4.key()])
        vop(lambda e: e.tensor_mul(pw8.ap(), pw4.ap(), pw4.ap()), [pw4.key()], [pw8.key()])
        vop(lambda e: e.tensor_scalar(S_v.ap(), pw8.ap(), 2.0 ** 64, None, Alu.mult),
            [pw8.key()], [S_v.key()])
        vop(lambda e: e.tensor_scalar(G_v.ap(), pw8.ap(), 2.0 ** 52, None, Alu.mult),
            [pw8.key()], [G_v.key()])
        vop(lambda e: e.tensor_mul(pwt.ap(), pw4.ap(), pw2.ap()), [pw4.key(), pw2.key()], [pwt.key()])
        vop(lambda e: e.tensor_scalar(T_v.ap(), pwt.ap(), float((2 - 2.0 ** -23) * 2.0 ** 63),
                                      None, Alu.mult), [pwt.key()], [T_v.key()])
        vop(lambda e: e.tensor_scalar(nT_v.ap(), T_v.ap(), -1.0, None, Alu.mult),
            [T_v.key()], [nT_v.key()])
        vop(lambda e: e.tensor_mul(pwt.ap(), pw8.ap(), pwt.ap()), [pw8.key(), pwt.key()], [pwt.key()])
        vop(lambda e: e.tensor_scalar(U_v.ap(), pwt.ap(), float((2 - 2.0 ** -23) * 2.0 ** 115),
                                      None, Alu.mult), [pwt.key()], [U_v.key()])
        vop(lambda e: e.tensor_scalar(nU_v.ap(), U_v.ap(), -1.0, None, Alu.mult),
            [U_v.key()], [nU_v.key()])
        rop(sIv.ap(), s0v.ap(), [s0v.key()], [sIv.key()])
        vop(lambda e: e.tensor_mul(pwt.ap(), sIv.ap(), sIv.ap()), [sIv.key()], [pwt.key()])
        vop(lambda e: e.tensor_mul(pw2.ap(), pwt.ap(), pwt.ap()), [pwt.key()], [pw2.key()])
        vop(lambda e: e.tensor_mul(pw4.ap(), pw2.ap(), pw2.ap()), [pw2.key()], [pw4.key()])
        vop(lambda e: e.tensor_mul(pw4.ap(), pw4.ap(), pw2.ap()), [pw2.key(), pw4.key()], [pw4.key()])
        vop(lambda e: e.tensor_mul(pw4.ap(), pw4.ap(), sIv.ap()), [sIv.key(), pw4.key()], [pw4.key()])
        vop(lambda e: e.tensor_scalar(F_v.ap(), pw4.ap(), 2.0 ** -125, None, Alu.mult),
            [pw4.key()], [F_v.key()])
        vop(lambda e: e.tensor_scalar(Mb_v.ap(), flag.ap(), float(np.float32(1.5 * 2.0 ** 52 - M23)),
                                      M23, Alu.mult, Alu.add), [flag.key()], [Mb_v.key()])
        vop(lambda e: e.tensor_scalar(nMb_v.ap(), Mb_v.ap(), -1.0, None, Alu.mult),
            [Mb_v.key()], [nMb_v.key()])
        vop(lambda e: e.tensor_scalar(nMb1_v.ap(), Mb_v.ap(), -1.0, 1.0, Alu.mult, Alu.add),
            [Mb_v.key()], [nMb1_v.key()])
        aop(lambda e: e.activation(gsig.ap(), tsg_r.ap(), Act.Sigmoid), [tsg_r.key()], [gsig.key()])
        vop(lambda e: e.tensor_scalar(gsig1m.ap(), gsig.ap(), 1.0, -1.0, Alu.subtract, Alu.mult),
            [gsig.key()], [gsig1m.key()])
        aop(lambda e: e.activation(tmpv.ap(0, DC), alog_r.ap(), Act.Abs), [alog_r.key()], [tmpv.key(0, DC)])
        aop(lambda e: e.activation(tmpv.ap(0, DC), tmpv.ap(0, DC), Act.Exp, scale=-1.0),
            [tmpv.key(0, DC)], [tmpv.key(0, DC)])
        aop(lambda e: e.activation(tmpv.ap(0, DC), tmpv.ap(0, DC), Act.Ln, bias=1.0),
            [tmpv.key(0, DC)], [tmpv.key(0, DC)])
        vop(lambda e: e.scalar_tensor_tensor(aneg.ap(), alog_r.ap(), 0.0, tmpv.ap(0, DC), Alu.max, Alu.add),
            [alog_r.key(), tmpv.key(0, DC)], [aneg.key()])
        vop(lambda e: e.tensor_scalar(aneg.ap(), aneg.ap(), -1.0, None, Alu.mult),
            [aneg.key()], [aneg.key()])
        aop(lambda e: e.activation(ppv(wv, p0=0, p1=64), ppv(tdv, p0=0, p1=64), Act.Exp),
            [tdv.key()], [wv.key()])
        aop(lambda e: e.activation(ppv(etf, p0=0, p1=64), ppv(tfv, p0=0, p1=64), Act.Exp),
            [tfv.key()], [etf.key()])
        vop(lambda e: e.tensor_scalar(ppv(negetf, p0=0, p1=64), ppv(etf, p0=0, p1=64), -1.0, None, Alu.mult),
            [etf.key()], [negetf.key()])

        snap_i = [0]

        def snap_vec(tag):
            if not any(d[0] == "snap" for d in debug):
                return
            i = snap_i[0]
            snap_i[0] += 1
            vop(lambda e, i=i: e.tensor_copy(srB.ap(0, 640), pf_t[:, 0:640]),
                [("pf", 0, 640)], [srB.key(0, 640)])
            dma_out(outs["dbg_snap"][:, i * 640:(i + 1) * 640], srB.ap(0, 640), srB.key(0, 640))

        snap_vec("after_aneg")
        kvstg = F_(PH + 2048, 2 * DC * N)
        dma_in(kvstg.ap(0, DC * N), kwt, kvstg.key(0, DC * N))
        dma_in(kvstg.ap(DC * N, 2 * DC * N), vwt, kvstg.key(DC * N, 2 * DC * N))
        for c in range(DC):
            ck = kvstg.key(c * N, (c + 1) * N)
            vop(lambda e, c=c: e.tensor_scalar(kp_bf.ap(c * N, (c + 1) * N),
                                               kvstg.ap(c * N, (c + 1) * N),
                                               ppv(lnw_v, c), None, Alu.mult),
                [ck, lnw_v.key()], [kp_bf.key(c * N, (c + 1) * N)])
            pop_(lambda e, c=c: e.matmul(psC_t[0:64, 0:1], kvstg.ap(c * N, (c + 1) * N),
                                         ppv(lnb_v, c), start=(c == 0), stop=(c == DC - 1)),
                 [ck, lnb_v.key()], [kC])
        vop(lambda e: e.tensor_copy(ppv(bkv, p0=0, p1=64), psC_t[0:64, 0:1]), [kC], [bkv.key()])
        for c in range(DC):
            ck = kvstg.key(DC * N + c * N, DC * N + (c + 1) * N)
            vop(lambda e, c=c: e.tensor_scalar(vpw_bf.ap(c * N, (c + 1) * N),
                                               kvstg.ap(DC * N + c * N, DC * N + (c + 1) * N),
                                               ppv(lnw_v, c), None, Alu.mult),
                [ck, lnw_v.key()], [vpw_bf.key(c * N, (c + 1) * N)])
            pop_(lambda e, c=c: e.matmul(psT_t[0:64, 0:1],
                                         kvstg.ap(DC * N + c * N, DC * N + (c + 1) * N),
                                         ppv(lnb_v, c), start=(c == 0), stop=(c == DC - 1)),
                 [ck, lnb_v.key()], [kT])
        vop(lambda e: e.tensor_copy(ppv(bvv, p0=0, p1=64), psT_t[0:64, 0:1]), [kT], [bvv.key()])
        vop(lambda e: e.tensor_mul(ppv(ekb, p0=0, p1=64), ppv(negetf, p0=0, p1=64),
                                   ppv(bkv, p0=0, p1=64)), [negetf.key(), bkv.key()], [ekb.key()])
        owtf = F_(PH + 2048 + N, D)
        dma_in(owtf.ap(p1=64), owt, owtf.key())
        vop(lambda e: e.tensor_copy(owt_bf.ap(), owtf.ap(p1=64)), [owtf.key()], [owt_bf.key()])

        absstg = Reg(pf_t, "pf", PH + 13824, 2048)

        def stream_abs_gen(hbm, rows, cols, sumreg, piece, dumpreg, first, stgreg=None):
            stgreg = absstg if stgreg is None else stgreg
            nchunk, npiece = rows // 128, cols // piece
            idx = 0
            for c in range(nchunk):
                for p in range(npiece):
                    stg = stgreg.sub((idx % 2) * piece, piece)
                    dma_in(stg.ap(), hbm[c * 128:(c + 1) * 128, p * piece:(p + 1) * piece], stg.key())
                    aop(lambda e, stg=stg: e.activation(dumpreg.ap(0, stg.n), stg.ap(), Act.Abs,
                                                        accum_out=tmpv.ap(6, 7)),
                        [stg.key()], [dumpreg.key(0, stg.n), tmpv.key(6, 7)])
                    if idx == 0 and first:
                        vop(lambda e: e.tensor_copy(sumreg.ap(), tmpv.ap(6, 7)),
                            [tmpv.key(6, 7)], [sumreg.key()])
                    else:
                        vop(lambda e: e.tensor_add(sumreg.ap(), sumreg.ap(), tmpv.ap(6, 7)),
                            [tmpv.key(6, 7), sumreg.key()], [sumreg.key()])
                    idx += 1
                    yield

        def stream_abs(hbm, rows, cols, sumreg, piece, dumpreg):
            for _ in stream_abs_gen(hbm, rows, cols, sumreg, piece, dumpreg, True):
                pass

        pump_gen = [None]

        def pump(n=1):
            g_ = pump_gen[0]
            if g_ is None:
                return
            for _ in range(n):
                if next(g_, "END") == "END":
                    pump_gen[0] = None
                    return

        def finish_scale(nm, numel, eps, mode):
            d = scl[nm]
            pop_(lambda e: e.matmul(psC_t[0:1, 0:1], d["sum"].ap(), ones_f.ap(),
                                    start=True, stop=True), [d["sum"].key(), ones_f.key()], [kC])
            vop(lambda e: e.tensor_copy(tmpv.ap(7, 8, 0, 1), psC_t[0:1, 0:1]), [kC], [tmpv.key(7, 8)])
            pop_(lambda e: e.matmul(psC_t[:, 0:1], r0(onesbcreg), tmpv.ap(7, 8, 0, 1),
                                    start=True, stop=True), [tmpv.key(7, 8), onesbcreg.key()], [kC])
            op1 = Alu.max if mode == "max" else Alu.add
            vop(lambda e: e.tensor_scalar(d["scale"].ap(), psC_t[:, 0:1], 1.0 / numel, eps,
                                          Alu.mult, op1), [kC], [d["scale"].key()])
            rop(d["r"].ap(), d["scale"].ap(), [d["scale"].key()], [d["r"].key()])
            vop(lambda e: e.tensor_scalar(d["negr"].ap(), d["r"].ap(), -1.0, None, Alu.mult),
                [d["r"].key()], [d["negr"].key()])

        def ternarize(dst_reg, dst_off, stg, off, ncols, rv, negrv, sa, sb):
            wk = stg.key(off, off + ncols)
            dk = dst_reg.key(dst_off, dst_off + ncols)
            aop(lambda e: e.activation(sa.ap(0, ncols), stg.ap(off, off + ncols), Act.Relu,
                                       scale=rv.ap(), bias=Mb_v.ap()),
                [wk, rv.key(), Mb_v.key()], [sa.key(0, ncols)])
            aop(lambda e: e.activation(sa.ap(0, ncols), sa.ap(0, ncols), Act.Relu,
                                       bias=nMb1_v.ap()),
                [sa.key(0, ncols), nMb1_v.key()], [sa.key(0, ncols)])
            vop(lambda e: e.tensor_scalar(dst_reg.ap(dst_off, dst_off + ncols),
                                          sa.ap(0, ncols), 2.0, -1.0, Alu.min, Alu.add),
                [sa.key(0, ncols)], [dk])

        snap_vec("after_p0")
        import itertools
        pump_gen[0] = itertools.chain(
            stream_abs_gen(w1t, D, FF, scl["w1"]["sum"], 512, dump1, True, stgreg=free8192),
            stream_abs_gen(w2t, FF, D, scl["w2"]["sum"], 512, dump1, True, stgreg=free8192))
        st.marker("p1")

        if any(d[0] == "vec0" for d in debug):
            vop(lambda e: e.tensor_copy(srA.ap(0, 640), pf_t[:, 0:640]), [("pf", 0, 640)], [srA.key(0, 640)])
            dma_out(outs["dbg_vec0"], srA.ap(0, 640), srA.key(0, 640))
        def st_sq(par):
            return (sqbf, par * GW) if par < 2 else (hq, (par - 2) * GW)

        def st_cs(par):
            return (sqbf, 1024 + par * GW) if par < 2 else (hq, 2048 + (par - 2) * GW)

        def stats_piece(c, g, par, sap, sk, bfap, bfk, alt=False):
            sqreg, sq0 = st_sq(par)
            sq1 = sq0 + GW
            aop(lambda e, s=sap, a=sq0, b=sq1, r=sqreg: e.activation(r.ap(a, b), s,
                                                                     Act.Square),
                [sk], [sqreg.key(sq0, sq1)])
            if bfap is None:
                csreg, cs0 = st_cs(par)
                cs1 = cs0 + GW
                vop(lambda e, s=sap, a=cs0, b=cs1, r=csreg: e.tensor_copy(r.ap(a, b), s),
                    [sk], [csreg.key(cs0, cs1)])
                bfap, bfk = csreg.ap(cs0, cs1), csreg.key(cs0, cs1)
            pop_(lambda e, c=c, g=g, x=bfap: e.matmul(
                psA_t[0:1, g * 512:(g + 1) * 512], onesb_bf.ap(0, 1), x,
                start=(c == 0), stop=(c == DC - 1)),
                [bfk, onesb_bf.key()], [kA(g * 512, (g + 1) * 512)])
            if alt:
                tgt, tk = psA_t[32:33, g * 512:(g + 1) * 512], kA(g * 512, (g + 1) * 512)
            elif g < 2:
                tgt, tk = psB_t[0:1, g * 512:(g + 1) * 512], kB(g * 512, (g + 1) * 512)
            elif g == 2:
                tgt, tk = psC_t[0:1, :], kC
            else:
                tgt, tk = psT_t[0:1, :], kT
            pop_(lambda e, c=c, t=tgt, a=sq0, b=sq1, r=sqreg: e.matmul(
                t, onesb_bf.ap(0, 1), r.ap(a, b),
                start=(c == 0), stop=(c == DC - 1)),
                [sqreg.key(sq0, sq1), onesb_bf.key()], [tk])

        def stats_final(alt=False):
            vop(lambda e: e.tensor_copy(r0(srA), psA_t[0:1, :]), [kA(0, 2048)], [srA.key()])
            if alt:
                vop(lambda e: e.tensor_copy(r0(srB), psA_t[32:33, :]), [kA(0, 2048)], [srB.key()])
            else:
                vop(lambda e: e.tensor_copy(r0(srB, 0, 1024), psB_t[0:1, :]), [kB(0, 1024)],
                    [srB.key(0, 1024)])
                vop(lambda e: e.tensor_copy(r0(srB, 1024, 1536), psC_t[0:1, :]), [kC],
                    [srB.key(1024, 1536)])
                vop(lambda e: e.tensor_copy(r0(srB, 1536, 2048), psT_t[0:1, :]), [kT],
                    [srB.key(1536, 2048)])
            vop(lambda e: e.tensor_scalar(r0(srA), r0(srA), 1.0 / D, None, Alu.mult),
                [srA.key()], [srA.key()])
            vop(lambda e: e.tensor_scalar(r0(srB), r0(srB), 1.0 / D, None, Alu.mult),
                [srB.key()], [srB.key()])
            vop(lambda e: e.tensor_mul(r0(srC), r0(srA), r0(srA)), [srA.key()], [srC.key()])
            vop(lambda e: e.tensor_sub(r0(srB), r0(srB), r0(srC)), [srB.key(), srC.key()], [srB.key()])
            aop(lambda e: e.activation(r0(srB), r0(srB), Act.Ln, bias=ppv(c_eps, 0, 0, 1)),
                [srB.key(), c_eps.key()], [srB.key()])
            aop(lambda e: e.activation(r0(srB), r0(srB), Act.Exp, scale=ppv(c_mhalf, 0, 0, 1)),
                [srB.key(), c_mhalf.key()], [srB.key()])
            vop(lambda e: e.tensor_mul(r0(srC), r0(srA), r0(srB)), [srA.key(), srB.key()], [srC.key()])
            vop(lambda e: e.tensor_scalar(r0(srC), r0(srC), -1.0, None, Alu.mult),
                [srC.key()], [srC.key()])

        def stats_from(get_src, nslots=2):
            for g in range(G):
                for c in range(DC):
                    pump(1)
                    par = (g * DC + c) % nslots
                    sap, sk, bfap, bfk = get_src(c, g, par)
                    stats_piece(c, g, par, sap, sk, bfap, bfk)
            stats_final()

        def bcast_AB(g):
            pop_(lambda e, g=g: e.matmul(psC_t[:, :], r0(onesbcreg), r0(srB, g * GW, (g + 1) * GW),
                                         start=True, stop=True), [srB.key(), onesbcreg.key()], [kC])
            pop_(lambda e, g=g: e.matmul(psT_t[:, :], r0(onesbcreg), r0(srC, g * GW, (g + 1) * GW),
                                         start=True, stop=True), [srC.key(), onesbcreg.key()], [kT])
            return psC_t[:, :], kC, psT_t[:, :], kT

        def xpiece(c, g, par):
            stg = wstage.sub(par * 512, GW)
            dma_in(stg.ap(), xT[c * 128:(c + 1) * 128, g * GW:(g + 1) * GW], stg.key())
            return stg

        pi = [0]
        _xpl = [(c, g) for g in range(G) for c in range(DC)]
        _xis = [0]

        def x_prefetch(upto):
            while _xis[0] < min(upto, len(_xpl)):
                i = _xis[0]
                c, g = _xpl[i]
                stg = wstage.sub((i % 4) * 512, GW)
                dma_in(stg.ap(), xT[c * 128:(c + 1) * 128, g * GW:(g + 1) * GW], stg.key())
                _xis[0] += 1

        def xnext():
            i = pi[0]
            x_prefetch(i + 4)
            stg = wstage.sub((i % 4) * 512, GW)
            pi[0] += 1
            return stg

        def ln1_src(c, g, par):
            stg = xnext()
            a, b = c * S + g * GW, c * S + (g + 1) * GW
            vop(lambda e, s=stg, a=a, b=b: e.tensor_copy(xn.ap(a, b), s.ap()),
                [stg.key()], [xn.key(a, b)])
            return (xn.ap(a, b), xn.key(a, b), xn.ap(a, b), xn.key(a, b))

        stats_from(ln1_src, nslots=4)
        for g in range(G):
            aap, ak, bap, bk = bcast_AB(g)
            for c in range(DC):
                pump(1)
                dk = xn.key(c * S + g * GW, c * S + (g + 1) * GW)
                tb0 = (c % 4) * 512
                vop(lambda e, c=c, g=g, a=aap, tb0=tb0: e.tensor_mul(
                    tbuf.ap(tb0, tb0 + GW), a, xn.ap(c * S + g * GW, c * S + (g + 1) * GW)),
                    [dk, ak], [tbuf.key(tb0, tb0 + GW)])
                vop(lambda e, c=c, g=g, b=bap, tb0=tb0: e.tensor_add(
                    xn.ap(c * S + g * GW, c * S + (g + 1) * GW), b, tbuf.ap(tb0, tb0 + GW)),
                    [tbuf.key(tb0, tb0 + GW), bk], [dk])

        if any(d[0] == "xn" for d in debug):
            for c in range(DC):
                vop(lambda e, c=c: e.tensor_copy(tbuf.ap(), xn.ap(c * S, (c + 1) * S)),
                    [xn.key(c * S, (c + 1) * S)], [tbuf.key()])
                dma_out(outs["dbg_xn"][c * 128:(c + 1) * 128, :], tbuf.ap(), tbuf.key())

        def mix_group(c, g):
            src_off = (g * GW + 1024) % S
            xck = xn.key(c * S + src_off, c * S + src_off + GW)
            xnk = xn.key(c * S + g * GW, c * S + (g + 1) * GW)
            dap, dk = tbuf.ap(0, GW), tbuf.key(0, GW)
            vop(lambda e, c=c, x=xn.ap(c * S + src_off, c * S + src_off + GW), d=dap:
                e.tensor_scalar(d, x, ppv(gsig, c), None, Alu.mult), [xck, gsig.key()], [dk])
            vop(lambda e, c=c, x=xn.ap(c * S + g * GW, c * S + (g + 1) * GW), d=dap:
                e.scalar_tensor_tensor(d, x, ppv(gsig1m, c), d, Alu.mult, Alu.add),
                [xnk, gsig1m.key(), dk], [dk])
            return dap, dk

        def shift_src(c, g, par):
            dap, dk = mix_group(c, g)
            t0, t1 = par * GW, (par + 1) * GW
            vop(lambda e, s=dap, a=t0, b=t1: e.tensor_copy(tb2b.ap(a, b), s),
                [dk], [tb2b.key(t0, t1)])
            return (dap, dk, tb2b.ap(t0, t1), tb2b.key(t0, t1))

        stats_from(shift_src)
        vop(lambda e: e.tensor_copy(wdec.ap(), ppv(wv, p0=0, p1=64).broadcast_to([64, S])),
            [wv.key()], [wdec.key()])
        for g in range(G):
            aap, ak, bap, bk = bcast_AB(g)
            for c in range(DC):
                pump(1)
                dap, dk = mix_group(c, g)
                vop(lambda e, d=dap, a=aap: e.tensor_mul(tbuf2.ap(0, GW), a, d),
                    [dk, ak], [tbuf2.key()])
                zk = zb.key(c * GW, (c + 1) * GW)
                vop(lambda e, c=c, b=bap: e.tensor_add(zb.ap(c * GW, (c + 1) * GW), b, tbuf2.ap(0, GW)),
                    [tbuf2.key(), bk], [zk])
            for c in range(DC):
                pop_(lambda e, c=c: e.matmul(psC_t[0:64, :], kp_bf.ap(c * N, (c + 1) * N),
                                             zb.ap(c * GW, (c + 1) * GW),
                                             start=(c == 0), stop=(c == DC - 1)),
                     [zb.key(c * GW, (c + 1) * GW), kp_bf.key(c * N, (c + 1) * N)], [kC])
            for c in range(DC):
                pop_(lambda e, c=c: e.matmul(psT_t[0:64, :], vpw_bf.ap(c * N, (c + 1) * N),
                                             zb.ap(c * GW, (c + 1) * GW),
                                             start=(c == 0), stop=(c == DC - 1)),
                     [zb.key(c * GW, (c + 1) * GW), vpw_bf.key(c * N, (c + 1) * N)], [kT])
            aop(lambda e: e.activation(ekbuf.ap(), psC_t[0:64, :], Act.Exp,
                                       bias=ppv(ekb, p0=0, p1=64), scale=ppv(negetf, p0=0, p1=64)),
                [kC, ekb.key(), negetf.key()], [ekbuf.key()])
            vop(lambda e, g=g: e.scalar_tensor_tensor(
                kvf.ap(g * GW, (g + 1) * GW), psT_t[0:64, :], ppv(bvv, p0=0, p1=64),
                ekbuf.ap(), Alu.add, Alu.mult),
                [kT, bvv.key(), ekbuf.key()], [kvf.key(g * GW, (g + 1) * GW)])
        vop(lambda e: e.tensor_tensor_scan(stf.ap(), wdec.ap(), kvf.ap(), 0.0, Alu.mult, Alu.add),
            [wdec.key(), kvf.key()], [stf.key()])
        vop(lambda e: e.tensor_copy(states_bf.ap(), stf.ap()), [stf.key()], [states_bf.key()])
        if any(d[0] == "states" for d in debug):
            dma_out(outs["dbg_states"], stf.ap(), stf.key())

        pump(200)
        assert pump_gen[0] is None, "W1/W2 abs pump must be drained here (frees free8192)" 
        finish_scale("w1", D * FF, 1e-8, "max")
        finish_scale("w2", D * FF, 1e-8, "max")
        pump_gen[0] = itertools.chain(
            stream_abs_gen(wxt, 2 * D, D, scl["wx"]["sum"], 1024, sqbf, True),
            stream_abs_gen(wdtt, D, D, scl["wdt"]["sum"], 1024, sqbf, True),
            stream_abs_gen(wft, D, D, scl["wf"]["sum"], 1024, sqbf, True))
        QE = FF // 8

        def tern_thunks(e8):
            buf = e8 % 2
            w1q8 = warena.sub(buf * 8192, 4096)
            w2q8 = warena.sub(buf * 8192 + 4096, 4096)
            th = []
            for c in range(DC):
                def t1(c=c, buf=buf, w1q8=w1q8, e8=e8):
                    stg = wstage.sub((c % 2) * 1024, 512)
                    dma_in(stg.ap(0, 512), w1t[c * 128:(c + 1) * 128, e8 * QE:(e8 + 1) * QE],
                           stg.key(0, 512))
                    ternarize(w1q8, c * 512, stg, 0, 512, scl["w1"]["r"], scl["w1"]["negr"],
                              free8192.sub((c % 2) * 512, 512), tbuf2)
                th.append(t1)
            for j4 in range(4):
                def t2(j4=j4, buf=buf, w2q8=w2q8, e8=e8):
                    fc = e8 * 4 + j4
                    stg = wstage.sub((j4 % 2) * 1024, 1024)
                    dma_in(stg.ap(), w2t[fc * 128:(fc + 1) * 128, :], stg.key())
                    ternarize(w2q8, j4 * 1024, stg, 0, 1024, scl["w2"]["r"], scl["w2"]["negr"],
                              tbuf.sub((j4 % 2) * 1024, 1024), tbuf2)
                th.append(t2)
            return th

        for th in tern_thunks(0):
            th()
        for e8 in range(8):
            buf = e8 % 2
            w1q8 = warena.sub(buf * 8192, 4096)
            w2q8 = warena.sub(buf * 8192 + 4096, 4096)
            h8 = hq.sub(buf * 2048, 2048)
            nxt = tern_thunks(e8 + 1) if e8 + 1 < 8 else []
            for g in range(G):
                for th in nxt[g * 3:(g + 1) * 3]:
                    th()
                pump(1)
                for j in range(4):
                    for c in range(DC):
                        pop_(lambda e, j=j, c=c, g=g, w1q8=w1q8: e.matmul(
                            psA_t[:, j * 512:(j + 1) * 512],
                            w1q8.ap(c * 512 + j * 128, c * 512 + (j + 1) * 128),
                            xn.ap(c * S + g * GW, c * S + (g + 1) * GW),
                            start=(c == 0), stop=(c == DC - 1)),
                            [w1q8.key(c * 512 + j * 128, c * 512 + (j + 1) * 128),
                             xn.key(c * S + g * GW, c * S + (g + 1) * GW)],
                            [kA(j * 512, (j + 1) * 512)])
                for j in range(4):
                    aop(lambda e, j=j, e8=e8, h8=h8: e.activation(
                        h8.ap(j * GW, (j + 1) * GW), psA_t[:, j * 512:(j + 1) * 512],
                        Act.Gelu, bias=ppv(b1_v, e8 * 4 + j)),
                        [kA(j * 512, (j + 1) * 512), b1_v.key()],
                        [h8.key(j * GW, (j + 1) * GW)])
                for c in range(DC):
                    pbk = (c % 2) * 512
                    for j in range(4):
                        pop_(lambda e, c=c, j=j, pbk=pbk, w2q8=w2q8, h8=h8: e.matmul(
                            psB_t[:, pbk:pbk + 512],
                            w2q8.ap(j * 1024 + c * 128, j * 1024 + (c + 1) * 128),
                            h8.ap(j * GW, (j + 1) * GW),
                            start=(j == 0), stop=(j == 3)),
                            [w2q8.key(j * 1024 + c * 128, j * 1024 + (c + 1) * 128),
                             h8.key(j * GW, (j + 1) * GW)], [kB(pbk, pbk + 512)])
                    oa = ro_r(c, g)
                    if e8 == 0:
                        vop(lambda e, c=c, pbk=pbk, oa=oa: e.tensor_scalar(
                            oa.ap(), psB_t[:, pbk:pbk + 512], ppv(b2_v, c), None, Alu.add),
                            [kB(pbk, pbk + 512), b2_v.key()], [oa.key()])
                    else:
                        vop(lambda e, pbk=pbk, oa=oa: e.tensor_add(
                            oa.ap(), psB_t[:, pbk:pbk + 512], oa.ap()),
                            [kB(pbk, pbk + 512), oa.key()], [oa.key()])
        pump(200)
        if any(d[0] == "ffn" for d in debug):
            for g in range(G):
                for c in range(DC):
                    oa = ro_r(c, g)
                    dma_out(outs["dbg_ffn"][c * 128:(c + 1) * 128, g * GW:(g + 1) * GW],
                            oa.ap(), oa.key())
        ep_xslots = [wstage.sub(p * 512, GW) for p in range(4)] + \
                    [free8192.sub(p * 512, GW) for p in range(2)]
        ep_silu = [F_(PH + 11776, GW), F_(PH + 12288, GW)]
        ep_cg = [(g, c) for g in range(G) for c in range(DC)]
        p5slots = [srA.sub(0, 1024), srA.sub(1024, 1024), srB.sub(0, 1024), srB.sub(1024, 1024)]
        p5src = [(wdtt, c) for c in range(DC)] + [(wft, c) for c in range(DC)]

        def _p5dma(i):
            hbm, c = p5src[i]
            stg = p5slots[i % 4]
            dma_in(stg.ap(), hbm[c * 128:(c + 1) * 128, :], stg.key())
        p5dma = [lambda i=i: _p5dma(i) for i in range(len(p5src))]

        def ep_xdma(i):
            g, c = ep_cg[i]
            stg = ep_xslots[i % 6]
            dma_in(stg.ap(), xT[c * 128:(c + 1) * 128, g * GW:(g + 1) * GW], stg.key())

        for i in range(6):
            ep_xdma(i)
        for i, (g, c) in enumerate(ep_cg):
            psX, kX = (psC_t, kC) if i % 2 == 0 else (psT_t, kT)
            pop_(lambda e, c=c, g=g, p=psX: e.matmul(
                p[:, :], owt_bf.ap(c * 128, (c + 1) * 128),
                states_bf.ap(g * GW, (g + 1) * GW), start=True, stop=True),
                [owt_bf.key(c * 128, (c + 1) * 128), states_bf.key(g * GW, (g + 1) * GW)], [kX])
            oa = ro_r(c, g)
            sil = ep_silu[i % 2]
            aop(lambda e, oa=oa, sil=sil: e.activation(sil.ap(), oa.ap(), Act.Silu),
                [oa.key()], [sil.key()])
            stg = ep_xslots[i % 6]
            tb0 = (i % 4) * 512
            vop(lambda e, p=psX, sil=sil, tb0=tb0: e.tensor_mul(tbuf.ap(tb0, tb0 + GW),
                                                                p[:, :], sil.ap()),
                [kX, sil.key()], [tbuf.key(tb0, tb0 + GW)])
            vop(lambda e, s=stg, tb0=tb0: e.tensor_add(tbuf.ap(tb0, tb0 + GW),
                                                       tbuf.ap(tb0, tb0 + GW), s.ap()),
                [tbuf.key(tb0, tb0 + GW), stg.key()], [tbuf.key(tb0, tb0 + GW)])
            if i + 6 < len(ep_cg):
                ep_xdma(i + 6)
            dma_out(x1sp[c * 128:(c + 1) * 128, g * GW:(g + 1) * GW],
                    tbuf.ap(tb0, tb0 + GW), tbuf.key(tb0, tb0 + GW),
                    hbmkey=("x1sp", c * S + g * GW, c * S + (g + 1) * GW))
            vop(lambda e, oa=oa, tb0=tb0: e.tensor_copy(oa.ap(), tbuf.ap(tb0, tb0 + GW)),
                [tbuf.key(tb0, tb0 + GW)], [oa.key()])
            if i % 2 == 0 and i // 2 < len(p5dma):
                p5dma[i // 2]()
            if any(d[0] == "x1" for d in debug):
                dma_out(outs["dbg_x1"][c * 128:(c + 1) * 128, g * GW:(g + 1) * GW],
                        tbuf.ap(tb0, tb0 + GW), tbuf.key(tb0, tb0 + GW))

        if any(d[0] == "w1q" for d in debug):
            dma_out(outs["dbg_w1q"], warena.ap(0, 8192), warena.key(0, 8192))
            dma_out(outs["dbg_w2q"], warena.ap(8192, 16384), warena.key(8192, 16384))
        if any(d[0] == "hq" for d in debug):
            dma_out(outs["dbg_hq"], hq.ap(), hq.key())
        if any(d[0] == "vec1" for d in debug):
            vop(lambda e: e.tensor_copy(srA.ap(0, 640), pf_t[:, 0:640]), [("pf", 0, 640)], [srA.key(0, 640)])
            dma_out(outs["dbg_vec1"], srA.ap(0, 640), srA.key(0, 640))
        if any(d[0] == "pn" for d in debug):
            dma_out(outs["dbg_pn"][:, 0:512], tbuf.ap(0, 512), tbuf.key(0, 512))
            dma_out(outs["dbg_pn"][:, 512:1024], tbuf2.ap(0, 512), tbuf2.key(0, 512))
            for i, nm in enumerate(("w1", "w2")):
                for j, f in enumerate(("scale", "r", "negr")):
                    vop(lambda e, i=i, j=j, nm=nm, f=f: e.tensor_copy(
                        tbuf.ap(600 + i * 3 + j, 601 + i * 3 + j), scl[nm][f].ap()),
                        [scl[nm][f].key()], [tbuf.key(600 + i * 3 + j, 601 + i * 3 + j)])
            dma_out(outs["dbg_pn"][:, 1024:1030], tbuf.ap(600, 606), tbuf.key(600, 606))
        if any(d[0] == "sums" for d in debug):
            for i, nm in enumerate(("w1", "w2", "wx", "wdt", "wf")):
                vop(lambda e, i=i, nm=nm: e.tensor_copy(tbuf2.ap(i, i + 1), scl[nm]["sum"].ap()),
                    [scl[nm]["sum"].key()], [tbuf2.key(i, i + 1)])
            dma_out(outs["dbg_sums"], tbuf2.ap(0, 5), tbuf2.key(0, 5))
        finish_scale("wx", 2 * D * D, 1e-5, "add")
        finish_scale("wdt", D * D, 1e-5, "add")
        finish_scale("wf", D * D, 1e-5, "max")
        wdtq = warena.sub(0, 8192)
        wfq = warena.sub(8192, 8192)
        p5sa = [free8192.sub(0, 512), free8192.sub(512, 512), tern7a, tern7b]
        for i in range(len(p5src)):
            stg = p5slots[i % 4]
            dstq = wdtq if i < DC else wfq
            nm = "wdt" if i < DC else "wf"
            c = p5src[i][1]
            for pc in range(2):
                ternarize(dstq, c * 1024 + pc * 512, stg, pc * 512, 512,
                          scl[nm]["r"], scl[nm]["negr"], p5sa[(i * 2 + pc) % 4], tbuf2)

        assert pump_gen[0] is None, "L1 abs pump must drain before P6 uses xf1 region"
        def x1_src(c, g, par):
            oa = ro_r(c, g)
            return (oa.ap(), oa.key(), oa.ap(), oa.key())

        stats_from(x1_src, nslots=4)
        for g in range(G):
            aap, ak, bap, bk = bcast_AB(g)
            for c in range(DC):
                oa = ro_r(c, g)
                xk = xf1.key(g * GW, (g + 1) * GW)
                vop(lambda e, oa=oa, g=g, a=aap: e.tensor_mul(xf1.ap(g * GW, (g + 1) * GW), a, oa.ap()),
                    [oa.key(), ak], [xk])
                vop(lambda e, g=g, b=bap: e.tensor_add(xf1.ap(g * GW, (g + 1) * GW),
                                                       b, xf1.ap(g * GW, (g + 1) * GW)),
                    [xk, bk], [xk])
                vop(lambda e, c=c, g=g: e.tensor_copy(xn.ap(c * S + g * GW, c * S + (g + 1) * GW),
                                                      xf1.ap(g * GW, (g + 1) * GW)),
                    [xk], [xn.key(c * S + g * GW, c * S + (g + 1) * GW)])
                vop(lambda e, g=g: e.scalar_tensor_tensor(xf1.ap(g * GW, (g + 1) * GW),
                                                          xf1.ap(g * GW, (g + 1) * GW), -1.0,
                                                          xf1.ap(g * GW, (g + 1) * GW),
                                                          Alu.mult, Alu.max), [xk], [xk])
                if c == 0:
                    vop(lambda e, g=g: e.tensor_copy(fold.ap(g * GW, (g + 1) * GW),
                                                     xf1.ap(g * GW, (g + 1) * GW)),
                        [xk], [fold.key(g * GW, (g + 1) * GW)])
                else:
                    vop(lambda e, g=g: e.tensor_max(fold.ap(g * GW, (g + 1) * GW),
                                                    fold.ap(g * GW, (g + 1) * GW),
                                                    xf1.ap(g * GW, (g + 1) * GW)),
                        [xk, fold.key(g * GW, (g + 1) * GW)], [fold.key(g * GW, (g + 1) * GW)])
        for bI in range(16):
            top_(lambda e, b_=bI: e.transpose(psT_t[:, 0:128],
                                              fold.ap(b_ * 128, (b_ + 1) * 128), idreg.ap()),
                 [fold.key(bI * 128, (bI + 1) * 128), idreg.key()], [kT])
            vop(lambda e, b_=bI: e.tensor_reduce(amaxt.ap(b_, b_ + 1), psT_t[:, 0:128], AxX, Alu.max),
                [kT], [amaxt.key(bI, bI + 1)])
        vop(lambda e: e.tensor_scalar(amaxt.ap(), amaxt.ap(), 1e-5, None, Alu.max),
            [amaxt.key()], [amaxt.key()])
        rop(sptk.ap(), amaxt.ap(), [amaxt.key()], [sptk.key()])
        vop(lambda e: e.tensor_scalar(sptk.ap(), sptk.ap(), 127.0, None, Alu.mult),
            [sptk.key()], [sptk.key()])
        vop(lambda e: e.tensor_scalar(rhot.ap(), amaxt.ap(), 1.0 / 127.0, None, Alu.mult),
            [amaxt.key()], [rhot.key()])
        if any(d[0] == "fold" for d in debug):
            dma_out(outs["dbg_fold"], fold.ap(), fold.key())
            dma_out(outs["dbg_amax"], amaxt.ap(), amaxt.key())
        dma_in(onehreg.ap(), c_oneh, onehreg.key())
        top_(lambda e: e.transpose(psT_t[0:16, 0:128], sptk.ap(), idreg.ap()),
             [sptk.key(), idreg.key()], [kT])
        vop(lambda e: e.tensor_copy(v16sp.ap(), psT_t[0:16, 0:128]), [kT], [v16sp.key()])
        top_(lambda e: e.transpose(psT_t[0:16, 0:128], rhot.ap(), idreg.ap()),
             [rhot.key(), idreg.key()], [kT])
        vop(lambda e: e.tensor_copy(v16rh.ap(), psT_t[0:16, 0:128]), [kT], [v16rh.key()])
        for bI in range(16):
            pop_(lambda e, b_=bI: e.matmul(psA_t[:, b_ * 128:(b_ + 1) * 128],
                                           onehreg.ap(b_ * 128, (b_ + 1) * 128), v16sp.ap(),
                                           start=True, stop=True),
                 [onehreg.key(), v16sp.key()], [kA(bI * 128, (bI + 1) * 128)])
        aop(lambda e: e.activation(spbc.ap(), psA_t[:, :], Act.Copy), [kA(0, 2048)], [spbc.key()])
        for bI in range(16):
            pop_(lambda e, b_=bI: e.matmul(psA_t[:, b_ * 128:(b_ + 1) * 128],
                                           onehreg.ap(b_ * 128, (b_ + 1) * 128), v16rh.ap(),
                                           start=True, stop=True),
                 [onehreg.key(), v16rh.key()], [kA(bI * 128, (bI + 1) * 128)])
        vop(lambda e: e.tensor_scalar(wsxr.ap(), psA_t[:, :], scl["wx"]["scale"].ap(), None, Alu.mult),
            [kA(0, 2048), scl["wx"]["scale"].key()], [wsxr.key()])
        if any(d[0] == "spbc" for d in debug):
            dma_out(outs["dbg_spbc"], pf_t[:, spbc.lo:spbc.lo + S], spbc.key())
        vop(lambda e: e.tensor_copy(sprow.ap(p1=1), r0(spbc, 0, S)), [spbc.key()], [sprow.key()])
        dma_in(r0(tbuf, 0, 2 * D), v_bx.unsqueeze(0), tbuf.key())
        vop(lambda e: e.tensor_scalar(bxrow.ap(p1=1), r0(tbuf, 0, 2 * D),
                                      ppv(scl["wx"]["r"], 0, 0, 1), None, Alu.mult),
            [tbuf.key(), scl["wx"]["r"].key()], [bxrow.key()])
        dma_in(r0(tbuf, 0, D), v_bdt.unsqueeze(0), tbuf.key(0, D))
        vop(lambda e: e.tensor_scalar(bdrow.ap(p1=1), r0(tbuf, 0, D),
                                      ppv(scl["wdt"]["r"], 0, 0, 1), None, Alu.mult),
            [tbuf.key(0, D), scl["wdt"]["r"].key()], [bdrow.key()])
        for g in range(G):
            aap, ak, bap, bk = bcast_AB(g)
            for c in range(DC):
                oa = ro_r(c, g)
                xk = xf1.key(0, GW)
                vop(lambda e, oa=oa, a=aap: e.tensor_mul(xf1.ap(0, GW), a, oa.ap()),
                    [oa.key(), ak], [xk])
                vop(lambda e, b=bap: e.tensor_add(xf1.ap(0, GW), b, xf1.ap(0, GW)),
                    [xk, bk], [xk])
                vop(lambda e, g=g: e.tensor_mul(xf1.ap(0, GW), xf1.ap(0, GW),
                                                spbc.ap(g * GW, (g + 1) * GW)),
                    [xk, spbc.key()], [xk])
                vop(lambda e: e.tensor_scalar(xf1.ap(0, GW), xf1.ap(0, GW), Mb_v.ap(), nMb_v.ap(),
                                              Alu.add, Alu.add),
                    [xk, Mb_v.key(), nMb_v.key()], [xk])
                vop(lambda e, oa=oa: e.tensor_scalar(oa.ap(), xf1.ap(0, GW), -128.0, 127.0,
                                                     Alu.max, Alu.min), [xk], [oa.key()])
        if any(d[0] == "rq" for d in debug):
            for c in range(DC):
                for g in range(G):
                    oa = ro_r(c, g)
                    dma_out(outs["dbg_rq"][c * 128:(c + 1) * 128, g * GW:(g + 1) * GW], oa.ap(), oa.key())
        for bI in range(16):
            pop_(lambda e, b_=bI: e.matmul(psA_t[:, b_ * 128:(b_ + 1) * 128],
                                           onehreg.ap(b_ * 128, (b_ + 1) * 128), v16rh.ap(),
                                           start=True, stop=True),
                 [onehreg.key(), v16rh.key()], [kA(bI * 128, (bI + 1) * 128)])
        vop(lambda e: e.tensor_scalar(wsdr.ap(), psA_t[:, :], scl["wdt"]["scale"].ap(), None, Alu.mult),
            [kA(0, 2048), scl["wdt"]["scale"].key()], [wsdr.key()])

        vop(lambda e: e.memset(zerot.ap(), 0.0), writes=[zerot.key()])
        b1r, b2r, b3r, b4r, b5r = Bs

        def wx_dma(oc, par):
            stg = wstage.sub(par * 1024, 1024)
            for q in range(4):
                dma_in(stg.ap(q * 256, (q + 1) * 256),
                       wxt[oc * 128:(oc + 1) * 128, q * 256:(q + 1) * 256],
                       stg.key(q * 256, (q + 1) * 256))

        def wx_tern(oc, par):
            stg = wstage.sub(par * 1024, 1024)
            wq = wxs.sub(par * 1024, 1024)
            for pc in range(2):
                ternarize(wq, pc * 512, stg, pc * 512, 512,
                          scl["wx"]["r"], scl["wx"]["negr"], (tern7a, tern7b)[pc], tbuf2)

        def l1_mm(c):
            for g in range(G):
                for dc2 in range(DC):
                    pop_(lambda e, c=c, g=g, dc2=dc2: e.matmul(
                        psA_t[:, g * 512:(g + 1) * 512],
                        wdtq.ap(dc2 * 1024 + c * 128, dc2 * 1024 + (c + 1) * 128),
                        ro_r(dc2, g).ap(), start=(dc2 == 0), stop=False),
                        [wdtq.key(dc2 * 1024 + c * 128, dc2 * 1024 + (c + 1) * 128),
                         ro_r(dc2, g).key()], [kA(g * 512, (g + 1) * 512)])
                pop_(lambda e, c=c, g=g: e.matmul(
                    psA_t[:, g * 512:(g + 1) * 512], bdrow.ap(c * 128, (c + 1) * 128, 0, 1),
                    sprow.ap(g * GW, (g + 1) * GW, 0, 1), start=False, stop=True),
                    [bdrow.key(c * 128, (c + 1) * 128), sprow.key(g * GW, (g + 1) * GW)],
                    [kA(g * 512, (g + 1) * 512)])
                vop(lambda e, g=g: e.tensor_mul(b1r.ap(g * 512, (g + 1) * 512),
                                                psA_t[:, g * 512:(g + 1) * 512],
                                                wsdr.ap(g * 512, (g + 1) * 512)),
                    [kA(g * 512, (g + 1) * 512), wsdr.key()], [b1r.key(g * 512, (g + 1) * 512)])
            for oc, dst, par in ((c, b2r, 0), (c + 8, b3r, 1)):
                wq = wxs.sub(par * 1024, 1024)
                for g in range(G):
                    for dc2 in range(DC):
                        pop_(lambda e, g=g, dc2=dc2, wq=wq: e.matmul(
                            psA_t[:, g * 512:(g + 1) * 512], wq.ap(dc2 * 128, (dc2 + 1) * 128),
                            ro_r(dc2, g).ap(), start=(dc2 == 0), stop=False),
                            [wq.key(dc2 * 128, (dc2 + 1) * 128), ro_r(dc2, g).key()],
                            [kA(g * 512, (g + 1) * 512)])
                    pop_(lambda e, oc=oc, g=g: e.matmul(
                        psA_t[:, g * 512:(g + 1) * 512], bxrow.ap(oc * 128, (oc + 1) * 128, 0, 1),
                        sprow.ap(g * GW, (g + 1) * GW, 0, 1), start=False, stop=True),
                        [bxrow.key(oc * 128, (oc + 1) * 128), sprow.key(g * GW, (g + 1) * GW)],
                        [kA(g * 512, (g + 1) * 512)])
                    vop(lambda e, dst=dst, g=g: e.tensor_mul(dst.ap(g * 512, (g + 1) * 512),
                                                             psA_t[:, g * 512:(g + 1) * 512],
                                                             wsxr.ap(g * 512, (g + 1) * 512)),
                        [kA(g * 512, (g + 1) * 512), wsxr.key()],
                        [dst.key(g * 512, (g + 1) * 512)])
                    vop(lambda e, dst=dst, g=g: e.tensor_scalar(dst.ap(g * 512, (g + 1) * 512),
                                                                dst.ap(g * 512, (g + 1) * 512),
                                                                5.0, -5.0, Alu.min, Alu.max),
                        [dst.key(g * 512, (g + 1) * 512)], [dst.key(g * 512, (g + 1) * 512)])
            if any(d[0] == "l1t" for d in debug) and c == 0:
                dma_out(outs["dbg_l1t"][:, 0:S], b1r.ap(), b1r.key())
                dma_out(outs["dbg_l1t"][:, S:2 * S], b2r.ap(), b2r.key())
                dma_out(outs["dbg_l1t"][:, 2 * S:3 * S], b3r.ap(), b3r.key())
        def l1_chain(c, tern_thunk=None):
            for fn, scale, bias in ((Act.Abs, None, None), (Act.Exp, -1.0, None),
                                    (Act.Ln, None, 1.0)):
                for g in range(G):
                    s_, src_ = ((b1r, b5r) if fn == Act.Abs else (b5r, b5r))
                    aop(lambda e, fn=fn, scale=scale, bias=bias, g=g, s_=s_:
                        e.activation(b5r.ap(g * 512, (g + 1) * 512),
                                     s_.ap(g * 512, (g + 1) * 512) if fn != Act.Abs
                                     else b1r.ap(g * 512, (g + 1) * 512), fn,
                                     **({} if scale is None else {"scale": scale}),
                                     **({} if bias is None else {"bias": bias})),
                        [(b1r if fn == Act.Abs else b5r).key(g * 512, (g + 1) * 512)],
                        [b5r.key(g * 512, (g + 1) * 512)])
            if tern_thunk is not None:
                tern_thunk()
            vop(lambda e: e.scalar_tensor_tensor(b1r.ap(), b1r.ap(), 0.0, b5r.ap(),
                                                 Alu.max, Alu.add), [b1r.key(), b5r.key()], [b1r.key()])
            vop(lambda e: e.tensor_scalar(b1r.ap(), b1r.ap(), 0.001, 1.0, Alu.max, Alu.min),
                [b1r.key()], [b1r.key()])
            vop(lambda e, c=c: e.tensor_scalar(b4r.ap(), b1r.ap(), ppv(aneg, c), -10.0,
                                               Alu.mult, Alu.max), [b1r.key(), aneg.key()], [b4r.key()])
            aop(lambda e: e.activation(b4r.ap(), b4r.ap(), Act.Exp), [b4r.key()], [b4r.key()])
            aop(lambda e: e.activation(b3r.ap(), b3r.ap(), Act.Tanh), [b3r.key()], [b3r.key()])
            vop(lambda e: e.tensor_mul(b2r.ap(), b1r.ap(), b2r.ap()), [b1r.key(), b2r.key()], [b2r.key()])
            vop(lambda e: e.tensor_tensor_scan(b4r.ap(), b4r.ap(), zerot.ap(), 1.0,
                                               Alu.mult, Alu.add), [b4r.key(), zerot.key()], [b4r.key()])
            if any(d[0] == "l1t" for d in debug) and c == 0:
                dma_out(outs["dbg_l1t"][:, 3 * S:4 * S], b4r.ap(), b4r.key())
            vop(lambda e: e.tensor_copy(b5r.ap(0, 1), S_v.ap()), [S_v.key()], [b5r.key(0, 1)])
            vop(lambda e: e.tensor_scalar(b5r.ap(1, S), b4r.ap(0, S - 1), S_v.ap(), F_v.ap(),
                                          Alu.mult, Alu.max),
                [b4r.key(), S_v.key(), F_v.key()], [b5r.key(1, S)])
            rop(b5r.ap(), b5r.ap(), [b5r.key()], [b5r.key()])
            vop(lambda e: e.tensor_mul(b2r.ap(), b2r.ap(), b5r.ap()), [b2r.key(), b5r.key()], [b2r.key()])
            vop(lambda e: e.tensor_scalar(b5r.ap(), b2r.ap(), T_v.ap(), nT_v.ap(), Alu.min, Alu.max),
                [b2r.key(), T_v.key(), nT_v.key()], [b5r.key()])
            vop(lambda e: e.tensor_tensor(b1r.ap(), b5r.ap(), b2r.ap(), Alu.is_equal),
                [b5r.key(), b2r.key()], [b1r.key()])
            vop(lambda e: e.tensor_scalar(b2r.ap(), b5r.ap(), G_v.ap(), None, Alu.mult),
                [b5r.key(), G_v.key()], [b2r.key()])
            vop(lambda e: e.tensor_tensor_scan(b2r.ap(), b2r.ap(), zerot.ap(), 0.0,
                                               Alu.add, Alu.add), [b2r.key(), zerot.key()], [b2r.key()])
            vop(lambda e: e.scalar_tensor_tensor(b5r.ap(), b2r.ap(), -1.0, b2r.ap(),
                                                 Alu.mult, Alu.max), [b2r.key()], [b5r.key()])
            vop(lambda e: e.tensor_scalar(b5r.ap(), b5r.ap(), U_v.ap(), None, Alu.is_le),
                [b5r.key(), U_v.key()], [b5r.key()])
            vop(lambda e: e.tensor_tensor_scan(b1r.ap(), b1r.ap(), b5r.ap(), 4096.0,
                                               Alu.mult, Alu.mult), [b1r.key(), b5r.key()], [b1r.key()])
            vop(lambda e: e.tensor_mul(b2r.ap(), b4r.ap(), b2r.ap()), [b4r.key(), b2r.key()], [b2r.key()])
            vop(lambda e: e.tensor_scalar(b2r.ap(), b2r.ap(), U_v.ap(), nU_v.ap(), Alu.min, Alu.max),
                [b2r.key(), U_v.key(), nU_v.key()], [b2r.key()])
            vop(lambda e: e.tensor_mul(b2r.ap(), b2r.ap(), b3r.ap()), [b2r.key(), b3r.key()], [b2r.key()])
            vop(lambda e: e.tensor_mul(b2r.ap(), b2r.ap(), b1r.ap()), [b2r.key(), b1r.key()], [b2r.key()])
            if any(d[0] == "l1t" for d in debug) and c == 0:
                dma_out(outs["dbg_l1t"][:, 4 * S:5 * S], b2r.ap(), b2r.key())
            g1sl = [(psC_t[:, :], kC), (psT_t[:, :], kT),
                    (free8192.ap(0, GW), free8192.key(0, GW)),
                    (free8192.ap(GW, 2 * GW), free8192.key(GW, 2 * GW))]
            for g in range(G):
                pbk = (g % 2) * 512
                for dc2 in range(DC):
                    pop_(lambda e, c=c, g=g, dc2=dc2, pbk=pbk: e.matmul(
                        psB_t[:, pbk:pbk + 512],
                        wfq.ap(dc2 * 1024 + c * 128, dc2 * 1024 + (c + 1) * 128),
                        xn.ap(dc2 * S + g * GW, dc2 * S + (g + 1) * GW),
                        start=(dc2 == 0), stop=(dc2 == DC - 1)),
                        [wfq.key(dc2 * 1024 + c * 128, dc2 * 1024 + (c + 1) * 128),
                         xn.key(dc2 * S + g * GW, dc2 * S + (g + 1) * GW)], [kB(pbk, pbk + 512)])
                gap, gk = g1sl[g]
                aop(lambda e, g=g, pbk=pbk, gap=gap: e.activation(
                    gap, psB_t[:, pbk:pbk + 512], Act.Silu,
                    scale=scl["wf"]["scale"].ap()),
                    [kB(pbk, pbk + 512), scl["wf"]["scale"].key()], [gk])
            for g in range(G):
                gap, gk = g1sl[g]
                vop(lambda e, g=g, gap=gap: e.tensor_mul(b2r.ap(g * GW, (g + 1) * GW),
                                                         b2r.ap(g * GW, (g + 1) * GW), gap),
                    [b2r.key(g * GW, (g + 1) * GW), gk], [b2r.key(g * GW, (g + 1) * GW)])
            if any(d[0] == "l1t" for d in debug) and c == 0:
                for g in range(G):
                    gap, gk = g1sl[g]
                    vop(lambda e, g=g, gap=gap: e.tensor_copy(b4r.ap(g * GW, (g + 1) * GW), gap),
                        [gk], [b4r.key(g * GW, (g + 1) * GW)])
                dma_out(outs["dbg_l1t"][:, 5 * S:6 * S], b4r.ap(), b4r.key())
            dma_rd(b5r.ap(), x1sp[c * 128:(c + 1) * 128, :], b5r.key(),
                   hbmkey=("x1sp", c * S, (c + 1) * S))
            vop(lambda e: e.tensor_add(b2r.ap(), b2r.ap(), b5r.ap()), [b2r.key(), b5r.key()], [b2r.key()])
            if any(d[0] == "y0" for d in debug) and c == 0:
                dma_out(outs["dbg_y0"], b2r.ap(), b2r.key())
            dma_out(outT[c * 128:(c + 1) * 128, :], b2r.ap(), b2r.key())

        wx_dma(0, 0)
        wx_dma(8, 1)
        wx_tern(0, 0)
        wx_tern(8, 1)
        wx_dma(1, 0)
        wx_dma(9, 1)
        for c in range(DC):
            l1_mm(c)
            def _tern(c=c):
                if c + 1 < DC:
                    wx_tern(c + 1, 0)
                    wx_tern(c + 9, 1)
                    if c + 2 < DC:
                        wx_dma(c + 2, 0)
                        wx_dma(c + 10, 1)
            l1_chain(c, tern_thunk=_tern)

        if stop is not None:
            st.truncate(stop)
        if any(d[0] == "vsync" for d in debug):
            allw = [(s_, st.cnt[s_]) for s_ in Sched.STREAMS]
            st.op("din0", (lambda e, sem: e.dma_start(out=outs["dbg_vsync"], in_=pf_t[:, 0:640])
                           .then_inc(sem, 16)), inc=16, extra=allw)
            st.op("dout0", (lambda e, sem: e.dma_start(out=outs["dbg_vgp"], in_=pf_t[:, 0:640])
                            .then_inc(sem, 16)), inc=16, extra=allw)
        st.emit(blk, sems)
    return nc, ins, outs


def _host_layout(inputs):
    f32 = np.float32
    common = dict(
        w1t=np.ascontiguousarray(np.asarray(inputs["L0_ffn_W1"], f32).T),
        w2t=np.ascontiguousarray(np.asarray(inputs["L0_ffn_W2"], f32).T),
        wxt=np.ascontiguousarray(np.asarray(inputs["L1_attn_W_x"], f32).T.reshape(8, 128, 16, 128).transpose(2, 1, 0, 3).reshape(2048, 1024)),
        wdtt=np.ascontiguousarray(np.asarray(inputs["L1_attn_W_dt"], f32).T),
        wft=np.ascontiguousarray(np.asarray(inputs["L1_ffn_weight"], f32).T),
        kwt=np.ascontiguousarray(np.asarray(inputs["L0_attn_key_w"], f32).T
                                 .reshape(8, 128, -1).transpose(1, 0, 2).reshape(128, -1)),
        vwt=np.ascontiguousarray(np.asarray(inputs["L0_attn_value_w"], f32).T
                                 .reshape(8, 128, -1).transpose(1, 0, 2).reshape(128, -1)),
        owt=np.ascontiguousarray(np.asarray(inputs["L0_attn_output_w"], f32).T),
        v_td=np.asarray(inputs["L0_attn_time_decay"], f32),
        v_tf=np.asarray(inputs["L0_attn_time_first"], f32),
        v_tsg=np.asarray(inputs["L0_attn_time_shift_gates"], f32),
        v_lnw=np.asarray(inputs["L0_attn_ln_w"], f32),
        v_lnb=np.asarray(inputs["L0_attn_ln_b"], f32),
        v_b1=np.asarray(inputs["L0_ffn_b1"], f32),
        v_b2=np.asarray(inputs["L0_ffn_b2"], f32),
        v_bx=np.asarray(inputs["L1_attn_b_x"], f32),
        v_bdt=np.asarray(inputs["L1_attn_b_dt"], f32),
        v_alog=np.asarray(inputs["L1_attn_A_log"], f32),
        vpack=np.concatenate(
            [np.asarray(inputs[k], f32).reshape(-1, 128).T for k in
             ("L0_attn_time_shift_gates", "L0_attn_ln_w", "L0_attn_ln_b",
              "L1_attn_A_log", "L0_ffn_b2", "L0_ffn_b1")] +
            [np.pad(np.asarray(inputs[k], f32).reshape(-1, 64).T, ((0, 64), (0, 0)))
             for k in ("L0_attn_time_decay", "L0_attn_time_first")], axis=1),
        c_ident=np.eye(128, dtype=f32),
        c_ones_bf=np.ones((128, 1), NPBF),
        c_ones_f=np.ones((128, 1), f32),
        c_onesbc=np.ones((1, 128), f32),
    )
    oneh = np.zeros((16, S), f32)
    for i in range(16):
        oneh[i, i * 128:(i + 1) * 128] = 1.0
    common["c_oneh"] = oneh
    x = np.asarray(inputs["L0_attn_x"], f32)
    in_maps = []
    for c in range(B):
        m = dict(common)
        m["xT"] = np.ascontiguousarray(x[c].T)
        in_maps.append(m)
    return in_maps


def kernel(**inputs):
    in_maps = _host_layout(inputs)
    nc, ins, outs = build_core()
    res = run_bass_kernel_spmd(nc, in_maps, core_ids=list(range(B)))
    out = np.stack([np.asarray(r["outT"]).T for r in res.results], axis=0)
    return np.ascontiguousarray(out, dtype=np.float32)


# version 25
# speedup vs baseline: 1.0133x; 1.0101x over previous
import numpy as np
import ml_dtypes
import concourse.bass as bass
import concourse.mybir as mybir
from concourse.bass_utils import run_bass_kernel_spmd

F32 = mybir.dt.float32
BF16 = mybir.dt.bfloat16
Alu = mybir.AluOpType
Act = mybir.ActivationFunctionType
AxX = mybir.AxisListType.X
NPBF = ml_dtypes.bfloat16

FMAX = 3.4028235e38
M23 = 1.5 * 2.0 ** 23
M52 = 1.5 * 2.0 ** 52
P64 = 2.0 ** 64

B, S, D, N, FF = 8, 2048, 1024, 64, 4096
DC = D // 128
G = 4
GW = S // G


class Sched:
    STREAMS = ("pe", "dve", "act", "pool",
               "din0", "din1", "din2", "din3", "din4", "din5", "din6", "din7",
               "dout0", "dout1")

    def __init__(self):
        self.cnt = {s: 0 for s in self.STREAMS}
        self.ops = []
        self.access = {}

    def _deps(self, stream, rds, wrs):
        need = {}
        for (arena, lo, hi, is_w) in [(a, l, h, False) for (a, l, h) in rds] + \
                                     [(a, l, h, True) for (a, l, h) in wrs]:
            lst = self.access.get(arena)
            if not lst:
                continue
            for (l2, h2, s2, c2, w2) in lst:
                if l2 < hi and lo < h2 and (w2 or is_w) and not (s2 == stream == "pe"):
                    if need.get(s2, 0) < c2:
                        need[s2] = c2
        return need

    def marker(self, name):
        self.ops.append(("MARK", name, None))

    def truncate(self, name):
        idx = None
        for i, t in enumerate(self.ops):
            if t[0] == "MARK" and t[1] == name:
                idx = i
                break
        assert idx is not None, name
        self.ops = self.ops[:idx]
        self.cnt = {s: 0 for s in self.STREAMS}
        for rec in self.ops:
            if rec[0] != "MARK":
                self.cnt[rec[0]] += rec[3]
        self.access = {}

    def op(self, stream, fn, reads=(), writes=(), inc=1, extra=(), kind="dma"):
        need = self._deps(stream, reads, writes)
        for s2, c2 in extra:
            if need.get(s2, 0) < c2:
                need[s2] = c2
        c_after = self.cnt[stream] + inc
        self.cnt[stream] = c_after
        self.ops.append((stream, fn, need, inc, kind))
        for (a, lo, hi) in reads:
            self.access.setdefault(a, []).append((lo, hi, stream, c_after, False))
        for (a, lo, hi) in writes:
            lst = self.access.setdefault(a, [])
            lst[:] = [e for e in lst if not (lo <= e[0] and e[1] <= hi)]
            lst.append((lo, hi, stream, c_after, True))

    def emit(self, blk, sems):
        watermark = {}

        waited = set()
        for rec in self.ops:
            if rec[0] == "MARK":
                continue
            for dep, c in rec[2].items():
                waited.add((dep, c))

        def section(eng_obj, streams, tag):
            run = {s: 0 for s in self.STREAMS}
            for rec in self.ops:
                if rec[0] == "MARK":
                    continue
                (stream, fn, need, inc, kind) = rec
                if stream in self.STREAMS:
                    run[stream] += inc
                if stream not in streams:
                    continue
                for dep, c in sorted(need.items()):
                    key = (tag, dep)
                    if watermark.get(key, 0) < c:
                        watermark[key] = c
                        eng_obj.wait_ge(sems[dep], c)
                if kind == "dma":
                    fn(eng_obj, sems[stream])
                else:
                    instr = fn(eng_obj)
                    if kind == "compute_recip" or (
                            (stream, run[stream]) in waited and stream in ("dve", "act")):
                        eng_obj.drain().then_inc(sems[stream], inc)
                    else:
                        instr.then_inc(sems[stream], inc)
            if "dout0" in streams:
                fincnt = {s: 0 for s in self.cnt}
                for rec in self.ops:
                    if rec[0] != "MARK":
                        fincnt[rec[0]] += rec[3]
                for s_ in ("dout0", "dout1", "din0", "din1", "din2", "din3",
                           "din4", "din5", "din6", "din7"):
                    eng_obj.wait_ge(sems[s_], fincnt[s_])

        @blk.sync
        def _(e):
            section(e, ("din0", "din1", "din2", "din3", "din4", "din5", "din6", "din7"),
                    "sync")

        @blk.vector
        def _(e):
            section(e, ("dve",), "dve")

        @blk.scalar
        def _(e):
            section(e, ("act",), "act")

        @blk.gpsimd
        def _(e):
            section(e, ("pool", "dout0", "dout1"), "gp")

        @blk.tensor
        def _(e):
            section(e, ("pe",), "pe")


class Reg:

    def __init__(self, pool_t, pool_name, lo, n, p0=0, p1=128):
        self.t, self.name, self.lo, self.n, self.p0, self.p1 = pool_t, pool_name, lo, n, p0, p1

    def ap(self, c0=0, c1=None, p0=None, p1=None):
        c1 = self.n if c1 is None else c1
        p0 = self.p0 if p0 is None else p0
        p1 = self.p1 if p1 is None else p1
        return self.t[p0:p1, self.lo + c0:self.lo + c1]

    def key(self, c0=0, c1=None):
        c1 = self.n if c1 is None else c1
        return (self.name, self.lo + c0, self.lo + c1)

    def sub(self, off, n):
        return Reg(self.t, self.name, self.lo + off, n, self.p0, self.p1)


def build_core(debug=(), stop=None):
    nc = bass.Bass("TRN2", target_bir_lowering=False)
    ins, outs = {}, {}

    def I(name, shape, dt=F32):
        ins[name] = nc.dram_tensor(name, shape, dt, kind="ExternalInput").ap()
        return ins[name]

    def O(name, shape, dt=F32):
        outs[name] = nc.dram_tensor(name, shape, dt, kind="ExternalOutput").ap()
        return outs[name]

    xT = I("xT", [D, S])
    vpk = I("vpack", [128, 74])
    w1t = I("w1t", [D, FF])
    w2t = I("w2t", [FF, D])
    wxt = I("wxt", [2 * D, D])
    wdtt = I("wdtt", [D, D])
    wft = I("wft", [D, D])
    kwt = I("kwt", [128, DC * N])
    vwt = I("vwt", [128, DC * N])
    owt = I("owt", [N, D])
    v_td = I("v_td", [N])
    v_tf = I("v_tf", [N])
    v_tsg = I("v_tsg", [D])
    v_lnw = I("v_lnw", [D])
    v_lnb = I("v_lnb", [D])
    v_b1 = I("v_b1", [FF])
    v_b2 = I("v_b2", [D])
    v_bx = I("v_bx", [2 * D])
    v_bdt = I("v_bdt", [D])
    v_alog = I("v_alog", [D])
    c_ident = I("c_ident", [128, 128])
    c_oneh = I("c_oneh", [16, S])
    c_ones_bf = I("c_ones_bf", [128, 1], BF16)
    c_ones_f = I("c_ones_f", [128, 1])
    c_onesbc = I("c_onesbc", [1, 128])
    x1sp = nc.dram_tensor("x1spill", [D, S], F32, kind="Internal").ap()

    outT = O("outT", [D, S])
    for dbgspec in debug:
        dnm, dsh = dbgspec[0], dbgspec[1]
        ddt = dbgspec[2] if len(dbgspec) > 2 else F32
        O("dbg_" + dnm, dsh, ddt)

    st = Sched()

    import contextlib
    with contextlib.ExitStack() as EX:
        E = EX.enter_context
        PF_COLS = 21296
        PB_COLS = 63400
        pf_t = E(nc.sbuf_tensor("poolf", [128, PF_COLS], F32))
        pb_t = E(nc.sbuf_tensor("poolb", [128, PB_COLS], BF16))
        psA_t = E(nc.psum_tensor("psA", [128, 2048], F32))
        psB_t = E(nc.psum_tensor("psB", [128, 1024], F32))
        psC_t = E(nc.psum_tensor("psC", [128, 512], F32))
        psT_t = E(nc.psum_tensor("psT", [128, 512], F32))
        sems = {s: E(nc.semaphore("sem_" + s)) for s in Sched.STREAMS}
        blk = E(nc.Block())

        def kA(c0, c1):
            return ("psA", c0, c1)

        def kB(c0, c1):
            return ("psB", c0, c1)

        kC = ("psC", 0, 512)
        kT = ("psT", 0, 512)

        def F_(lo, n, p0=0, p1=128):
            assert lo + n <= PF_COLS
            return Reg(pf_t, "pf", lo, n, p0, p1)

        def B_(lo, n, p0=0, p1=128):
            assert lo + n <= PB_COLS
            return Reg(pb_t, "pb", lo, n, p0, p1)

        xn = B_(0, S * DC)
        ro = B_(16384, S * DC)
        warena = B_(32768, 16384)
        PBH = 49152
        zb = B_(PBH, GW * DC)
        tb2b = B_(PBH + 4096, 2 * GW)
        hq = B_(PBH, GW * 8)
        dump5 = B_(PBH, 2048)
        wxs = B_(PBH, 2048)
        sprow = B_(PBH + 2048, S)
        bxrow = B_(PBH + 4096, 2 * D)
        bdrow = B_(PBH + 6144, D)
        PBS = 56320
        states_bf = B_(PBS, S, p1=64)
        kp_bf = B_(PBS + 2048, N * DC)
        vpw_bf = B_(PBS + 2560, N * DC)
        owt_bf = B_(PBS + 3072, D, p1=64)
        onesb_bf = B_(PBS + 4096, 8)
        sqbf = B_(PBS + 4104, 2048)
        dump1 = ro.sub(0, 2048)
        assert PBS + 5128 <= PB_COLS

        def slot(c, g):
            return (g * DC + c) * GW

        def ro_r(c, g):
            return ro.sub(slot(c, g), GW)

        idreg = F_(0, 128)
        onesbcreg = F_(128, 128)
        VEC = 256
        vp_ = [VEC]

        def vecreg(k=1):
            o = vp_[0]
            vp_[0] += k
            assert vp_[0] <= VEC + 384
            return F_(o, k)

        flag = vecreg(); invflag = vecreg()
        gsig = vecreg(DC); gsig1m = vecreg(DC)
        lnw_v = vecreg(DC); lnb_v = vecreg(DC)
        aneg = vecreg(DC)
        b1_v = vecreg(32)
        b2_v = vecreg(DC)
        tdv = vecreg(); tfv = vecreg()
        etf = vecreg(); negetf = vecreg(); ekb = vecreg()
        wv = vecreg()
        bkv = vecreg(); bvv = vecreg()
        tsg_r = vecreg(DC); alog_r = vecreg(DC)
        ones_f = vecreg()
        scl = {}
        for nm in ("w1", "w2", "wx", "wdt", "wf"):
            scl[nm] = dict(sum=vecreg(), r=vecreg(), negr=vecreg(), scale=vecreg())
        tmpv = vecreg(8)
        c_eps = vecreg(); c_mhalf = vecreg(); c_mone = vecreg()
        s0v = vecreg(); sIv = vecreg()
        pw2 = vecreg(); pw4 = vecreg(); pw8 = vecreg(); pwt = vecreg()
        S_v = vecreg(); F_v = vecreg(); T_v = vecreg(); nT_v = vecreg()
        U_v = vecreg(); nU_v = vecreg(); G_v = vecreg()
        Mb_v = vecreg(); nMb_v = vecreg(); nMb1_v = vecreg()

        PH = 1024
        wstage = F_(PH, 2048)
        srA = F_(PH + 2048, S)
        srB = F_(PH + 4096, S)
        srC = F_(PH + 6144, S)
        free8192 = F_(PH + 8192, 1024)
        tbuf = F_(PH + 9216, S)
        tbuf2 = F_(PH + 11264, 512)
        kvf = F_(PH + 11776, S, p1=64)
        wdec = F_(PH + 13824, S, p1=64)
        stf = F_(PH + 15872, S, p1=64)
        ekbuf = F_(PH + 17920, GW, p1=64)
        fold = F_(PH + 11776, S)
        xf1 = F_(PH + 13824, S)
        amaxt = F_(PH + 15872, 16)
        sptk = F_(PH + 15888, 16)
        rhot = F_(PH + 15904, 16)
        onehreg = F_(PH + 15920, S, p1=16)
        v16sp = F_(PH + 17968, 128, p1=16)
        v16rh = F_(PH + 18096, 128, p1=16)
        spbc = F_(PH + 11776, S)
        wsxr = F_(PH + 18224, 2048)
        wsdr = F_(PH + 15920, 2048)
        assert PH + 18224 + 2048 <= PF_COLS
        Bs = [F_(PH + 2048 + i * S, S) for i in range(5)]
        zerot = F_(PH + 12288, S)
        tern7a = F_(PH + 14336, 512)
        tern7b = F_(PH + 14848, 512)

        din_rot = [0]
        dout_rot = [0]

        def dma_in(dst_ap, src_ap, wkey, rkeys=(), slow=False):
            s_ = "din%d" % (din_rot[0] % 4)
            din_rot[0] += 1
            prev = st.cnt[s_]
            def fn(e, sem, d=dst_ap, s=src_ap, sl=slow):
                e.dma_start(out=d, in_=s, allow_slow_non_contiguous=sl).then_inc(sem, 16)
            st.op(s_, fn, reads=rkeys, writes=[wkey], inc=16, extra=[(s_, prev)])

        def dma_io(dst_ap, src_ap, rkeys, wkeys):
            s_ = "dout%d" % (dout_rot[0] % 2)
            dout_rot[0] += 1
            prev = st.cnt[s_]
            def fn(e, sem, d=dst_ap, s=src_ap):
                e.dma_start(out=d, in_=s).then_inc(sem, 16)
            st.op(s_, fn, reads=rkeys, writes=wkeys, inc=16, extra=[(s_, prev)])

        def dma_out(dst_ap, src_ap, rkey, hbmkey=None):
            dma_io(dst_ap, src_ap, [rkey], [hbmkey] if hbmkey else [])

        def dma_rd(dst_ap, src_ap, wkey, hbmkey=None):
            dma_io(dst_ap, src_ap, [hbmkey] if hbmkey else [], [wkey])

        def vop(fn, reads=(), writes=()):
            st.op("dve", fn, reads, writes, kind="compute")

        def aop(fn, reads=(), writes=()):
            st.op("act", fn, reads, writes, kind="compute")

        def pop_(fn, reads=(), writes=()):
            def f2(e, sem, fn=fn):
                fn(e).then_inc(sem, 1)
            st.op("pe", f2, reads, writes, kind="dma")

        def top_(fn, reads=(), writes=()):
            def f2(e, sem, fn=fn):
                fn(e)
                e.drain().then_inc(sem, 1)
            st.op("pe", f2, reads, writes, kind="dma")

        def rop(out_ap, in_ap, reads, writes):
            def fn(e, o=out_ap, i=in_ap):
                e.drain()
                r = e.reciprocal(o, i)
                return r
            st.op("dve", fn, reads, writes, kind="compute_recip")

        def ppv(reg, c=0, p0=None, p1=None):
            p0 = reg.p0 if p0 is None else p0
            p1 = reg.p1 if p1 is None else p1
            return reg.t[p0:p1, reg.lo + c:reg.lo + c + 1]

        def r0(reg, a=0, b=None):
            b = reg.n if b is None else b
            return reg.t[0:1, reg.lo + a:reg.lo + b]

        dma_in(idreg.ap(), c_ident, idreg.key())
        dma_in(r0(onesbcreg), c_onesbc, onesbcreg.key())
        dma_in(onesb_bf.ap(0, 1), c_ones_bf, onesb_bf.key())
        dma_in(ones_f.ap(), c_ones_f, ones_f.key())

        def load_pp(reg, hbm, p=128):
            dma_in(reg.ap(p0=0, p1=p), hbm.rearrange("(c p) -> p c", p=p), reg.key(), slow=True)

        dma_in(tbuf.ap(0, 74), vpk, tbuf.key(0, 74))
        vp_off = 0
        for reg, w in ((tsg_r, DC), (lnw_v, DC), (lnb_v, DC), (alog_r, DC),
                       (b2_v, DC), (b1_v, 32), (tdv, 1), (tfv, 1)):
            vop(lambda e, reg=reg, o=vp_off, w=w: e.tensor_copy(reg.ap(), tbuf.ap(o, o + w)),
                [tbuf.key(vp_off, vp_off + w)], [reg.key()])
            vp_off += w

        vop(lambda e: e.memset(c_eps.ap(), 1e-5), writes=[c_eps.key()])
        vop(lambda e: e.memset(c_mhalf.ap(), -0.5), writes=[c_mhalf.key()])
        vop(lambda e: e.memset(c_mone.ap(), -1.0), writes=[c_mone.key()])
        vop(lambda e: e.memset(flag.ap(), 2.0 ** -40), writes=[flag.key()])
        vop(lambda e: e.tensor_scalar(flag.ap(), flag.ap(), 1.0, -1.0, Alu.add, Alu.add),
            [flag.key()], [flag.key()])
        vop(lambda e: e.tensor_scalar(flag.ap(), flag.ap(), 2.0 ** 40, None, Alu.mult),
            [flag.key()], [flag.key()])
        vop(lambda e: e.tensor_scalar(invflag.ap(), flag.ap(), 1.0, -1.0, Alu.subtract, Alu.mult),
            [flag.key()], [invflag.key()])
        vop(lambda e: e.tensor_scalar(s0v.ap(), flag.ap(), 1.8446744073709552e19, 1.0,
                                      Alu.mult, Alu.add), [flag.key()], [s0v.key()])
        vop(lambda e: e.tensor_mul(pw2.ap(), s0v.ap(), s0v.ap()), [s0v.key()], [pw2.key()])
        vop(lambda e: e.tensor_mul(pw4.ap(), pw2.ap(), pw2.ap()), [pw2.key()], [pw4.key()])
        vop(lambda e: e.tensor_mul(pw8.ap(), pw4.ap(), pw4.ap()), [pw4.key()], [pw8.key()])
        vop(lambda e: e.tensor_scalar(S_v.ap(), pw8.ap(), 2.0 ** 64, None, Alu.mult),
            [pw8.key()], [S_v.key()])
        vop(lambda e: e.tensor_scalar(G_v.ap(), pw8.ap(), 2.0 ** 52, None, Alu.mult),
            [pw8.key()], [G_v.key()])
        vop(lambda e: e.tensor_mul(pwt.ap(), pw4.ap(), pw2.ap()), [pw4.key(), pw2.key()], [pwt.key()])
        vop(lambda e: e.tensor_scalar(T_v.ap(), pwt.ap(), float((2 - 2.0 ** -23) * 2.0 ** 63),
                                      None, Alu.mult), [pwt.key()], [T_v.key()])
        vop(lambda e: e.tensor_scalar(nT_v.ap(), T_v.ap(), -1.0, None, Alu.mult),
            [T_v.key()], [nT_v.key()])
        vop(lambda e: e.tensor_mul(pwt.ap(), pw8.ap(), pwt.ap()), [pw8.key(), pwt.key()], [pwt.key()])
        vop(lambda e: e.tensor_scalar(U_v.ap(), pwt.ap(), float((2 - 2.0 ** -23) * 2.0 ** 115),
                                      None, Alu.mult), [pwt.key()], [U_v.key()])
        vop(lambda e: e.tensor_scalar(nU_v.ap(), U_v.ap(), -1.0, None, Alu.mult),
            [U_v.key()], [nU_v.key()])
        rop(sIv.ap(), s0v.ap(), [s0v.key()], [sIv.key()])
        vop(lambda e: e.tensor_mul(pwt.ap(), sIv.ap(), sIv.ap()), [sIv.key()], [pwt.key()])
        vop(lambda e: e.tensor_mul(pw2.ap(), pwt.ap(), pwt.ap()), [pwt.key()], [pw2.key()])
        vop(lambda e: e.tensor_mul(pw4.ap(), pw2.ap(), pw2.ap()), [pw2.key()], [pw4.key()])
        vop(lambda e: e.tensor_mul(pw4.ap(), pw4.ap(), pw2.ap()), [pw2.key(), pw4.key()], [pw4.key()])
        vop(lambda e: e.tensor_mul(pw4.ap(), pw4.ap(), sIv.ap()), [sIv.key(), pw4.key()], [pw4.key()])
        vop(lambda e: e.tensor_scalar(F_v.ap(), pw4.ap(), 2.0 ** -125, None, Alu.mult),
            [pw4.key()], [F_v.key()])
        vop(lambda e: e.tensor_scalar(Mb_v.ap(), flag.ap(), float(np.float32(1.5 * 2.0 ** 52 - M23)),
                                      M23, Alu.mult, Alu.add), [flag.key()], [Mb_v.key()])
        vop(lambda e: e.tensor_scalar(nMb_v.ap(), Mb_v.ap(), -1.0, None, Alu.mult),
            [Mb_v.key()], [nMb_v.key()])
        vop(lambda e: e.tensor_scalar(nMb1_v.ap(), Mb_v.ap(), -1.0, 1.0, Alu.mult, Alu.add),
            [Mb_v.key()], [nMb1_v.key()])
        aop(lambda e: e.activation(gsig.ap(), tsg_r.ap(), Act.Sigmoid), [tsg_r.key()], [gsig.key()])
        vop(lambda e: e.tensor_scalar(gsig1m.ap(), gsig.ap(), 1.0, -1.0, Alu.subtract, Alu.mult),
            [gsig.key()], [gsig1m.key()])
        aop(lambda e: e.activation(tmpv.ap(0, DC), alog_r.ap(), Act.Abs), [alog_r.key()], [tmpv.key(0, DC)])
        aop(lambda e: e.activation(tmpv.ap(0, DC), tmpv.ap(0, DC), Act.Exp, scale=-1.0),
            [tmpv.key(0, DC)], [tmpv.key(0, DC)])
        aop(lambda e: e.activation(tmpv.ap(0, DC), tmpv.ap(0, DC), Act.Ln, bias=1.0),
            [tmpv.key(0, DC)], [tmpv.key(0, DC)])
        vop(lambda e: e.scalar_tensor_tensor(aneg.ap(), alog_r.ap(), 0.0, tmpv.ap(0, DC), Alu.max, Alu.add),
            [alog_r.key(), tmpv.key(0, DC)], [aneg.key()])
        vop(lambda e: e.tensor_scalar(aneg.ap(), aneg.ap(), -1.0, None, Alu.mult),
            [aneg.key()], [aneg.key()])
        aop(lambda e: e.activation(ppv(wv, p0=0, p1=64), ppv(tdv, p0=0, p1=64), Act.Exp),
            [tdv.key()], [wv.key()])
        aop(lambda e: e.activation(ppv(etf, p0=0, p1=64), ppv(tfv, p0=0, p1=64), Act.Exp),
            [tfv.key()], [etf.key()])
        vop(lambda e: e.tensor_scalar(ppv(negetf, p0=0, p1=64), ppv(etf, p0=0, p1=64), -1.0, None, Alu.mult),
            [etf.key()], [negetf.key()])

        snap_i = [0]

        def snap_vec(tag):
            if not any(d[0] == "snap" for d in debug):
                return
            i = snap_i[0]
            snap_i[0] += 1
            vop(lambda e, i=i: e.tensor_copy(srB.ap(0, 640), pf_t[:, 0:640]),
                [("pf", 0, 640)], [srB.key(0, 640)])
            dma_out(outs["dbg_snap"][:, i * 640:(i + 1) * 640], srB.ap(0, 640), srB.key(0, 640))

        snap_vec("after_aneg")
        kvstg = F_(PH + 2048, 2 * DC * N)
        dma_in(kvstg.ap(0, DC * N), kwt, kvstg.key(0, DC * N))
        dma_in(kvstg.ap(DC * N, 2 * DC * N), vwt, kvstg.key(DC * N, 2 * DC * N))
        for c in range(DC):
            ck = kvstg.key(c * N, (c + 1) * N)
            vop(lambda e, c=c: e.tensor_scalar(kp_bf.ap(c * N, (c + 1) * N),
                                               kvstg.ap(c * N, (c + 1) * N),
                                               ppv(lnw_v, c), None, Alu.mult),
                [ck, lnw_v.key()], [kp_bf.key(c * N, (c + 1) * N)])
            pop_(lambda e, c=c: e.matmul(psC_t[0:64, 0:1], kvstg.ap(c * N, (c + 1) * N),
                                         ppv(lnb_v, c), start=(c == 0), stop=(c == DC - 1)),
                 [ck, lnb_v.key()], [kC])
        vop(lambda e: e.tensor_copy(ppv(bkv, p0=0, p1=64), psC_t[0:64, 0:1]), [kC], [bkv.key()])
        for c in range(DC):
            ck = kvstg.key(DC * N + c * N, DC * N + (c + 1) * N)
            vop(lambda e, c=c: e.tensor_scalar(vpw_bf.ap(c * N, (c + 1) * N),
                                               kvstg.ap(DC * N + c * N, DC * N + (c + 1) * N),
                                               ppv(lnw_v, c), None, Alu.mult),
                [ck, lnw_v.key()], [vpw_bf.key(c * N, (c + 1) * N)])
            pop_(lambda e, c=c: e.matmul(psT_t[0:64, 0:1],
                                         kvstg.ap(DC * N + c * N, DC * N + (c + 1) * N),
                                         ppv(lnb_v, c), start=(c == 0), stop=(c == DC - 1)),
                 [ck, lnb_v.key()], [kT])
        vop(lambda e: e.tensor_copy(ppv(bvv, p0=0, p1=64), psT_t[0:64, 0:1]), [kT], [bvv.key()])
        vop(lambda e: e.tensor_mul(ppv(ekb, p0=0, p1=64), ppv(negetf, p0=0, p1=64),
                                   ppv(bkv, p0=0, p1=64)), [negetf.key(), bkv.key()], [ekb.key()])
        owtf = F_(PH + 2048 + N, D)
        dma_in(owtf.ap(p1=64), owt, owtf.key())
        vop(lambda e: e.tensor_copy(owt_bf.ap(), owtf.ap(p1=64)), [owtf.key()], [owt_bf.key()])

        absstg = Reg(pf_t, "pf", PH + 13824, 2048)

        def stream_abs_gen(hbm, rows, cols, sumreg, piece, dumpreg, first, stgreg=None):
            stgreg = absstg if stgreg is None else stgreg
            nchunk, npiece = rows // 128, cols // piece
            idx = 0
            for c in range(nchunk):
                for p in range(npiece):
                    stg = stgreg.sub((idx % 2) * piece, piece)
                    dma_in(stg.ap(), hbm[c * 128:(c + 1) * 128, p * piece:(p + 1) * piece], stg.key())
                    aop(lambda e, stg=stg: e.activation(dumpreg.ap(0, stg.n), stg.ap(), Act.Abs,
                                                        accum_out=tmpv.ap(6, 7)),
                        [stg.key()], [dumpreg.key(0, stg.n), tmpv.key(6, 7)])
                    if idx == 0 and first:
                        vop(lambda e: e.tensor_copy(sumreg.ap(), tmpv.ap(6, 7)),
                            [tmpv.key(6, 7)], [sumreg.key()])
                    else:
                        vop(lambda e: e.tensor_add(sumreg.ap(), sumreg.ap(), tmpv.ap(6, 7)),
                            [tmpv.key(6, 7), sumreg.key()], [sumreg.key()])
                    idx += 1
                    yield

        def stream_abs(hbm, rows, cols, sumreg, piece, dumpreg):
            for _ in stream_abs_gen(hbm, rows, cols, sumreg, piece, dumpreg, True):
                pass

        pump_gen = [None]

        def pump(n=1):
            g_ = pump_gen[0]
            if g_ is None:
                return
            for _ in range(n):
                if next(g_, "END") == "END":
                    pump_gen[0] = None
                    return

        def finish_scale(nm, numel, eps, mode):
            d = scl[nm]
            pop_(lambda e: e.matmul(psC_t[0:1, 0:1], d["sum"].ap(), ones_f.ap(),
                                    start=True, stop=True), [d["sum"].key(), ones_f.key()], [kC])
            vop(lambda e: e.tensor_copy(tmpv.ap(7, 8, 0, 1), psC_t[0:1, 0:1]), [kC], [tmpv.key(7, 8)])
            pop_(lambda e: e.matmul(psC_t[:, 0:1], r0(onesbcreg), tmpv.ap(7, 8, 0, 1),
                                    start=True, stop=True), [tmpv.key(7, 8), onesbcreg.key()], [kC])
            op1 = Alu.max if mode == "max" else Alu.add
            vop(lambda e: e.tensor_scalar(d["scale"].ap(), psC_t[:, 0:1], 1.0 / numel, eps,
                                          Alu.mult, op1), [kC], [d["scale"].key()])
            rop(d["r"].ap(), d["scale"].ap(), [d["scale"].key()], [d["r"].key()])
            vop(lambda e: e.tensor_scalar(d["negr"].ap(), d["r"].ap(), -1.0, None, Alu.mult),
                [d["r"].key()], [d["negr"].key()])

        def ternarize(dst_reg, dst_off, stg, off, ncols, rv, negrv, sa, sb):
            wk = stg.key(off, off + ncols)
            dk = dst_reg.key(dst_off, dst_off + ncols)
            aop(lambda e: e.activation(sa.ap(0, ncols), stg.ap(off, off + ncols), Act.Relu,
                                       scale=rv.ap(), bias=Mb_v.ap()),
                [wk, rv.key(), Mb_v.key()], [sa.key(0, ncols)])
            aop(lambda e: e.activation(sa.ap(0, ncols), sa.ap(0, ncols), Act.Relu,
                                       bias=nMb1_v.ap()),
                [sa.key(0, ncols), nMb1_v.key()], [sa.key(0, ncols)])
            vop(lambda e: e.tensor_scalar(dst_reg.ap(dst_off, dst_off + ncols),
                                          sa.ap(0, ncols), 2.0, -1.0, Alu.min, Alu.add),
                [sa.key(0, ncols)], [dk])

        snap_vec("after_p0")
        import itertools
        pump_gen[0] = itertools.chain(
            stream_abs_gen(w1t, D, FF, scl["w1"]["sum"], 512, dump1, True, stgreg=free8192),
            stream_abs_gen(w2t, FF, D, scl["w2"]["sum"], 512, dump1, True, stgreg=free8192))
        st.marker("p1")

        if any(d[0] == "vec0" for d in debug):
            vop(lambda e: e.tensor_copy(srA.ap(0, 640), pf_t[:, 0:640]), [("pf", 0, 640)], [srA.key(0, 640)])
            dma_out(outs["dbg_vec0"], srA.ap(0, 640), srA.key(0, 640))
        def st_sq(par):
            return (sqbf, par * GW) if par < 2 else (hq, (par - 2) * GW)

        def st_cs(par):
            return (sqbf, 1024 + par * GW) if par < 2 else (hq, 2048 + (par - 2) * GW)

        def stats_piece(c, g, par, sap, sk, bfap, bfk, alt=False):
            sqreg, sq0 = st_sq(par)
            sq1 = sq0 + GW
            aop(lambda e, s=sap, a=sq0, b=sq1, r=sqreg: e.activation(r.ap(a, b), s,
                                                                     Act.Square),
                [sk], [sqreg.key(sq0, sq1)])
            if bfap is None:
                csreg, cs0 = st_cs(par)
                cs1 = cs0 + GW
                vop(lambda e, s=sap, a=cs0, b=cs1, r=csreg: e.tensor_copy(r.ap(a, b), s),
                    [sk], [csreg.key(cs0, cs1)])
                bfap, bfk = csreg.ap(cs0, cs1), csreg.key(cs0, cs1)
            pop_(lambda e, c=c, g=g, x=bfap: e.matmul(
                psA_t[0:1, g * 512:(g + 1) * 512], onesb_bf.ap(0, 1), x,
                start=(c == 0), stop=(c == DC - 1)),
                [bfk, onesb_bf.key()], [kA(g * 512, (g + 1) * 512)])
            if alt:
                tgt, tk = psA_t[32:33, g * 512:(g + 1) * 512], kA(g * 512, (g + 1) * 512)
            elif g < 2:
                tgt, tk = psB_t[0:1, g * 512:(g + 1) * 512], kB(g * 512, (g + 1) * 512)
            elif g == 2:
                tgt, tk = psC_t[0:1, :], kC
            else:
                tgt, tk = psT_t[0:1, :], kT
            pop_(lambda e, c=c, t=tgt, a=sq0, b=sq1, r=sqreg: e.matmul(
                t, onesb_bf.ap(0, 1), r.ap(a, b),
                start=(c == 0), stop=(c == DC - 1)),
                [sqreg.key(sq0, sq1), onesb_bf.key()], [tk])

        def stats_final(alt=False):
            vop(lambda e: e.tensor_copy(r0(srA), psA_t[0:1, :]), [kA(0, 2048)], [srA.key()])
            if alt:
                vop(lambda e: e.tensor_copy(r0(srB), psA_t[32:33, :]), [kA(0, 2048)], [srB.key()])
            else:
                vop(lambda e: e.tensor_copy(r0(srB, 0, 1024), psB_t[0:1, :]), [kB(0, 1024)],
                    [srB.key(0, 1024)])
                vop(lambda e: e.tensor_copy(r0(srB, 1024, 1536), psC_t[0:1, :]), [kC],
                    [srB.key(1024, 1536)])
                vop(lambda e: e.tensor_copy(r0(srB, 1536, 2048), psT_t[0:1, :]), [kT],
                    [srB.key(1536, 2048)])
            vop(lambda e: e.tensor_scalar(r0(srA), r0(srA), 1.0 / D, None, Alu.mult),
                [srA.key()], [srA.key()])
            vop(lambda e: e.tensor_scalar(r0(srB), r0(srB), 1.0 / D, None, Alu.mult),
                [srB.key()], [srB.key()])
            vop(lambda e: e.tensor_mul(r0(srC), r0(srA), r0(srA)), [srA.key()], [srC.key()])
            vop(lambda e: e.tensor_sub(r0(srB), r0(srB), r0(srC)), [srB.key(), srC.key()], [srB.key()])
            aop(lambda e: e.activation(r0(srB), r0(srB), Act.Ln, bias=ppv(c_eps, 0, 0, 1)),
                [srB.key(), c_eps.key()], [srB.key()])
            aop(lambda e: e.activation(r0(srB), r0(srB), Act.Exp, scale=ppv(c_mhalf, 0, 0, 1)),
                [srB.key(), c_mhalf.key()], [srB.key()])
            vop(lambda e: e.tensor_mul(r0(srC), r0(srA), r0(srB)), [srA.key(), srB.key()], [srC.key()])
            vop(lambda e: e.tensor_scalar(r0(srC), r0(srC), -1.0, None, Alu.mult),
                [srC.key()], [srC.key()])

        def stats_from(get_src, nslots=2):
            for g in range(G):
                for c in range(DC):
                    pump(1)
                    par = (g * DC + c) % nslots
                    sap, sk, bfap, bfk = get_src(c, g, par)
                    stats_piece(c, g, par, sap, sk, bfap, bfk)
            stats_final()

        def bcast_AB(g):
            pop_(lambda e, g=g: e.matmul(psC_t[:, :], r0(onesbcreg), r0(srB, g * GW, (g + 1) * GW),
                                         start=True, stop=True), [srB.key(), onesbcreg.key()], [kC])
            pop_(lambda e, g=g: e.matmul(psT_t[:, :], r0(onesbcreg), r0(srC, g * GW, (g + 1) * GW),
                                         start=True, stop=True), [srC.key(), onesbcreg.key()], [kT])
            return psC_t[:, :], kC, psT_t[:, :], kT

        def xpiece(c, g, par):
            stg = wstage.sub(par * 512, GW)
            dma_in(stg.ap(), xT[c * 128:(c + 1) * 128, g * GW:(g + 1) * GW], stg.key())
            return stg

        pi = [0]
        _xpl = [(c, g) for g in range(G) for c in range(DC)]
        _xis = [0]

        def x_prefetch(upto):
            while _xis[0] < min(upto, len(_xpl)):
                i = _xis[0]
                c, g = _xpl[i]
                stg = wstage.sub((i % 4) * 512, GW)
                dma_in(stg.ap(), xT[c * 128:(c + 1) * 128, g * GW:(g + 1) * GW], stg.key())
                _xis[0] += 1

        def xnext():
            i = pi[0]
            x_prefetch(i + 4)
            stg = wstage.sub((i % 4) * 512, GW)
            pi[0] += 1
            return stg

        def ln1_src(c, g, par):
            stg = xnext()
            a, b = c * S + g * GW, c * S + (g + 1) * GW
            vop(lambda e, s=stg, a=a, b=b: e.tensor_copy(xn.ap(a, b), s.ap()),
                [stg.key()], [xn.key(a, b)])
            return (xn.ap(a, b), xn.key(a, b), xn.ap(a, b), xn.key(a, b))

        stats_from(ln1_src, nslots=4)
        for g in range(G):
            aap, ak, bap, bk = bcast_AB(g)
            for c in range(DC):
                pump(1)
                dk = xn.key(c * S + g * GW, c * S + (g + 1) * GW)
                tb0 = (c % 4) * 512
                vop(lambda e, c=c, g=g, a=aap, tb0=tb0: e.tensor_mul(
                    tbuf.ap(tb0, tb0 + GW), a, xn.ap(c * S + g * GW, c * S + (g + 1) * GW)),
                    [dk, ak], [tbuf.key(tb0, tb0 + GW)])
                vop(lambda e, c=c, g=g, b=bap, tb0=tb0: e.tensor_add(
                    xn.ap(c * S + g * GW, c * S + (g + 1) * GW), b, tbuf.ap(tb0, tb0 + GW)),
                    [tbuf.key(tb0, tb0 + GW), bk], [dk])

        if any(d[0] == "xn" for d in debug):
            for c in range(DC):
                vop(lambda e, c=c: e.tensor_copy(tbuf.ap(), xn.ap(c * S, (c + 1) * S)),
                    [xn.key(c * S, (c + 1) * S)], [tbuf.key()])
                dma_out(outs["dbg_xn"][c * 128:(c + 1) * 128, :], tbuf.ap(), tbuf.key())

        def mix_group(c, g):
            src_off = (g * GW + 1024) % S
            xck = xn.key(c * S + src_off, c * S + src_off + GW)
            xnk = xn.key(c * S + g * GW, c * S + (g + 1) * GW)
            dap, dk = tbuf.ap(0, GW), tbuf.key(0, GW)
            vop(lambda e, c=c, x=xn.ap(c * S + src_off, c * S + src_off + GW), d=dap:
                e.tensor_scalar(d, x, ppv(gsig, c), None, Alu.mult), [xck, gsig.key()], [dk])
            vop(lambda e, c=c, x=xn.ap(c * S + g * GW, c * S + (g + 1) * GW), d=dap:
                e.scalar_tensor_tensor(d, x, ppv(gsig1m, c), d, Alu.mult, Alu.add),
                [xnk, gsig1m.key(), dk], [dk])
            return dap, dk

        def shift_src(c, g, par):
            dap, dk = mix_group(c, g)
            t0, t1 = par * GW, (par + 1) * GW
            vop(lambda e, s=dap, a=t0, b=t1: e.tensor_copy(tb2b.ap(a, b), s),
                [dk], [tb2b.key(t0, t1)])
            return (dap, dk, tb2b.ap(t0, t1), tb2b.key(t0, t1))

        stats_from(shift_src)
        vop(lambda e: e.tensor_copy(wdec.ap(), ppv(wv, p0=0, p1=64).broadcast_to([64, S])),
            [wv.key()], [wdec.key()])
        for g in range(G):
            aap, ak, bap, bk = bcast_AB(g)
            for c in range(DC):
                pump(1)
                dap, dk = mix_group(c, g)
                vop(lambda e, d=dap, a=aap: e.tensor_mul(tbuf2.ap(0, GW), a, d),
                    [dk, ak], [tbuf2.key()])
                zk = zb.key(c * GW, (c + 1) * GW)
                vop(lambda e, c=c, b=bap: e.tensor_add(zb.ap(c * GW, (c + 1) * GW), b, tbuf2.ap(0, GW)),
                    [tbuf2.key(), bk], [zk])
            for c in range(DC):
                pop_(lambda e, c=c: e.matmul(psC_t[0:64, :], kp_bf.ap(c * N, (c + 1) * N),
                                             zb.ap(c * GW, (c + 1) * GW),
                                             start=(c == 0), stop=(c == DC - 1)),
                     [zb.key(c * GW, (c + 1) * GW), kp_bf.key(c * N, (c + 1) * N)], [kC])
            for c in range(DC):
                pop_(lambda e, c=c: e.matmul(psT_t[0:64, :], vpw_bf.ap(c * N, (c + 1) * N),
                                             zb.ap(c * GW, (c + 1) * GW),
                                             start=(c == 0), stop=(c == DC - 1)),
                     [zb.key(c * GW, (c + 1) * GW), vpw_bf.key(c * N, (c + 1) * N)], [kT])
            aop(lambda e: e.activation(ekbuf.ap(), psC_t[0:64, :], Act.Exp,
                                       bias=ppv(ekb, p0=0, p1=64), scale=ppv(negetf, p0=0, p1=64)),
                [kC, ekb.key(), negetf.key()], [ekbuf.key()])
            vop(lambda e, g=g: e.scalar_tensor_tensor(
                kvf.ap(g * GW, (g + 1) * GW), psT_t[0:64, :], ppv(bvv, p0=0, p1=64),
                ekbuf.ap(), Alu.add, Alu.mult),
                [kT, bvv.key(), ekbuf.key()], [kvf.key(g * GW, (g + 1) * GW)])
        vop(lambda e: e.tensor_tensor_scan(stf.ap(), wdec.ap(), kvf.ap(), 0.0, Alu.mult, Alu.add),
            [wdec.key(), kvf.key()], [stf.key()])
        vop(lambda e: e.tensor_copy(states_bf.ap(), stf.ap()), [stf.key()], [states_bf.key()])
        if any(d[0] == "states" for d in debug):
            dma_out(outs["dbg_states"], stf.ap(), stf.key())

        pump(200)
        assert pump_gen[0] is None, "W1/W2 abs pump must be drained here (frees free8192)" 
        finish_scale("w1", D * FF, 1e-8, "max")
        finish_scale("w2", D * FF, 1e-8, "max")
        pump_gen[0] = itertools.chain(
            stream_abs_gen(wxt, 2 * D, D, scl["wx"]["sum"], 1024, sqbf, True),
            stream_abs_gen(wdtt, D, D, scl["wdt"]["sum"], 1024, sqbf, True),
            stream_abs_gen(wft, D, D, scl["wf"]["sum"], 1024, sqbf, True))
        QE = FF // 8

        def tern_thunks(e8):
            buf = e8 % 2
            w1q8 = warena.sub(buf * 8192, 4096)
            w2q8 = warena.sub(buf * 8192 + 4096, 4096)
            th = []
            for c in range(DC):
                def t1(c=c, buf=buf, w1q8=w1q8, e8=e8):
                    stg = wstage.sub((c % 2) * 1024, 512)
                    dma_in(stg.ap(0, 512), w1t[c * 128:(c + 1) * 128, e8 * QE:(e8 + 1) * QE],
                           stg.key(0, 512))
                    ternarize(w1q8, c * 512, stg, 0, 512, scl["w1"]["r"], scl["w1"]["negr"],
                              free8192.sub((c % 2) * 512, 512), tbuf2)
                th.append(t1)
            for j4 in range(4):
                def t2(j4=j4, buf=buf, w2q8=w2q8, e8=e8):
                    fc = e8 * 4 + j4
                    stg = wstage.sub((j4 % 2) * 1024, 1024)
                    dma_in(stg.ap(), w2t[fc * 128:(fc + 1) * 128, :], stg.key())
                    ternarize(w2q8, j4 * 1024, stg, 0, 1024, scl["w2"]["r"], scl["w2"]["negr"],
                              tbuf.sub((j4 % 2) * 1024, 1024), tbuf2)
                th.append(t2)
            return th

        for th in tern_thunks(0):
            th()
        for e8 in range(8):
            buf = e8 % 2
            w1q8 = warena.sub(buf * 8192, 4096)
            w2q8 = warena.sub(buf * 8192 + 4096, 4096)
            h8 = hq.sub(buf * 2048, 2048)
            nxt = tern_thunks(e8 + 1) if e8 + 1 < 8 else []
            for g in range(G):
                for th in nxt[g * 3:(g + 1) * 3]:
                    th()
                pump(1)
                for j in range(4):
                    for c in range(DC):
                        pop_(lambda e, j=j, c=c, g=g, w1q8=w1q8: e.matmul(
                            psA_t[:, j * 512:(j + 1) * 512],
                            w1q8.ap(c * 512 + j * 128, c * 512 + (j + 1) * 128),
                            xn.ap(c * S + g * GW, c * S + (g + 1) * GW),
                            start=(c == 0), stop=(c == DC - 1)),
                            [w1q8.key(c * 512 + j * 128, c * 512 + (j + 1) * 128),
                             xn.key(c * S + g * GW, c * S + (g + 1) * GW)],
                            [kA(j * 512, (j + 1) * 512)])
                for j in range(4):
                    aop(lambda e, j=j, e8=e8, h8=h8: e.activation(
                        h8.ap(j * GW, (j + 1) * GW), psA_t[:, j * 512:(j + 1) * 512],
                        Act.Gelu, bias=ppv(b1_v, e8 * 4 + j)),
                        [kA(j * 512, (j + 1) * 512), b1_v.key()],
                        [h8.key(j * GW, (j + 1) * GW)])
                for c in range(DC):
                    pbk = (c % 2) * 512
                    for j in range(4):
                        pop_(lambda e, c=c, j=j, pbk=pbk, w2q8=w2q8, h8=h8: e.matmul(
                            psB_t[:, pbk:pbk + 512],
                            w2q8.ap(j * 1024 + c * 128, j * 1024 + (c + 1) * 128),
                            h8.ap(j * GW, (j + 1) * GW),
                            start=(j == 0), stop=(j == 3)),
                            [w2q8.key(j * 1024 + c * 128, j * 1024 + (c + 1) * 128),
                             h8.key(j * GW, (j + 1) * GW)], [kB(pbk, pbk + 512)])
                    oa = ro_r(c, g)
                    if e8 == 0:
                        vop(lambda e, c=c, pbk=pbk, oa=oa: e.tensor_scalar(
                            oa.ap(), psB_t[:, pbk:pbk + 512], ppv(b2_v, c), None, Alu.add),
                            [kB(pbk, pbk + 512), b2_v.key()], [oa.key()])
                    else:
                        vop(lambda e, pbk=pbk, oa=oa: e.tensor_add(
                            oa.ap(), psB_t[:, pbk:pbk + 512], oa.ap()),
                            [kB(pbk, pbk + 512), oa.key()], [oa.key()])
        pump(200)
        if any(d[0] == "ffn" for d in debug):
            for g in range(G):
                for c in range(DC):
                    oa = ro_r(c, g)
                    dma_out(outs["dbg_ffn"][c * 128:(c + 1) * 128, g * GW:(g + 1) * GW],
                            oa.ap(), oa.key())
        ep_xslots = [wstage.sub(p * 512, GW) for p in range(4)] + \
                    [free8192.sub(p * 512, GW) for p in range(2)]
        ep_silu = [F_(PH + 11776, GW), F_(PH + 12288, GW)]
        ep_cg = [(g, c) for g in range(G) for c in range(DC)]
        p5slots = [srA.sub(0, 1024), srA.sub(1024, 1024), srB.sub(0, 1024), srB.sub(1024, 1024)]
        p5src = [(wdtt, c) for c in range(DC)] + [(wft, c) for c in range(DC)]

        def _p5dma(i):
            hbm, c = p5src[i]
            stg = p5slots[i % 4]
            dma_in(stg.ap(), hbm[c * 128:(c + 1) * 128, :], stg.key())
        p5dma = [lambda i=i: _p5dma(i) for i in range(len(p5src))]

        def ep_xdma(i):
            g, c = ep_cg[i]
            stg = ep_xslots[i % 6]
            dma_in(stg.ap(), xT[c * 128:(c + 1) * 128, g * GW:(g + 1) * GW], stg.key())

        for i in range(6):
            ep_xdma(i)
        for i, (g, c) in enumerate(ep_cg):
            psX, kX = (psC_t, kC) if i % 2 == 0 else (psT_t, kT)
            pop_(lambda e, c=c, g=g, p=psX: e.matmul(
                p[:, :], owt_bf.ap(c * 128, (c + 1) * 128),
                states_bf.ap(g * GW, (g + 1) * GW), start=True, stop=True),
                [owt_bf.key(c * 128, (c + 1) * 128), states_bf.key(g * GW, (g + 1) * GW)], [kX])
            oa = ro_r(c, g)
            sil = ep_silu[i % 2]
            aop(lambda e, oa=oa, sil=sil: e.activation(sil.ap(), oa.ap(), Act.Silu),
                [oa.key()], [sil.key()])
            stg = ep_xslots[i % 6]
            tb0 = (i % 4) * 512
            vop(lambda e, p=psX, sil=sil, tb0=tb0: e.tensor_mul(tbuf.ap(tb0, tb0 + GW),
                                                                p[:, :], sil.ap()),
                [kX, sil.key()], [tbuf.key(tb0, tb0 + GW)])
            vop(lambda e, s=stg, tb0=tb0: e.tensor_add(tbuf.ap(tb0, tb0 + GW),
                                                       tbuf.ap(tb0, tb0 + GW), s.ap()),
                [tbuf.key(tb0, tb0 + GW), stg.key()], [tbuf.key(tb0, tb0 + GW)])
            if i + 6 < len(ep_cg):
                ep_xdma(i + 6)
            dma_out(x1sp[c * 128:(c + 1) * 128, g * GW:(g + 1) * GW],
                    tbuf.ap(tb0, tb0 + GW), tbuf.key(tb0, tb0 + GW),
                    hbmkey=("x1sp", c * S + g * GW, c * S + (g + 1) * GW))
            vop(lambda e, oa=oa, tb0=tb0: e.tensor_copy(oa.ap(), tbuf.ap(tb0, tb0 + GW)),
                [tbuf.key(tb0, tb0 + GW)], [oa.key()])
            if any(d[0] == "x1" for d in debug):
                dma_out(outs["dbg_x1"][c * 128:(c + 1) * 128, g * GW:(g + 1) * GW],
                        tbuf.ap(tb0, tb0 + GW), tbuf.key(tb0, tb0 + GW))

        if any(d[0] == "w1q" for d in debug):
            dma_out(outs["dbg_w1q"], warena.ap(0, 8192), warena.key(0, 8192))
            dma_out(outs["dbg_w2q"], warena.ap(8192, 16384), warena.key(8192, 16384))
        if any(d[0] == "hq" for d in debug):
            dma_out(outs["dbg_hq"], hq.ap(), hq.key())
        if any(d[0] == "vec1" for d in debug):
            vop(lambda e: e.tensor_copy(srA.ap(0, 640), pf_t[:, 0:640]), [("pf", 0, 640)], [srA.key(0, 640)])
            dma_out(outs["dbg_vec1"], srA.ap(0, 640), srA.key(0, 640))
        if any(d[0] == "pn" for d in debug):
            dma_out(outs["dbg_pn"][:, 0:512], tbuf.ap(0, 512), tbuf.key(0, 512))
            dma_out(outs["dbg_pn"][:, 512:1024], tbuf2.ap(0, 512), tbuf2.key(0, 512))
            for i, nm in enumerate(("w1", "w2")):
                for j, f in enumerate(("scale", "r", "negr")):
                    vop(lambda e, i=i, j=j, nm=nm, f=f: e.tensor_copy(
                        tbuf.ap(600 + i * 3 + j, 601 + i * 3 + j), scl[nm][f].ap()),
                        [scl[nm][f].key()], [tbuf.key(600 + i * 3 + j, 601 + i * 3 + j)])
            dma_out(outs["dbg_pn"][:, 1024:1030], tbuf.ap(600, 606), tbuf.key(600, 606))
        if any(d[0] == "sums" for d in debug):
            for i, nm in enumerate(("w1", "w2", "wx", "wdt", "wf")):
                vop(lambda e, i=i, nm=nm: e.tensor_copy(tbuf2.ap(i, i + 1), scl[nm]["sum"].ap()),
                    [scl[nm]["sum"].key()], [tbuf2.key(i, i + 1)])
            dma_out(outs["dbg_sums"], tbuf2.ap(0, 5), tbuf2.key(0, 5))
        finish_scale("wx", 2 * D * D, 1e-5, "add")
        finish_scale("wdt", D * D, 1e-5, "add")
        finish_scale("wf", D * D, 1e-5, "max")
        wdtq = warena.sub(0, 8192)
        wfq = warena.sub(8192, 8192)
        p5sa = [free8192.sub(0, 512), free8192.sub(512, 512), tern7a, tern7b]
        for i in range(4):
            p5dma[i]()
        for i in range(len(p5src)):
            if i + 4 < len(p5src):
                p5dma[i + 4]()
            stg = p5slots[i % 4]
            dstq = wdtq if i < DC else wfq
            nm = "wdt" if i < DC else "wf"
            c = p5src[i][1]
            for pc in range(2):
                ternarize(dstq, c * 1024 + pc * 512, stg, pc * 512, 512,
                          scl[nm]["r"], scl[nm]["negr"], p5sa[(i * 2 + pc) % 4], tbuf2)

        assert pump_gen[0] is None, "L1 abs pump must drain before P6 uses xf1 region"
        def x1_src(c, g, par):
            oa = ro_r(c, g)
            return (oa.ap(), oa.key(), oa.ap(), oa.key())

        stats_from(x1_src, nslots=4)
        for g in range(G):
            aap, ak, bap, bk = bcast_AB(g)
            for c in range(DC):
                oa = ro_r(c, g)
                xk = xf1.key(g * GW, (g + 1) * GW)
                vop(lambda e, oa=oa, g=g, a=aap: e.tensor_mul(xf1.ap(g * GW, (g + 1) * GW), a, oa.ap()),
                    [oa.key(), ak], [xk])
                vop(lambda e, g=g, b=bap: e.tensor_add(xf1.ap(g * GW, (g + 1) * GW),
                                                       b, xf1.ap(g * GW, (g + 1) * GW)),
                    [xk, bk], [xk])
                vop(lambda e, c=c, g=g: e.tensor_copy(xn.ap(c * S + g * GW, c * S + (g + 1) * GW),
                                                      xf1.ap(g * GW, (g + 1) * GW)),
                    [xk], [xn.key(c * S + g * GW, c * S + (g + 1) * GW)])
                vop(lambda e, g=g: e.scalar_tensor_tensor(xf1.ap(g * GW, (g + 1) * GW),
                                                          xf1.ap(g * GW, (g + 1) * GW), -1.0,
                                                          xf1.ap(g * GW, (g + 1) * GW),
                                                          Alu.mult, Alu.max), [xk], [xk])
                if c == 0:
                    vop(lambda e, g=g: e.tensor_copy(fold.ap(g * GW, (g + 1) * GW),
                                                     xf1.ap(g * GW, (g + 1) * GW)),
                        [xk], [fold.key(g * GW, (g + 1) * GW)])
                else:
                    vop(lambda e, g=g: e.tensor_max(fold.ap(g * GW, (g + 1) * GW),
                                                    fold.ap(g * GW, (g + 1) * GW),
                                                    xf1.ap(g * GW, (g + 1) * GW)),
                        [xk, fold.key(g * GW, (g + 1) * GW)], [fold.key(g * GW, (g + 1) * GW)])
        for bI in range(16):
            top_(lambda e, b_=bI: e.transpose(psT_t[:, 0:128],
                                              fold.ap(b_ * 128, (b_ + 1) * 128), idreg.ap()),
                 [fold.key(bI * 128, (bI + 1) * 128), idreg.key()], [kT])
            vop(lambda e, b_=bI: e.tensor_reduce(amaxt.ap(b_, b_ + 1), psT_t[:, 0:128], AxX, Alu.max),
                [kT], [amaxt.key(bI, bI + 1)])
        vop(lambda e: e.tensor_scalar(amaxt.ap(), amaxt.ap(), 1e-5, None, Alu.max),
            [amaxt.key()], [amaxt.key()])
        rop(sptk.ap(), amaxt.ap(), [amaxt.key()], [sptk.key()])
        vop(lambda e: e.tensor_scalar(sptk.ap(), sptk.ap(), 127.0, None, Alu.mult),
            [sptk.key()], [sptk.key()])
        vop(lambda e: e.tensor_scalar(rhot.ap(), amaxt.ap(), 1.0 / 127.0, None, Alu.mult),
            [amaxt.key()], [rhot.key()])
        if any(d[0] == "fold" for d in debug):
            dma_out(outs["dbg_fold"], fold.ap(), fold.key())
            dma_out(outs["dbg_amax"], amaxt.ap(), amaxt.key())
        dma_in(onehreg.ap(), c_oneh, onehreg.key())
        top_(lambda e: e.transpose(psT_t[0:16, 0:128], sptk.ap(), idreg.ap()),
             [sptk.key(), idreg.key()], [kT])
        vop(lambda e: e.tensor_copy(v16sp.ap(), psT_t[0:16, 0:128]), [kT], [v16sp.key()])
        top_(lambda e: e.transpose(psT_t[0:16, 0:128], rhot.ap(), idreg.ap()),
             [rhot.key(), idreg.key()], [kT])
        vop(lambda e: e.tensor_copy(v16rh.ap(), psT_t[0:16, 0:128]), [kT], [v16rh.key()])
        for bI in range(16):
            pop_(lambda e, b_=bI: e.matmul(psA_t[:, b_ * 128:(b_ + 1) * 128],
                                           onehreg.ap(b_ * 128, (b_ + 1) * 128), v16sp.ap(),
                                           start=True, stop=True),
                 [onehreg.key(), v16sp.key()], [kA(bI * 128, (bI + 1) * 128)])
        aop(lambda e: e.activation(spbc.ap(), psA_t[:, :], Act.Copy), [kA(0, 2048)], [spbc.key()])
        for bI in range(16):
            pop_(lambda e, b_=bI: e.matmul(psA_t[:, b_ * 128:(b_ + 1) * 128],
                                           onehreg.ap(b_ * 128, (b_ + 1) * 128), v16rh.ap(),
                                           start=True, stop=True),
                 [onehreg.key(), v16rh.key()], [kA(bI * 128, (bI + 1) * 128)])
        vop(lambda e: e.tensor_scalar(wsxr.ap(), psA_t[:, :], scl["wx"]["scale"].ap(), None, Alu.mult),
            [kA(0, 2048), scl["wx"]["scale"].key()], [wsxr.key()])
        if any(d[0] == "spbc" for d in debug):
            dma_out(outs["dbg_spbc"], pf_t[:, spbc.lo:spbc.lo + S], spbc.key())
        vop(lambda e: e.tensor_copy(sprow.ap(p1=1), r0(spbc, 0, S)), [spbc.key()], [sprow.key()])
        dma_in(r0(tbuf, 0, 2 * D), v_bx.unsqueeze(0), tbuf.key())
        vop(lambda e: e.tensor_scalar(bxrow.ap(p1=1), r0(tbuf, 0, 2 * D),
                                      ppv(scl["wx"]["r"], 0, 0, 1), None, Alu.mult),
            [tbuf.key(), scl["wx"]["r"].key()], [bxrow.key()])
        dma_in(r0(tbuf, 0, D), v_bdt.unsqueeze(0), tbuf.key(0, D))
        vop(lambda e: e.tensor_scalar(bdrow.ap(p1=1), r0(tbuf, 0, D),
                                      ppv(scl["wdt"]["r"], 0, 0, 1), None, Alu.mult),
            [tbuf.key(0, D), scl["wdt"]["r"].key()], [bdrow.key()])
        for g in range(G):
            aap, ak, bap, bk = bcast_AB(g)
            for c in range(DC):
                oa = ro_r(c, g)
                xk = xf1.key(0, GW)
                vop(lambda e, oa=oa, a=aap: e.tensor_mul(xf1.ap(0, GW), a, oa.ap()),
                    [oa.key(), ak], [xk])
                vop(lambda e, b=bap: e.tensor_add(xf1.ap(0, GW), b, xf1.ap(0, GW)),
                    [xk, bk], [xk])
                vop(lambda e, g=g: e.tensor_mul(xf1.ap(0, GW), xf1.ap(0, GW),
                                                spbc.ap(g * GW, (g + 1) * GW)),
                    [xk, spbc.key()], [xk])
                vop(lambda e: e.tensor_scalar(xf1.ap(0, GW), xf1.ap(0, GW), Mb_v.ap(), nMb_v.ap(),
                                              Alu.add, Alu.add),
                    [xk, Mb_v.key(), nMb_v.key()], [xk])
                vop(lambda e, oa=oa: e.tensor_scalar(oa.ap(), xf1.ap(0, GW), -128.0, 127.0,
                                                     Alu.max, Alu.min), [xk], [oa.key()])
        if any(d[0] == "rq" for d in debug):
            for c in range(DC):
                for g in range(G):
                    oa = ro_r(c, g)
                    dma_out(outs["dbg_rq"][c * 128:(c + 1) * 128, g * GW:(g + 1) * GW], oa.ap(), oa.key())
        for bI in range(16):
            pop_(lambda e, b_=bI: e.matmul(psA_t[:, b_ * 128:(b_ + 1) * 128],
                                           onehreg.ap(b_ * 128, (b_ + 1) * 128), v16rh.ap(),
                                           start=True, stop=True),
                 [onehreg.key(), v16rh.key()], [kA(bI * 128, (bI + 1) * 128)])
        vop(lambda e: e.tensor_scalar(wsdr.ap(), psA_t[:, :], scl["wdt"]["scale"].ap(), None, Alu.mult),
            [kA(0, 2048), scl["wdt"]["scale"].key()], [wsdr.key()])

        vop(lambda e: e.memset(zerot.ap(), 0.0), writes=[zerot.key()])
        b1r, b2r, b3r, b4r, b5r = Bs

        def wx_dma(oc, par):
            stg = wstage.sub(par * 1024, 1024)
            for q in range(4):
                dma_in(stg.ap(q * 256, (q + 1) * 256),
                       wxt[oc * 128:(oc + 1) * 128, q * 256:(q + 1) * 256],
                       stg.key(q * 256, (q + 1) * 256))

        def wx_tern(oc, par):
            stg = wstage.sub(par * 1024, 1024)
            wq = wxs.sub(par * 1024, 1024)
            for pc in range(2):
                ternarize(wq, pc * 512, stg, pc * 512, 512,
                          scl["wx"]["r"], scl["wx"]["negr"], (tern7a, tern7b)[pc], tbuf2)

        def l1_mm(c):
            for g in range(G):
                for dc2 in range(DC):
                    pop_(lambda e, c=c, g=g, dc2=dc2: e.matmul(
                        psA_t[:, g * 512:(g + 1) * 512],
                        wdtq.ap(dc2 * 1024 + c * 128, dc2 * 1024 + (c + 1) * 128),
                        ro_r(dc2, g).ap(), start=(dc2 == 0), stop=False),
                        [wdtq.key(dc2 * 1024 + c * 128, dc2 * 1024 + (c + 1) * 128),
                         ro_r(dc2, g).key()], [kA(g * 512, (g + 1) * 512)])
                pop_(lambda e, c=c, g=g: e.matmul(
                    psA_t[:, g * 512:(g + 1) * 512], bdrow.ap(c * 128, (c + 1) * 128, 0, 1),
                    sprow.ap(g * GW, (g + 1) * GW, 0, 1), start=False, stop=True),
                    [bdrow.key(c * 128, (c + 1) * 128), sprow.key(g * GW, (g + 1) * GW)],
                    [kA(g * 512, (g + 1) * 512)])
                vop(lambda e, g=g: e.tensor_mul(b1r.ap(g * 512, (g + 1) * 512),
                                                psA_t[:, g * 512:(g + 1) * 512],
                                                wsdr.ap(g * 512, (g + 1) * 512)),
                    [kA(g * 512, (g + 1) * 512), wsdr.key()], [b1r.key(g * 512, (g + 1) * 512)])
            for oc, dst, par in ((c, b2r, 0), (c + 8, b3r, 1)):
                wq = wxs.sub(par * 1024, 1024)
                for g in range(G):
                    for dc2 in range(DC):
                        pop_(lambda e, g=g, dc2=dc2, wq=wq: e.matmul(
                            psA_t[:, g * 512:(g + 1) * 512], wq.ap(dc2 * 128, (dc2 + 1) * 128),
                            ro_r(dc2, g).ap(), start=(dc2 == 0), stop=False),
                            [wq.key(dc2 * 128, (dc2 + 1) * 128), ro_r(dc2, g).key()],
                            [kA(g * 512, (g + 1) * 512)])
                    pop_(lambda e, oc=oc, g=g: e.matmul(
                        psA_t[:, g * 512:(g + 1) * 512], bxrow.ap(oc * 128, (oc + 1) * 128, 0, 1),
                        sprow.ap(g * GW, (g + 1) * GW, 0, 1), start=False, stop=True),
                        [bxrow.key(oc * 128, (oc + 1) * 128), sprow.key(g * GW, (g + 1) * GW)],
                        [kA(g * 512, (g + 1) * 512)])
                    vop(lambda e, dst=dst, g=g: e.tensor_mul(dst.ap(g * 512, (g + 1) * 512),
                                                             psA_t[:, g * 512:(g + 1) * 512],
                                                             wsxr.ap(g * 512, (g + 1) * 512)),
                        [kA(g * 512, (g + 1) * 512), wsxr.key()],
                        [dst.key(g * 512, (g + 1) * 512)])
                    vop(lambda e, dst=dst, g=g: e.tensor_scalar(dst.ap(g * 512, (g + 1) * 512),
                                                                dst.ap(g * 512, (g + 1) * 512),
                                                                5.0, -5.0, Alu.min, Alu.max),
                        [dst.key(g * 512, (g + 1) * 512)], [dst.key(g * 512, (g + 1) * 512)])
            if any(d[0] == "l1t" for d in debug) and c == 0:
                dma_out(outs["dbg_l1t"][:, 0:S], b1r.ap(), b1r.key())
                dma_out(outs["dbg_l1t"][:, S:2 * S], b2r.ap(), b2r.key())
                dma_out(outs["dbg_l1t"][:, 2 * S:3 * S], b3r.ap(), b3r.key())
        def l1_chain(c, tern_thunk=None):
            for fn, scale, bias in ((Act.Abs, None, None), (Act.Exp, -1.0, None),
                                    (Act.Ln, None, 1.0)):
                for g in range(G):
                    s_, src_ = ((b1r, b5r) if fn == Act.Abs else (b5r, b5r))
                    aop(lambda e, fn=fn, scale=scale, bias=bias, g=g, s_=s_:
                        e.activation(b5r.ap(g * 512, (g + 1) * 512),
                                     s_.ap(g * 512, (g + 1) * 512) if fn != Act.Abs
                                     else b1r.ap(g * 512, (g + 1) * 512), fn,
                                     **({} if scale is None else {"scale": scale}),
                                     **({} if bias is None else {"bias": bias})),
                        [(b1r if fn == Act.Abs else b5r).key(g * 512, (g + 1) * 512)],
                        [b5r.key(g * 512, (g + 1) * 512)])
            if tern_thunk is not None:
                tern_thunk()
            vop(lambda e: e.scalar_tensor_tensor(b1r.ap(), b1r.ap(), 0.0, b5r.ap(),
                                                 Alu.max, Alu.add), [b1r.key(), b5r.key()], [b1r.key()])
            vop(lambda e: e.tensor_scalar(b1r.ap(), b1r.ap(), 0.001, 1.0, Alu.max, Alu.min),
                [b1r.key()], [b1r.key()])
            vop(lambda e, c=c: e.tensor_scalar(b4r.ap(), b1r.ap(), ppv(aneg, c), -10.0,
                                               Alu.mult, Alu.max), [b1r.key(), aneg.key()], [b4r.key()])
            aop(lambda e: e.activation(b4r.ap(), b4r.ap(), Act.Exp), [b4r.key()], [b4r.key()])
            aop(lambda e: e.activation(b3r.ap(), b3r.ap(), Act.Tanh), [b3r.key()], [b3r.key()])
            vop(lambda e: e.tensor_mul(b2r.ap(), b1r.ap(), b2r.ap()), [b1r.key(), b2r.key()], [b2r.key()])
            vop(lambda e: e.tensor_tensor_scan(b4r.ap(), b4r.ap(), zerot.ap(), 1.0,
                                               Alu.mult, Alu.add), [b4r.key(), zerot.key()], [b4r.key()])
            if any(d[0] == "l1t" for d in debug) and c == 0:
                dma_out(outs["dbg_l1t"][:, 3 * S:4 * S], b4r.ap(), b4r.key())
            vop(lambda e: e.tensor_copy(b5r.ap(0, 1), S_v.ap()), [S_v.key()], [b5r.key(0, 1)])
            vop(lambda e: e.tensor_scalar(b5r.ap(1, S), b4r.ap(0, S - 1), S_v.ap(), F_v.ap(),
                                          Alu.mult, Alu.max),
                [b4r.key(), S_v.key(), F_v.key()], [b5r.key(1, S)])
            rop(b5r.ap(), b5r.ap(), [b5r.key()], [b5r.key()])
            vop(lambda e: e.tensor_mul(b2r.ap(), b2r.ap(), b5r.ap()), [b2r.key(), b5r.key()], [b2r.key()])
            vop(lambda e: e.tensor_scalar(b5r.ap(), b2r.ap(), T_v.ap(), nT_v.ap(), Alu.min, Alu.max),
                [b2r.key(), T_v.key(), nT_v.key()], [b5r.key()])
            vop(lambda e: e.tensor_tensor(b1r.ap(), b5r.ap(), b2r.ap(), Alu.is_equal),
                [b5r.key(), b2r.key()], [b1r.key()])
            vop(lambda e: e.tensor_scalar(b2r.ap(), b5r.ap(), G_v.ap(), None, Alu.mult),
                [b5r.key(), G_v.key()], [b2r.key()])
            vop(lambda e: e.tensor_tensor_scan(b2r.ap(), b2r.ap(), zerot.ap(), 0.0,
                                               Alu.add, Alu.add), [b2r.key(), zerot.key()], [b2r.key()])
            vop(lambda e: e.scalar_tensor_tensor(b5r.ap(), b2r.ap(), -1.0, b2r.ap(),
                                                 Alu.mult, Alu.max), [b2r.key()], [b5r.key()])
            vop(lambda e: e.tensor_scalar(b5r.ap(), b5r.ap(), U_v.ap(), None, Alu.is_le),
                [b5r.key(), U_v.key()], [b5r.key()])
            vop(lambda e: e.tensor_tensor_scan(b1r.ap(), b1r.ap(), b5r.ap(), 4096.0,
                                               Alu.mult, Alu.mult), [b1r.key(), b5r.key()], [b1r.key()])
            vop(lambda e: e.tensor_mul(b2r.ap(), b4r.ap(), b2r.ap()), [b4r.key(), b2r.key()], [b2r.key()])
            vop(lambda e: e.tensor_scalar(b2r.ap(), b2r.ap(), U_v.ap(), nU_v.ap(), Alu.min, Alu.max),
                [b2r.key(), U_v.key(), nU_v.key()], [b2r.key()])
            vop(lambda e: e.tensor_mul(b2r.ap(), b2r.ap(), b3r.ap()), [b2r.key(), b3r.key()], [b2r.key()])
            vop(lambda e: e.tensor_mul(b2r.ap(), b2r.ap(), b1r.ap()), [b2r.key(), b1r.key()], [b2r.key()])
            if any(d[0] == "l1t" for d in debug) and c == 0:
                dma_out(outs["dbg_l1t"][:, 4 * S:5 * S], b2r.ap(), b2r.key())
            g1sl = [(psC_t[:, :], kC), (psT_t[:, :], kT),
                    (free8192.ap(0, GW), free8192.key(0, GW)),
                    (free8192.ap(GW, 2 * GW), free8192.key(GW, 2 * GW))]
            for g in range(G):
                pbk = (g % 2) * 512
                for dc2 in range(DC):
                    pop_(lambda e, c=c, g=g, dc2=dc2, pbk=pbk: e.matmul(
                        psB_t[:, pbk:pbk + 512],
                        wfq.ap(dc2 * 1024 + c * 128, dc2 * 1024 + (c + 1) * 128),
                        xn.ap(dc2 * S + g * GW, dc2 * S + (g + 1) * GW),
                        start=(dc2 == 0), stop=(dc2 == DC - 1)),
                        [wfq.key(dc2 * 1024 + c * 128, dc2 * 1024 + (c + 1) * 128),
                         xn.key(dc2 * S + g * GW, dc2 * S + (g + 1) * GW)], [kB(pbk, pbk + 512)])
                gap, gk = g1sl[g]
                aop(lambda e, g=g, pbk=pbk, gap=gap: e.activation(
                    gap, psB_t[:, pbk:pbk + 512], Act.Silu,
                    scale=scl["wf"]["scale"].ap()),
                    [kB(pbk, pbk + 512), scl["wf"]["scale"].key()], [gk])
            for g in range(G):
                gap, gk = g1sl[g]
                vop(lambda e, g=g, gap=gap: e.tensor_mul(b2r.ap(g * GW, (g + 1) * GW),
                                                         b2r.ap(g * GW, (g + 1) * GW), gap),
                    [b2r.key(g * GW, (g + 1) * GW), gk], [b2r.key(g * GW, (g + 1) * GW)])
            if any(d[0] == "l1t" for d in debug) and c == 0:
                for g in range(G):
                    gap, gk = g1sl[g]
                    vop(lambda e, g=g, gap=gap: e.tensor_copy(b4r.ap(g * GW, (g + 1) * GW), gap),
                        [gk], [b4r.key(g * GW, (g + 1) * GW)])
                dma_out(outs["dbg_l1t"][:, 5 * S:6 * S], b4r.ap(), b4r.key())
            dma_rd(b5r.ap(), x1sp[c * 128:(c + 1) * 128, :], b5r.key(),
                   hbmkey=("x1sp", c * S, (c + 1) * S))
            vop(lambda e: e.tensor_add(b2r.ap(), b2r.ap(), b5r.ap()), [b2r.key(), b5r.key()], [b2r.key()])
            if any(d[0] == "y0" for d in debug) and c == 0:
                dma_out(outs["dbg_y0"], b2r.ap(), b2r.key())
            dma_out(outT[c * 128:(c + 1) * 128, :], b2r.ap(), b2r.key())

        wx_dma(0, 0)
        wx_dma(8, 1)
        wx_tern(0, 0)
        wx_tern(8, 1)
        wx_dma(1, 0)
        wx_dma(9, 1)
        for c in range(DC):
            l1_mm(c)
            def _tern(c=c):
                if c + 1 < DC:
                    wx_tern(c + 1, 0)
                    wx_tern(c + 9, 1)
                    if c + 2 < DC:
                        wx_dma(c + 2, 0)
                        wx_dma(c + 10, 1)
            l1_chain(c, tern_thunk=_tern)

        if stop is not None:
            st.truncate(stop)
        if any(d[0] == "vsync" for d in debug):
            allw = [(s_, st.cnt[s_]) for s_ in Sched.STREAMS]
            st.op("din0", (lambda e, sem: e.dma_start(out=outs["dbg_vsync"], in_=pf_t[:, 0:640])
                           .then_inc(sem, 16)), inc=16, extra=allw)
            st.op("dout0", (lambda e, sem: e.dma_start(out=outs["dbg_vgp"], in_=pf_t[:, 0:640])
                            .then_inc(sem, 16)), inc=16, extra=allw)
        st.emit(blk, sems)
    return nc, ins, outs


def _host_layout(inputs):
    f32 = np.float32
    common = dict(
        w1t=np.ascontiguousarray(np.asarray(inputs["L0_ffn_W1"], f32).T),
        w2t=np.ascontiguousarray(np.asarray(inputs["L0_ffn_W2"], f32).T),
        wxt=np.ascontiguousarray(np.asarray(inputs["L1_attn_W_x"], f32).T.reshape(8, 128, 16, 128).transpose(2, 1, 0, 3).reshape(2048, 1024)),
        wdtt=np.ascontiguousarray(np.asarray(inputs["L1_attn_W_dt"], f32).T),
        wft=np.ascontiguousarray(np.asarray(inputs["L1_ffn_weight"], f32).T),
        kwt=np.ascontiguousarray(np.asarray(inputs["L0_attn_key_w"], f32).T
                                 .reshape(8, 128, -1).transpose(1, 0, 2).reshape(128, -1)),
        vwt=np.ascontiguousarray(np.asarray(inputs["L0_attn_value_w"], f32).T
                                 .reshape(8, 128, -1).transpose(1, 0, 2).reshape(128, -1)),
        owt=np.ascontiguousarray(np.asarray(inputs["L0_attn_output_w"], f32).T),
        v_td=np.asarray(inputs["L0_attn_time_decay"], f32),
        v_tf=np.asarray(inputs["L0_attn_time_first"], f32),
        v_tsg=np.asarray(inputs["L0_attn_time_shift_gates"], f32),
        v_lnw=np.asarray(inputs["L0_attn_ln_w"], f32),
        v_lnb=np.asarray(inputs["L0_attn_ln_b"], f32),
        v_b1=np.asarray(inputs["L0_ffn_b1"], f32),
        v_b2=np.asarray(inputs["L0_ffn_b2"], f32),
        v_bx=np.asarray(inputs["L1_attn_b_x"], f32),
        v_bdt=np.asarray(inputs["L1_attn_b_dt"], f32),
        v_alog=np.asarray(inputs["L1_attn_A_log"], f32),
        vpack=np.concatenate(
            [np.asarray(inputs[k], f32).reshape(-1, 128).T for k in
             ("L0_attn_time_shift_gates", "L0_attn_ln_w", "L0_attn_ln_b",
              "L1_attn_A_log", "L0_ffn_b2", "L0_ffn_b1")] +
            [np.pad(np.asarray(inputs[k], f32).reshape(-1, 64).T, ((0, 64), (0, 0)))
             for k in ("L0_attn_time_decay", "L0_attn_time_first")], axis=1),
        c_ident=np.eye(128, dtype=f32),
        c_ones_bf=np.ones((128, 1), NPBF),
        c_ones_f=np.ones((128, 1), f32),
        c_onesbc=np.ones((1, 128), f32),
    )
    oneh = np.zeros((16, S), f32)
    for i in range(16):
        oneh[i, i * 128:(i + 1) * 128] = 1.0
    common["c_oneh"] = oneh
    x = np.asarray(inputs["L0_attn_x"], f32)
    in_maps = []
    for c in range(B):
        m = dict(common)
        m["xT"] = np.ascontiguousarray(x[c].T)
        in_maps.append(m)
    return in_maps


def kernel(**inputs):
    in_maps = _host_layout(inputs)
    nc, ins, outs = build_core()
    res = run_bass_kernel_spmd(nc, in_maps, core_ids=list(range(B)))
    out = np.stack([np.asarray(r["outT"]).T for r in res.results], axis=0)
    return np.ascontiguousarray(out, dtype=np.float32)


# version 26 (final)
# speedup vs baseline: 1.0475x; 1.0337x over previous
import numpy as np
import ml_dtypes
import concourse.bass as bass
import concourse.mybir as mybir
from concourse.bass_utils import run_bass_kernel_spmd

F32 = mybir.dt.float32
BF16 = mybir.dt.bfloat16
Alu = mybir.AluOpType
Act = mybir.ActivationFunctionType
AxX = mybir.AxisListType.X
NPBF = ml_dtypes.bfloat16

FMAX = 3.4028235e38
M23 = 1.5 * 2.0 ** 23
M52 = 1.5 * 2.0 ** 52
P64 = 2.0 ** 64

B, S, D, N, FF = 8, 2048, 1024, 64, 4096
DC = D // 128
G = 4
GW = S // G


class Sched:
    STREAMS = ("pe", "dve", "act", "pool",
               "din0", "din1", "din2", "din3", "din4", "din5", "din6", "din7",
               "dout0", "dout1")

    def __init__(self):
        self.cnt = {s: 0 for s in self.STREAMS}
        self.ops = []
        self.access = {}

    def _deps(self, stream, rds, wrs):
        need = {}
        for (arena, lo, hi, is_w) in [(a, l, h, False) for (a, l, h) in rds] + \
                                     [(a, l, h, True) for (a, l, h) in wrs]:
            lst = self.access.get(arena)
            if not lst:
                continue
            for (l2, h2, s2, c2, w2) in lst:
                if l2 < hi and lo < h2 and (w2 or is_w) and not (s2 == stream == "pe"):
                    if need.get(s2, 0) < c2:
                        need[s2] = c2
        return need

    def marker(self, name):
        self.ops.append(("MARK", name, None))

    def truncate(self, name):
        idx = None
        for i, t in enumerate(self.ops):
            if t[0] == "MARK" and t[1] == name:
                idx = i
                break
        assert idx is not None, name
        self.ops = self.ops[:idx]
        self.cnt = {s: 0 for s in self.STREAMS}
        for rec in self.ops:
            if rec[0] != "MARK":
                self.cnt[rec[0]] += rec[3]
        self.access = {}

    def op(self, stream, fn, reads=(), writes=(), inc=1, extra=(), kind="dma"):
        need = self._deps(stream, reads, writes)
        for s2, c2 in extra:
            if need.get(s2, 0) < c2:
                need[s2] = c2
        c_after = self.cnt[stream] + inc
        self.cnt[stream] = c_after
        self.ops.append((stream, fn, need, inc, kind))
        for (a, lo, hi) in reads:
            self.access.setdefault(a, []).append((lo, hi, stream, c_after, False))
        for (a, lo, hi) in writes:
            lst = self.access.setdefault(a, [])
            lst[:] = [e for e in lst if not (lo <= e[0] and e[1] <= hi)]
            lst.append((lo, hi, stream, c_after, True))

    def emit(self, blk, sems):
        watermark = {}

        waited = set()
        for rec in self.ops:
            if rec[0] == "MARK":
                continue
            for dep, c in rec[2].items():
                waited.add((dep, c))

        def section(eng_obj, streams, tag):
            run = {s: 0 for s in self.STREAMS}
            for rec in self.ops:
                if rec[0] == "MARK":
                    continue
                (stream, fn, need, inc, kind) = rec
                if stream in self.STREAMS:
                    run[stream] += inc
                if stream not in streams:
                    continue
                for dep, c in sorted(need.items()):
                    key = (tag, dep)
                    if watermark.get(key, 0) < c:
                        watermark[key] = c
                        eng_obj.wait_ge(sems[dep], c)
                if kind == "dma":
                    fn(eng_obj, sems[stream])
                else:
                    instr = fn(eng_obj)
                    if kind == "compute_recip" or (
                            (stream, run[stream]) in waited and stream in ("dve", "act")):
                        eng_obj.drain().then_inc(sems[stream], inc)
                    else:
                        instr.then_inc(sems[stream], inc)
            if "dout0" in streams:
                fincnt = {s: 0 for s in self.cnt}
                for rec in self.ops:
                    if rec[0] != "MARK":
                        fincnt[rec[0]] += rec[3]
                for s_ in ("dout0", "dout1", "din0", "din1", "din2", "din3",
                           "din4", "din5", "din6", "din7"):
                    eng_obj.wait_ge(sems[s_], fincnt[s_])

        @blk.sync
        def _(e):
            section(e, ("din0", "din1", "din2", "din3", "din4", "din5", "din6", "din7"),
                    "sync")

        @blk.vector
        def _(e):
            section(e, ("dve",), "dve")

        @blk.scalar
        def _(e):
            section(e, ("act",), "act")

        @blk.gpsimd
        def _(e):
            section(e, ("pool", "dout0", "dout1"), "gp")

        @blk.tensor
        def _(e):
            section(e, ("pe",), "pe")


class Reg:

    def __init__(self, pool_t, pool_name, lo, n, p0=0, p1=128):
        self.t, self.name, self.lo, self.n, self.p0, self.p1 = pool_t, pool_name, lo, n, p0, p1

    def ap(self, c0=0, c1=None, p0=None, p1=None):
        c1 = self.n if c1 is None else c1
        p0 = self.p0 if p0 is None else p0
        p1 = self.p1 if p1 is None else p1
        return self.t[p0:p1, self.lo + c0:self.lo + c1]

    def key(self, c0=0, c1=None):
        c1 = self.n if c1 is None else c1
        return (self.name, self.lo + c0, self.lo + c1)

    def sub(self, off, n):
        return Reg(self.t, self.name, self.lo + off, n, self.p0, self.p1)


def build_core(debug=(), stop=None):
    nc = bass.Bass("TRN2", target_bir_lowering=False)
    ins, outs = {}, {}

    def I(name, shape, dt=F32):
        ins[name] = nc.dram_tensor(name, shape, dt, kind="ExternalInput").ap()
        return ins[name]

    def O(name, shape, dt=F32):
        outs[name] = nc.dram_tensor(name, shape, dt, kind="ExternalOutput").ap()
        return outs[name]

    xT = I("xT", [D, S])
    vpk = I("vpack", [128, 74])
    w1t = I("w1t", [D, FF])
    w2t = I("w2t", [FF, D])
    wxt = I("wxt", [2 * D, D])
    wdtt = I("wdtt", [D, D])
    wft = I("wft", [D, D])
    kwt = I("kwt", [128, DC * N])
    vwt = I("vwt", [128, DC * N])
    owt = I("owt", [N, D])
    v_td = I("v_td", [N])
    v_tf = I("v_tf", [N])
    v_tsg = I("v_tsg", [D])
    v_lnw = I("v_lnw", [D])
    v_lnb = I("v_lnb", [D])
    v_b1 = I("v_b1", [FF])
    v_b2 = I("v_b2", [D])
    v_bx = I("v_bx", [2 * D])
    v_bdt = I("v_bdt", [D])
    v_alog = I("v_alog", [D])
    c_ident = I("c_ident", [128, 128])
    c_oneh = I("c_oneh", [16, S])
    c_ones_bf = I("c_ones_bf", [128, 1], BF16)
    c_ones_f = I("c_ones_f", [128, 1])
    c_onesbc = I("c_onesbc", [1, 128])
    x1sp = nc.dram_tensor("x1spill", [D, S], F32, kind="Internal").ap()

    outT = O("outT", [D, S])
    for dbgspec in debug:
        dnm, dsh = dbgspec[0], dbgspec[1]
        ddt = dbgspec[2] if len(dbgspec) > 2 else F32
        O("dbg_" + dnm, dsh, ddt)

    st = Sched()

    import contextlib
    with contextlib.ExitStack() as EX:
        E = EX.enter_context
        PF_COLS = 21296
        PB_COLS = 63400
        pf_t = E(nc.sbuf_tensor("poolf", [128, PF_COLS], F32))
        pb_t = E(nc.sbuf_tensor("poolb", [128, PB_COLS], BF16))
        psA_t = E(nc.psum_tensor("psA", [128, 2048], F32))
        psB_t = E(nc.psum_tensor("psB", [128, 1024], F32))
        psC_t = E(nc.psum_tensor("psC", [128, 512], F32))
        psT_t = E(nc.psum_tensor("psT", [128, 512], F32))
        sems = {s: E(nc.semaphore("sem_" + s)) for s in Sched.STREAMS}
        blk = E(nc.Block())

        def kA(c0, c1):
            return ("psA", c0, c1)

        def kB(c0, c1):
            return ("psB", c0, c1)

        kC = ("psC", 0, 512)
        kT = ("psT", 0, 512)

        def F_(lo, n, p0=0, p1=128):
            assert lo + n <= PF_COLS
            return Reg(pf_t, "pf", lo, n, p0, p1)

        def B_(lo, n, p0=0, p1=128):
            assert lo + n <= PB_COLS
            return Reg(pb_t, "pb", lo, n, p0, p1)

        xn = B_(0, S * DC)
        ro = B_(16384, S * DC)
        warena = B_(32768, 16384)
        PBH = 49152
        zb = B_(PBH, GW * DC)
        tb2b = B_(PBH + 4096, 2 * GW)
        hq = B_(PBH, GW * 8)
        dump5 = B_(PBH, 2048)
        wxs = B_(PBH, 2048)
        sprow = B_(PBH + 2048, S)
        bxrow = B_(PBH + 4096, 2 * D)
        bdrow = B_(PBH + 6144, D)
        PBS = 56320
        states_bf = B_(PBS, S, p1=64)
        kp_bf = B_(PBS + 2048, N * DC)
        vpw_bf = B_(PBS + 2560, N * DC)
        owt_bf = B_(PBS + 3072, D, p1=64)
        onesb_bf = B_(PBS + 4096, 8)
        sqbf = B_(PBS + 4104, 2048)
        dump1 = ro.sub(0, 2048)
        assert PBS + 5128 <= PB_COLS

        def slot(c, g):
            return (g * DC + c) * GW

        def ro_r(c, g):
            return ro.sub(slot(c, g), GW)

        idreg = F_(0, 128)
        onesbcreg = F_(128, 128)
        VEC = 256
        vp_ = [VEC]

        def vecreg(k=1):
            o = vp_[0]
            vp_[0] += k
            assert vp_[0] <= VEC + 384
            return F_(o, k)

        flag = vecreg(); invflag = vecreg()
        gsig = vecreg(DC); gsig1m = vecreg(DC)
        lnw_v = vecreg(DC); lnb_v = vecreg(DC)
        aneg = vecreg(DC)
        b1_v = vecreg(32)
        b2_v = vecreg(DC)
        tdv = vecreg(); tfv = vecreg()
        etf = vecreg(); negetf = vecreg(); ekb = vecreg()
        wv = vecreg()
        bkv = vecreg(); bvv = vecreg()
        tsg_r = vecreg(DC); alog_r = vecreg(DC)
        ones_f = vecreg()
        scl = {}
        for nm in ("w1", "w2", "wx", "wdt", "wf"):
            scl[nm] = dict(sum=vecreg(), r=vecreg(), negr=vecreg(), scale=vecreg())
        tmpv = vecreg(8)
        c_eps = vecreg(); c_mhalf = vecreg(); c_mone = vecreg()
        s0v = vecreg(); sIv = vecreg()
        pw2 = vecreg(); pw4 = vecreg(); pw8 = vecreg(); pwt = vecreg()
        S_v = vecreg(); F_v = vecreg(); T_v = vecreg(); nT_v = vecreg()
        U_v = vecreg(); nU_v = vecreg(); G_v = vecreg()
        Mb_v = vecreg(); nMb_v = vecreg(); nMb1_v = vecreg()

        PH = 1024
        wstage = F_(PH, 2048)
        srA = F_(PH + 2048, S)
        srB = F_(PH + 4096, S)
        srC = F_(PH + 6144, S)
        free8192 = F_(PH + 8192, 1024)
        tbuf = F_(PH + 9216, S)
        tbuf2 = F_(PH + 11264, 512)
        kvf = F_(PH + 11776, S, p1=64)
        wdec = F_(PH + 13824, S, p1=64)
        stf = F_(PH + 15872, S, p1=64)
        ekbuf = F_(PH + 17920, GW, p1=64)
        fold = F_(PH + 11776, S)
        xf1 = F_(PH + 13824, S)
        amaxt = F_(PH + 15872, 16)
        sptk = F_(PH + 15888, 16)
        rhot = F_(PH + 15904, 16)
        onehreg = F_(PH + 15920, S, p1=16)
        v16sp = F_(PH + 17968, 128, p1=16)
        v16rh = F_(PH + 18096, 128, p1=16)
        spbc = F_(PH + 11776, S)
        wsxr = F_(PH + 18224, 2048)
        wsdr = F_(PH + 15920, 2048)
        assert PH + 18224 + 2048 <= PF_COLS
        Bs = [F_(PH + 2048 + i * S, S) for i in range(5)]
        zerot = F_(PH + 12288, S)
        tern7a = F_(PH + 14336, 512)
        tern7b = F_(PH + 14848, 512)

        din_rot = [0]
        dout_rot = [0]

        def dma_in(dst_ap, src_ap, wkey, rkeys=(), slow=False):
            s_ = "din%d" % (din_rot[0] % 4)
            din_rot[0] += 1
            prev = st.cnt[s_]
            def fn(e, sem, d=dst_ap, s=src_ap, sl=slow):
                e.dma_start(out=d, in_=s, allow_slow_non_contiguous=sl).then_inc(sem, 16)
            st.op(s_, fn, reads=rkeys, writes=[wkey], inc=16, extra=[(s_, prev)])

        def dma_io(dst_ap, src_ap, rkeys, wkeys):
            s_ = "dout%d" % (dout_rot[0] % 2)
            dout_rot[0] += 1
            prev = st.cnt[s_]
            def fn(e, sem, d=dst_ap, s=src_ap):
                e.dma_start(out=d, in_=s).then_inc(sem, 16)
            st.op(s_, fn, reads=rkeys, writes=wkeys, inc=16, extra=[(s_, prev)])

        def dma_out(dst_ap, src_ap, rkey, hbmkey=None):
            dma_io(dst_ap, src_ap, [rkey], [hbmkey] if hbmkey else [])

        def dma_rd(dst_ap, src_ap, wkey, hbmkey=None):
            dma_io(dst_ap, src_ap, [hbmkey] if hbmkey else [], [wkey])

        def vop(fn, reads=(), writes=()):
            st.op("dve", fn, reads, writes, kind="compute")

        def aop(fn, reads=(), writes=()):
            st.op("act", fn, reads, writes, kind="compute")

        def pop_(fn, reads=(), writes=()):
            def f2(e, sem, fn=fn):
                fn(e).then_inc(sem, 1)
            st.op("pe", f2, reads, writes, kind="dma")

        def top_(fn, reads=(), writes=()):
            def f2(e, sem, fn=fn):
                fn(e)
                e.drain().then_inc(sem, 1)
            st.op("pe", f2, reads, writes, kind="dma")

        def rop(out_ap, in_ap, reads, writes):
            def fn(e, o=out_ap, i=in_ap):
                e.drain()
                r = e.reciprocal(o, i)
                return r
            st.op("dve", fn, reads, writes, kind="compute_recip")

        def ppv(reg, c=0, p0=None, p1=None):
            p0 = reg.p0 if p0 is None else p0
            p1 = reg.p1 if p1 is None else p1
            return reg.t[p0:p1, reg.lo + c:reg.lo + c + 1]

        def r0(reg, a=0, b=None):
            b = reg.n if b is None else b
            return reg.t[0:1, reg.lo + a:reg.lo + b]

        dma_in(idreg.ap(), c_ident, idreg.key())
        dma_in(r0(onesbcreg), c_onesbc, onesbcreg.key())
        dma_in(onesb_bf.ap(0, 1), c_ones_bf, onesb_bf.key())
        dma_in(ones_f.ap(), c_ones_f, ones_f.key())

        def load_pp(reg, hbm, p=128):
            dma_in(reg.ap(p0=0, p1=p), hbm.rearrange("(c p) -> p c", p=p), reg.key(), slow=True)

        dma_in(tbuf.ap(0, 74), vpk, tbuf.key(0, 74))
        vp_off = 0
        for reg, w in ((tsg_r, DC), (lnw_v, DC), (lnb_v, DC), (alog_r, DC),
                       (b2_v, DC), (b1_v, 32), (tdv, 1), (tfv, 1)):
            vop(lambda e, reg=reg, o=vp_off, w=w: e.tensor_copy(reg.ap(), tbuf.ap(o, o + w)),
                [tbuf.key(vp_off, vp_off + w)], [reg.key()])
            vp_off += w

        vop(lambda e: e.memset(c_eps.ap(), 1e-5), writes=[c_eps.key()])
        vop(lambda e: e.memset(c_mhalf.ap(), -0.5), writes=[c_mhalf.key()])
        vop(lambda e: e.memset(c_mone.ap(), -1.0), writes=[c_mone.key()])
        vop(lambda e: e.memset(flag.ap(), 2.0 ** -40), writes=[flag.key()])
        vop(lambda e: e.tensor_scalar(flag.ap(), flag.ap(), 1.0, -1.0, Alu.add, Alu.add),
            [flag.key()], [flag.key()])
        vop(lambda e: e.tensor_scalar(flag.ap(), flag.ap(), 2.0 ** 40, None, Alu.mult),
            [flag.key()], [flag.key()])
        vop(lambda e: e.tensor_scalar(invflag.ap(), flag.ap(), 1.0, -1.0, Alu.subtract, Alu.mult),
            [flag.key()], [invflag.key()])
        vop(lambda e: e.tensor_scalar(s0v.ap(), flag.ap(), 1.8446744073709552e19, 1.0,
                                      Alu.mult, Alu.add), [flag.key()], [s0v.key()])
        vop(lambda e: e.tensor_mul(pw2.ap(), s0v.ap(), s0v.ap()), [s0v.key()], [pw2.key()])
        vop(lambda e: e.tensor_mul(pw4.ap(), pw2.ap(), pw2.ap()), [pw2.key()], [pw4.key()])
        vop(lambda e: e.tensor_mul(pw8.ap(), pw4.ap(), pw4.ap()), [pw4.key()], [pw8.key()])
        vop(lambda e: e.tensor_scalar(S_v.ap(), pw8.ap(), 2.0 ** 64, None, Alu.mult),
            [pw8.key()], [S_v.key()])
        vop(lambda e: e.tensor_scalar(G_v.ap(), pw8.ap(), 2.0 ** 52, None, Alu.mult),
            [pw8.key()], [G_v.key()])
        vop(lambda e: e.tensor_mul(pwt.ap(), pw4.ap(), pw2.ap()), [pw4.key(), pw2.key()], [pwt.key()])
        vop(lambda e: e.tensor_scalar(T_v.ap(), pwt.ap(), float((2 - 2.0 ** -23) * 2.0 ** 63),
                                      None, Alu.mult), [pwt.key()], [T_v.key()])
        vop(lambda e: e.tensor_scalar(nT_v.ap(), T_v.ap(), -1.0, None, Alu.mult),
            [T_v.key()], [nT_v.key()])
        vop(lambda e: e.tensor_mul(pwt.ap(), pw8.ap(), pwt.ap()), [pw8.key(), pwt.key()], [pwt.key()])
        vop(lambda e: e.tensor_scalar(U_v.ap(), pwt.ap(), float((2 - 2.0 ** -23) * 2.0 ** 115),
                                      None, Alu.mult), [pwt.key()], [U_v.key()])
        vop(lambda e: e.tensor_scalar(nU_v.ap(), U_v.ap(), -1.0, None, Alu.mult),
            [U_v.key()], [nU_v.key()])
        rop(sIv.ap(), s0v.ap(), [s0v.key()], [sIv.key()])
        vop(lambda e: e.tensor_mul(pwt.ap(), sIv.ap(), sIv.ap()), [sIv.key()], [pwt.key()])
        vop(lambda e: e.tensor_mul(pw2.ap(), pwt.ap(), pwt.ap()), [pwt.key()], [pw2.key()])
        vop(lambda e: e.tensor_mul(pw4.ap(), pw2.ap(), pw2.ap()), [pw2.key()], [pw4.key()])
        vop(lambda e: e.tensor_mul(pw4.ap(), pw4.ap(), pw2.ap()), [pw2.key(), pw4.key()], [pw4.key()])
        vop(lambda e: e.tensor_mul(pw4.ap(), pw4.ap(), sIv.ap()), [sIv.key(), pw4.key()], [pw4.key()])
        vop(lambda e: e.tensor_scalar(F_v.ap(), pw4.ap(), 2.0 ** -125, None, Alu.mult),
            [pw4.key()], [F_v.key()])
        vop(lambda e: e.tensor_scalar(Mb_v.ap(), flag.ap(), float(np.float32(1.5 * 2.0 ** 52 - M23)),
                                      M23, Alu.mult, Alu.add), [flag.key()], [Mb_v.key()])
        vop(lambda e: e.tensor_scalar(nMb_v.ap(), Mb_v.ap(), -1.0, None, Alu.mult),
            [Mb_v.key()], [nMb_v.key()])
        vop(lambda e: e.tensor_scalar(nMb1_v.ap(), Mb_v.ap(), -1.0, 1.0, Alu.mult, Alu.add),
            [Mb_v.key()], [nMb1_v.key()])
        aop(lambda e: e.activation(gsig.ap(), tsg_r.ap(), Act.Sigmoid), [tsg_r.key()], [gsig.key()])
        vop(lambda e: e.tensor_scalar(gsig1m.ap(), gsig.ap(), 1.0, -1.0, Alu.subtract, Alu.mult),
            [gsig.key()], [gsig1m.key()])
        aop(lambda e: e.activation(tmpv.ap(0, DC), alog_r.ap(), Act.Abs), [alog_r.key()], [tmpv.key(0, DC)])
        aop(lambda e: e.activation(tmpv.ap(0, DC), tmpv.ap(0, DC), Act.Exp, scale=-1.0),
            [tmpv.key(0, DC)], [tmpv.key(0, DC)])
        aop(lambda e: e.activation(tmpv.ap(0, DC), tmpv.ap(0, DC), Act.Ln, bias=1.0),
            [tmpv.key(0, DC)], [tmpv.key(0, DC)])
        vop(lambda e: e.scalar_tensor_tensor(aneg.ap(), alog_r.ap(), 0.0, tmpv.ap(0, DC), Alu.max, Alu.add),
            [alog_r.key(), tmpv.key(0, DC)], [aneg.key()])
        vop(lambda e: e.tensor_scalar(aneg.ap(), aneg.ap(), -1.0, None, Alu.mult),
            [aneg.key()], [aneg.key()])
        aop(lambda e: e.activation(ppv(wv, p0=0, p1=64), ppv(tdv, p0=0, p1=64), Act.Exp),
            [tdv.key()], [wv.key()])
        aop(lambda e: e.activation(ppv(etf, p0=0, p1=64), ppv(tfv, p0=0, p1=64), Act.Exp),
            [tfv.key()], [etf.key()])
        vop(lambda e: e.tensor_scalar(ppv(negetf, p0=0, p1=64), ppv(etf, p0=0, p1=64), -1.0, None, Alu.mult),
            [etf.key()], [negetf.key()])

        snap_i = [0]

        def snap_vec(tag):
            if not any(d[0] == "snap" for d in debug):
                return
            i = snap_i[0]
            snap_i[0] += 1
            vop(lambda e, i=i: e.tensor_copy(srB.ap(0, 640), pf_t[:, 0:640]),
                [("pf", 0, 640)], [srB.key(0, 640)])
            dma_out(outs["dbg_snap"][:, i * 640:(i + 1) * 640], srB.ap(0, 640), srB.key(0, 640))

        snap_vec("after_aneg")
        kvstg = F_(PH + 2048, 2 * DC * N)
        dma_in(kvstg.ap(0, DC * N), kwt, kvstg.key(0, DC * N))
        dma_in(kvstg.ap(DC * N, 2 * DC * N), vwt, kvstg.key(DC * N, 2 * DC * N))
        for c in range(DC):
            ck = kvstg.key(c * N, (c + 1) * N)
            vop(lambda e, c=c: e.tensor_scalar(kp_bf.ap(c * N, (c + 1) * N),
                                               kvstg.ap(c * N, (c + 1) * N),
                                               ppv(lnw_v, c), None, Alu.mult),
                [ck, lnw_v.key()], [kp_bf.key(c * N, (c + 1) * N)])
            pop_(lambda e, c=c: e.matmul(psC_t[0:64, 0:1], kvstg.ap(c * N, (c + 1) * N),
                                         ppv(lnb_v, c), start=(c == 0), stop=(c == DC - 1)),
                 [ck, lnb_v.key()], [kC])
        vop(lambda e: e.tensor_copy(ppv(bkv, p0=0, p1=64), psC_t[0:64, 0:1]), [kC], [bkv.key()])
        for c in range(DC):
            ck = kvstg.key(DC * N + c * N, DC * N + (c + 1) * N)
            vop(lambda e, c=c: e.tensor_scalar(vpw_bf.ap(c * N, (c + 1) * N),
                                               kvstg.ap(DC * N + c * N, DC * N + (c + 1) * N),
                                               ppv(lnw_v, c), None, Alu.mult),
                [ck, lnw_v.key()], [vpw_bf.key(c * N, (c + 1) * N)])
            pop_(lambda e, c=c: e.matmul(psT_t[0:64, 0:1],
                                         kvstg.ap(DC * N + c * N, DC * N + (c + 1) * N),
                                         ppv(lnb_v, c), start=(c == 0), stop=(c == DC - 1)),
                 [ck, lnb_v.key()], [kT])
        vop(lambda e: e.tensor_copy(ppv(bvv, p0=0, p1=64), psT_t[0:64, 0:1]), [kT], [bvv.key()])
        vop(lambda e: e.tensor_mul(ppv(ekb, p0=0, p1=64), ppv(negetf, p0=0, p1=64),
                                   ppv(bkv, p0=0, p1=64)), [negetf.key(), bkv.key()], [ekb.key()])
        owtf = F_(PH + 2048 + N, D)
        dma_in(owtf.ap(p1=64), owt, owtf.key())
        vop(lambda e: e.tensor_copy(owt_bf.ap(), owtf.ap(p1=64)), [owtf.key()], [owt_bf.key()])

        absstg = Reg(pf_t, "pf", PH + 13824, 2048)

        def stream_abs_gen(hbm, rows, cols, sumreg, piece, dumpreg, first, stgreg=None):
            stgreg = absstg if stgreg is None else stgreg
            nchunk, npiece = rows // 128, cols // piece
            idx = 0
            for c in range(nchunk):
                for p in range(npiece):
                    stg = stgreg.sub((idx % 2) * piece, piece)
                    dma_in(stg.ap(), hbm[c * 128:(c + 1) * 128, p * piece:(p + 1) * piece], stg.key())
                    aop(lambda e, stg=stg: e.activation(dumpreg.ap(0, stg.n), stg.ap(), Act.Abs,
                                                        accum_out=tmpv.ap(6, 7)),
                        [stg.key()], [dumpreg.key(0, stg.n), tmpv.key(6, 7)])
                    if idx == 0 and first:
                        vop(lambda e: e.tensor_copy(sumreg.ap(), tmpv.ap(6, 7)),
                            [tmpv.key(6, 7)], [sumreg.key()])
                    else:
                        vop(lambda e: e.tensor_add(sumreg.ap(), sumreg.ap(), tmpv.ap(6, 7)),
                            [tmpv.key(6, 7), sumreg.key()], [sumreg.key()])
                    idx += 1
                    yield

        def stream_abs(hbm, rows, cols, sumreg, piece, dumpreg):
            for _ in stream_abs_gen(hbm, rows, cols, sumreg, piece, dumpreg, True):
                pass

        pump_gen = [None]

        def pump(n=1):
            g_ = pump_gen[0]
            if g_ is None:
                return
            for _ in range(n):
                if next(g_, "END") == "END":
                    pump_gen[0] = None
                    return

        def finish_scale(nm, numel, eps, mode):
            d = scl[nm]
            pop_(lambda e: e.matmul(psC_t[0:1, 0:1], d["sum"].ap(), ones_f.ap(),
                                    start=True, stop=True), [d["sum"].key(), ones_f.key()], [kC])
            vop(lambda e: e.tensor_copy(tmpv.ap(7, 8, 0, 1), psC_t[0:1, 0:1]), [kC], [tmpv.key(7, 8)])
            pop_(lambda e: e.matmul(psC_t[:, 0:1], r0(onesbcreg), tmpv.ap(7, 8, 0, 1),
                                    start=True, stop=True), [tmpv.key(7, 8), onesbcreg.key()], [kC])
            op1 = Alu.max if mode == "max" else Alu.add
            vop(lambda e: e.tensor_scalar(d["scale"].ap(), psC_t[:, 0:1], 1.0 / numel, eps,
                                          Alu.mult, op1), [kC], [d["scale"].key()])
            rop(d["r"].ap(), d["scale"].ap(), [d["scale"].key()], [d["r"].key()])
            vop(lambda e: e.tensor_scalar(d["negr"].ap(), d["r"].ap(), -1.0, None, Alu.mult),
                [d["r"].key()], [d["negr"].key()])

        def ternarize(dst_reg, dst_off, stg, off, ncols, rv, negrv, sa, sb):
            wk = stg.key(off, off + ncols)
            dk = dst_reg.key(dst_off, dst_off + ncols)
            aop(lambda e: e.activation(sa.ap(0, ncols), stg.ap(off, off + ncols), Act.Relu,
                                       scale=rv.ap(), bias=Mb_v.ap()),
                [wk, rv.key(), Mb_v.key()], [sa.key(0, ncols)])
            aop(lambda e: e.activation(sa.ap(0, ncols), sa.ap(0, ncols), Act.Relu,
                                       bias=nMb1_v.ap()),
                [sa.key(0, ncols), nMb1_v.key()], [sa.key(0, ncols)])
            vop(lambda e: e.tensor_scalar(dst_reg.ap(dst_off, dst_off + ncols),
                                          sa.ap(0, ncols), 2.0, -1.0, Alu.min, Alu.add),
                [sa.key(0, ncols)], [dk])

        snap_vec("after_p0")
        import itertools
        pump_gen[0] = itertools.chain(
            stream_abs_gen(w1t, D, FF, scl["w1"]["sum"], 512, dump1, True, stgreg=free8192),
            stream_abs_gen(w2t, FF, D, scl["w2"]["sum"], 512, dump1, True, stgreg=free8192))
        st.marker("p1")

        if any(d[0] == "vec0" for d in debug):
            vop(lambda e: e.tensor_copy(srA.ap(0, 640), pf_t[:, 0:640]), [("pf", 0, 640)], [srA.key(0, 640)])
            dma_out(outs["dbg_vec0"], srA.ap(0, 640), srA.key(0, 640))
        def st_sq(par):
            return (sqbf, par * GW) if par < 2 else (hq, (par - 2) * GW)

        def st_cs(par):
            return (sqbf, 1024 + par * GW) if par < 2 else (hq, 2048 + (par - 2) * GW)

        def stats_piece(c, g, par, sap, sk, bfap, bfk, alt=False):
            sqreg, sq0 = st_sq(par)
            sq1 = sq0 + GW
            aop(lambda e, s=sap, a=sq0, b=sq1, r=sqreg: e.activation(r.ap(a, b), s,
                                                                     Act.Square),
                [sk], [sqreg.key(sq0, sq1)])
            if bfap is None:
                csreg, cs0 = st_cs(par)
                cs1 = cs0 + GW
                vop(lambda e, s=sap, a=cs0, b=cs1, r=csreg: e.tensor_copy(r.ap(a, b), s),
                    [sk], [csreg.key(cs0, cs1)])
                bfap, bfk = csreg.ap(cs0, cs1), csreg.key(cs0, cs1)
            pop_(lambda e, c=c, g=g, x=bfap: e.matmul(
                psA_t[0:1, g * 512:(g + 1) * 512], onesb_bf.ap(0, 1), x,
                start=(c == 0), stop=(c == DC - 1)),
                [bfk, onesb_bf.key()], [kA(g * 512, (g + 1) * 512)])
            if alt:
                tgt, tk = psA_t[32:33, g * 512:(g + 1) * 512], kA(g * 512, (g + 1) * 512)
            elif g < 2:
                tgt, tk = psB_t[0:1, g * 512:(g + 1) * 512], kB(g * 512, (g + 1) * 512)
            elif g == 2:
                tgt, tk = psC_t[0:1, :], kC
            else:
                tgt, tk = psT_t[0:1, :], kT
            pop_(lambda e, c=c, t=tgt, a=sq0, b=sq1, r=sqreg: e.matmul(
                t, onesb_bf.ap(0, 1), r.ap(a, b),
                start=(c == 0), stop=(c == DC - 1)),
                [sqreg.key(sq0, sq1), onesb_bf.key()], [tk])

        def stats_final(alt=False):
            vop(lambda e: e.tensor_copy(r0(srA), psA_t[0:1, :]), [kA(0, 2048)], [srA.key()])
            if alt:
                vop(lambda e: e.tensor_copy(r0(srB), psA_t[32:33, :]), [kA(0, 2048)], [srB.key()])
            else:
                vop(lambda e: e.tensor_copy(r0(srB, 0, 1024), psB_t[0:1, :]), [kB(0, 1024)],
                    [srB.key(0, 1024)])
                vop(lambda e: e.tensor_copy(r0(srB, 1024, 1536), psC_t[0:1, :]), [kC],
                    [srB.key(1024, 1536)])
                vop(lambda e: e.tensor_copy(r0(srB, 1536, 2048), psT_t[0:1, :]), [kT],
                    [srB.key(1536, 2048)])
            vop(lambda e: e.tensor_scalar(r0(srA), r0(srA), 1.0 / D, None, Alu.mult),
                [srA.key()], [srA.key()])
            vop(lambda e: e.tensor_scalar(r0(srB), r0(srB), 1.0 / D, None, Alu.mult),
                [srB.key()], [srB.key()])
            vop(lambda e: e.tensor_mul(r0(srC), r0(srA), r0(srA)), [srA.key()], [srC.key()])
            vop(lambda e: e.tensor_sub(r0(srB), r0(srB), r0(srC)), [srB.key(), srC.key()], [srB.key()])
            aop(lambda e: e.activation(r0(srB), r0(srB), Act.Ln, bias=ppv(c_eps, 0, 0, 1)),
                [srB.key(), c_eps.key()], [srB.key()])
            aop(lambda e: e.activation(r0(srB), r0(srB), Act.Exp, scale=ppv(c_mhalf, 0, 0, 1)),
                [srB.key(), c_mhalf.key()], [srB.key()])
            vop(lambda e: e.tensor_mul(r0(srC), r0(srA), r0(srB)), [srA.key(), srB.key()], [srC.key()])
            vop(lambda e: e.tensor_scalar(r0(srC), r0(srC), -1.0, None, Alu.mult),
                [srC.key()], [srC.key()])

        def stats_from(get_src, nslots=2):
            for g in range(G):
                for c in range(DC):
                    pump(1)
                    par = (g * DC + c) % nslots
                    sap, sk, bfap, bfk = get_src(c, g, par)
                    stats_piece(c, g, par, sap, sk, bfap, bfk)
            stats_final()

        def bcast_AB(g):
            pop_(lambda e, g=g: e.matmul(psC_t[:, :], r0(onesbcreg), r0(srB, g * GW, (g + 1) * GW),
                                         start=True, stop=True), [srB.key(), onesbcreg.key()], [kC])
            pop_(lambda e, g=g: e.matmul(psT_t[:, :], r0(onesbcreg), r0(srC, g * GW, (g + 1) * GW),
                                         start=True, stop=True), [srC.key(), onesbcreg.key()], [kT])
            return psC_t[:, :], kC, psT_t[:, :], kT

        def xpiece(c, g, par):
            stg = wstage.sub(par * 512, GW)
            dma_in(stg.ap(), xT[c * 128:(c + 1) * 128, g * GW:(g + 1) * GW], stg.key())
            return stg

        pi = [0]
        _xpl = [(c, g) for g in range(G) for c in range(DC)]
        _xis = [0]

        def x_prefetch(upto):
            while _xis[0] < min(upto, len(_xpl)):
                i = _xis[0]
                c, g = _xpl[i]
                stg = wstage.sub((i % 4) * 512, GW)
                dma_in(stg.ap(), xT[c * 128:(c + 1) * 128, g * GW:(g + 1) * GW], stg.key())
                _xis[0] += 1

        def xnext():
            i = pi[0]
            x_prefetch(i + 4)
            stg = wstage.sub((i % 4) * 512, GW)
            pi[0] += 1
            return stg

        def ln1_src(c, g, par):
            stg = xnext()
            a, b = c * S + g * GW, c * S + (g + 1) * GW
            vop(lambda e, s=stg, a=a, b=b: e.tensor_copy(xn.ap(a, b), s.ap()),
                [stg.key()], [xn.key(a, b)])
            return (xn.ap(a, b), xn.key(a, b), xn.ap(a, b), xn.key(a, b))

        stats_from(ln1_src, nslots=4)
        for g in range(G):
            aap, ak, bap, bk = bcast_AB(g)
            for c in range(DC):
                pump(1)
                dk = xn.key(c * S + g * GW, c * S + (g + 1) * GW)
                tb0 = (c % 4) * 512
                vop(lambda e, c=c, g=g, a=aap, tb0=tb0: e.tensor_mul(
                    tbuf.ap(tb0, tb0 + GW), a, xn.ap(c * S + g * GW, c * S + (g + 1) * GW)),
                    [dk, ak], [tbuf.key(tb0, tb0 + GW)])
                vop(lambda e, c=c, g=g, b=bap, tb0=tb0: e.tensor_add(
                    xn.ap(c * S + g * GW, c * S + (g + 1) * GW), b, tbuf.ap(tb0, tb0 + GW)),
                    [tbuf.key(tb0, tb0 + GW), bk], [dk])

        if any(d[0] == "xn" for d in debug):
            for c in range(DC):
                vop(lambda e, c=c: e.tensor_copy(tbuf.ap(), xn.ap(c * S, (c + 1) * S)),
                    [xn.key(c * S, (c + 1) * S)], [tbuf.key()])
                dma_out(outs["dbg_xn"][c * 128:(c + 1) * 128, :], tbuf.ap(), tbuf.key())

        def mix_group(c, g):
            src_off = (g * GW + 1024) % S
            xck = xn.key(c * S + src_off, c * S + src_off + GW)
            xnk = xn.key(c * S + g * GW, c * S + (g + 1) * GW)
            dap, dk = tbuf.ap(0, GW), tbuf.key(0, GW)
            vop(lambda e, c=c, x=xn.ap(c * S + src_off, c * S + src_off + GW), d=dap:
                e.tensor_scalar(d, x, ppv(gsig, c), None, Alu.mult), [xck, gsig.key()], [dk])
            vop(lambda e, c=c, x=xn.ap(c * S + g * GW, c * S + (g + 1) * GW), d=dap:
                e.scalar_tensor_tensor(d, x, ppv(gsig1m, c), d, Alu.mult, Alu.add),
                [xnk, gsig1m.key(), dk], [dk])
            return dap, dk

        def shift_src(c, g, par):
            dap, dk = mix_group(c, g)
            t0, t1 = par * GW, (par + 1) * GW
            vop(lambda e, s=dap, a=t0, b=t1: e.tensor_copy(tb2b.ap(a, b), s),
                [dk], [tb2b.key(t0, t1)])
            return (dap, dk, tb2b.ap(t0, t1), tb2b.key(t0, t1))

        stats_from(shift_src)
        vop(lambda e: e.tensor_copy(wdec.ap(), ppv(wv, p0=0, p1=64).broadcast_to([64, S])),
            [wv.key()], [wdec.key()])
        for g in range(G):
            aap, ak, bap, bk = bcast_AB(g)
            for c in range(DC):
                pump(1)
                dap, dk = mix_group(c, g)
                vop(lambda e, d=dap, a=aap: e.tensor_mul(tbuf2.ap(0, GW), a, d),
                    [dk, ak], [tbuf2.key()])
                zk = zb.key(c * GW, (c + 1) * GW)
                vop(lambda e, c=c, b=bap: e.tensor_add(zb.ap(c * GW, (c + 1) * GW), b, tbuf2.ap(0, GW)),
                    [tbuf2.key(), bk], [zk])
            for c in range(DC):
                pop_(lambda e, c=c: e.matmul(psC_t[0:64, :], kp_bf.ap(c * N, (c + 1) * N),
                                             zb.ap(c * GW, (c + 1) * GW),
                                             start=(c == 0), stop=(c == DC - 1)),
                     [zb.key(c * GW, (c + 1) * GW), kp_bf.key(c * N, (c + 1) * N)], [kC])
            for c in range(DC):
                pop_(lambda e, c=c: e.matmul(psT_t[0:64, :], vpw_bf.ap(c * N, (c + 1) * N),
                                             zb.ap(c * GW, (c + 1) * GW),
                                             start=(c == 0), stop=(c == DC - 1)),
                     [zb.key(c * GW, (c + 1) * GW), vpw_bf.key(c * N, (c + 1) * N)], [kT])
            aop(lambda e: e.activation(ekbuf.ap(), psC_t[0:64, :], Act.Exp,
                                       bias=ppv(ekb, p0=0, p1=64), scale=ppv(negetf, p0=0, p1=64)),
                [kC, ekb.key(), negetf.key()], [ekbuf.key()])
            vop(lambda e, g=g: e.scalar_tensor_tensor(
                kvf.ap(g * GW, (g + 1) * GW), psT_t[0:64, :], ppv(bvv, p0=0, p1=64),
                ekbuf.ap(), Alu.add, Alu.mult),
                [kT, bvv.key(), ekbuf.key()], [kvf.key(g * GW, (g + 1) * GW)])
        vop(lambda e: e.tensor_tensor_scan(stf.ap(), wdec.ap(), kvf.ap(), 0.0, Alu.mult, Alu.add),
            [wdec.key(), kvf.key()], [stf.key()])
        vop(lambda e: e.tensor_copy(states_bf.ap(), stf.ap()), [stf.key()], [states_bf.key()])
        if any(d[0] == "states" for d in debug):
            dma_out(outs["dbg_states"], stf.ap(), stf.key())

        pump(200)
        assert pump_gen[0] is None, "W1/W2 abs pump must be drained here (frees free8192)" 
        finish_scale("w1", D * FF, 1e-8, "max")
        finish_scale("w2", D * FF, 1e-8, "max")
        pump_gen[0] = itertools.chain(
            stream_abs_gen(wxt, 2 * D, D, scl["wx"]["sum"], 1024, sqbf, True),
            stream_abs_gen(wdtt, D, D, scl["wdt"]["sum"], 1024, sqbf, True),
            stream_abs_gen(wft, D, D, scl["wf"]["sum"], 1024, sqbf, True))
        QE = FF // 8

        def tern_thunks(e8):
            buf = e8 % 2
            w1q8 = warena.sub(buf * 8192, 4096)
            w2q8 = warena.sub(buf * 8192 + 4096, 4096)
            th = []
            for c in range(DC):
                def t1(c=c, buf=buf, w1q8=w1q8, e8=e8):
                    stg = wstage.sub((c % 2) * 1024, 512)
                    dma_in(stg.ap(0, 512), w1t[c * 128:(c + 1) * 128, e8 * QE:(e8 + 1) * QE],
                           stg.key(0, 512))
                    ternarize(w1q8, c * 512, stg, 0, 512, scl["w1"]["r"], scl["w1"]["negr"],
                              free8192.sub((c % 2) * 512, 512), tbuf2)
                th.append(t1)
            for j4 in range(4):
                def t2(j4=j4, buf=buf, w2q8=w2q8, e8=e8):
                    fc = e8 * 4 + j4
                    stg = wstage.sub((j4 % 2) * 1024, 1024)
                    dma_in(stg.ap(), w2t[fc * 128:(fc + 1) * 128, :], stg.key())
                    ternarize(w2q8, j4 * 1024, stg, 0, 1024, scl["w2"]["r"], scl["w2"]["negr"],
                              tbuf.sub((j4 % 2) * 1024, 1024), tbuf2)
                th.append(t2)
            return th

        for th in tern_thunks(0):
            th()
        for e8 in range(8):
            buf = e8 % 2
            w1q8 = warena.sub(buf * 8192, 4096)
            w2q8 = warena.sub(buf * 8192 + 4096, 4096)
            h8 = hq.sub(buf * 2048, 2048)
            nxt = tern_thunks(e8 + 1) if e8 + 1 < 8 else []
            for g in range(G):
                for th in nxt[g * 3:(g + 1) * 3]:
                    th()
                pump(1)
                for j in range(4):
                    for c in range(DC):
                        pop_(lambda e, j=j, c=c, g=g, w1q8=w1q8: e.matmul(
                            psA_t[:, j * 512:(j + 1) * 512],
                            w1q8.ap(c * 512 + j * 128, c * 512 + (j + 1) * 128),
                            xn.ap(c * S + g * GW, c * S + (g + 1) * GW),
                            start=(c == 0), stop=(c == DC - 1)),
                            [w1q8.key(c * 512 + j * 128, c * 512 + (j + 1) * 128),
                             xn.key(c * S + g * GW, c * S + (g + 1) * GW)],
                            [kA(j * 512, (j + 1) * 512)])
                for j in range(4):
                    aop(lambda e, j=j, e8=e8, h8=h8: e.activation(
                        h8.ap(j * GW, (j + 1) * GW), psA_t[:, j * 512:(j + 1) * 512],
                        Act.Gelu, bias=ppv(b1_v, e8 * 4 + j)),
                        [kA(j * 512, (j + 1) * 512), b1_v.key()],
                        [h8.key(j * GW, (j + 1) * GW)])
                for c in range(DC):
                    pbk = (c % 2) * 512
                    for j in range(4):
                        pop_(lambda e, c=c, j=j, pbk=pbk, w2q8=w2q8, h8=h8: e.matmul(
                            psB_t[:, pbk:pbk + 512],
                            w2q8.ap(j * 1024 + c * 128, j * 1024 + (c + 1) * 128),
                            h8.ap(j * GW, (j + 1) * GW),
                            start=(j == 0), stop=(j == 3)),
                            [w2q8.key(j * 1024 + c * 128, j * 1024 + (c + 1) * 128),
                             h8.key(j * GW, (j + 1) * GW)], [kB(pbk, pbk + 512)])
                    oa = ro_r(c, g)
                    if e8 == 0:
                        vop(lambda e, c=c, pbk=pbk, oa=oa: e.tensor_scalar(
                            oa.ap(), psB_t[:, pbk:pbk + 512], ppv(b2_v, c), None, Alu.add),
                            [kB(pbk, pbk + 512), b2_v.key()], [oa.key()])
                    else:
                        vop(lambda e, pbk=pbk, oa=oa: e.tensor_add(
                            oa.ap(), psB_t[:, pbk:pbk + 512], oa.ap()),
                            [kB(pbk, pbk + 512), oa.key()], [oa.key()])
        pump(200)
        if any(d[0] == "ffn" for d in debug):
            for g in range(G):
                for c in range(DC):
                    oa = ro_r(c, g)
                    dma_out(outs["dbg_ffn"][c * 128:(c + 1) * 128, g * GW:(g + 1) * GW],
                            oa.ap(), oa.key())
        ep_xslots = [wstage.sub(p * 512, GW) for p in range(4)] + \
                    [free8192.sub(p * 512, GW) for p in range(2)]
        ep_silu = [F_(PH + 11776, GW), F_(PH + 12288, GW)]
        ep_cg = [(g, c) for g in range(G) for c in range(DC)]
        p5slots = [srA.sub(0, 1024), srA.sub(1024, 1024), srB.sub(0, 1024), srB.sub(1024, 1024)]
        p5src = [(wdtt, c) for c in range(DC)] + [(wft, c) for c in range(DC)]

        def _p5dma(i):
            hbm, c = p5src[i]
            stg = p5slots[i % 4]
            dma_in(stg.ap(), hbm[c * 128:(c + 1) * 128, :], stg.key())
        p5dma = [lambda i=i: _p5dma(i) for i in range(len(p5src))]

        def ep_xdma(i):
            g, c = ep_cg[i]
            stg = ep_xslots[i % 6]
            dma_in(stg.ap(), xT[c * 128:(c + 1) * 128, g * GW:(g + 1) * GW], stg.key())

        for i in range(6):
            ep_xdma(i)
        for i, (g, c) in enumerate(ep_cg):
            psX, kX = (psC_t, kC) if i % 2 == 0 else (psT_t, kT)
            pop_(lambda e, c=c, g=g, p=psX: e.matmul(
                p[:, :], owt_bf.ap(c * 128, (c + 1) * 128),
                states_bf.ap(g * GW, (g + 1) * GW), start=True, stop=True),
                [owt_bf.key(c * 128, (c + 1) * 128), states_bf.key(g * GW, (g + 1) * GW)], [kX])
            oa = ro_r(c, g)
            sil = ep_silu[i % 2]
            aop(lambda e, oa=oa, sil=sil: e.activation(sil.ap(), oa.ap(), Act.Silu),
                [oa.key()], [sil.key()])
            stg = ep_xslots[i % 6]
            tb0 = (i % 4) * 512
            vop(lambda e, p=psX, sil=sil, tb0=tb0: e.tensor_mul(tbuf.ap(tb0, tb0 + GW),
                                                                p[:, :], sil.ap()),
                [kX, sil.key()], [tbuf.key(tb0, tb0 + GW)])
            vop(lambda e, s=stg, tb0=tb0: e.tensor_add(tbuf.ap(tb0, tb0 + GW),
                                                       tbuf.ap(tb0, tb0 + GW), s.ap()),
                [tbuf.key(tb0, tb0 + GW), stg.key()], [tbuf.key(tb0, tb0 + GW)])
            if i + 6 < len(ep_cg):
                ep_xdma(i + 6)
            dma_out(x1sp[c * 128:(c + 1) * 128, g * GW:(g + 1) * GW],
                    tbuf.ap(tb0, tb0 + GW), tbuf.key(tb0, tb0 + GW),
                    hbmkey=("x1sp", c * S + g * GW, c * S + (g + 1) * GW))
            vop(lambda e, oa=oa, tb0=tb0: e.tensor_copy(oa.ap(), tbuf.ap(tb0, tb0 + GW)),
                [tbuf.key(tb0, tb0 + GW)], [oa.key()])
            if any(d[0] == "x1" for d in debug):
                dma_out(outs["dbg_x1"][c * 128:(c + 1) * 128, g * GW:(g + 1) * GW],
                        tbuf.ap(tb0, tb0 + GW), tbuf.key(tb0, tb0 + GW))

        if any(d[0] == "w1q" for d in debug):
            dma_out(outs["dbg_w1q"], warena.ap(0, 8192), warena.key(0, 8192))
            dma_out(outs["dbg_w2q"], warena.ap(8192, 16384), warena.key(8192, 16384))
        if any(d[0] == "hq" for d in debug):
            dma_out(outs["dbg_hq"], hq.ap(), hq.key())
        if any(d[0] == "vec1" for d in debug):
            vop(lambda e: e.tensor_copy(srA.ap(0, 640), pf_t[:, 0:640]), [("pf", 0, 640)], [srA.key(0, 640)])
            dma_out(outs["dbg_vec1"], srA.ap(0, 640), srA.key(0, 640))
        if any(d[0] == "pn" for d in debug):
            dma_out(outs["dbg_pn"][:, 0:512], tbuf.ap(0, 512), tbuf.key(0, 512))
            dma_out(outs["dbg_pn"][:, 512:1024], tbuf2.ap(0, 512), tbuf2.key(0, 512))
            for i, nm in enumerate(("w1", "w2")):
                for j, f in enumerate(("scale", "r", "negr")):
                    vop(lambda e, i=i, j=j, nm=nm, f=f: e.tensor_copy(
                        tbuf.ap(600 + i * 3 + j, 601 + i * 3 + j), scl[nm][f].ap()),
                        [scl[nm][f].key()], [tbuf.key(600 + i * 3 + j, 601 + i * 3 + j)])
            dma_out(outs["dbg_pn"][:, 1024:1030], tbuf.ap(600, 606), tbuf.key(600, 606))
        if any(d[0] == "sums" for d in debug):
            for i, nm in enumerate(("w1", "w2", "wx", "wdt", "wf")):
                vop(lambda e, i=i, nm=nm: e.tensor_copy(tbuf2.ap(i, i + 1), scl[nm]["sum"].ap()),
                    [scl[nm]["sum"].key()], [tbuf2.key(i, i + 1)])
            dma_out(outs["dbg_sums"], tbuf2.ap(0, 5), tbuf2.key(0, 5))
        finish_scale("wx", 2 * D * D, 1e-5, "add")
        finish_scale("wdt", D * D, 1e-5, "add")
        finish_scale("wf", D * D, 1e-5, "max")
        wdtq = warena.sub(0, 8192)
        wfq = warena.sub(8192, 8192)
        p5sa = [free8192.sub(0, 512), free8192.sub(512, 512), tern7a, tern7b]
        for i in range(4):
            p5dma[i]()
        for i in range(len(p5src)):
            if i + 4 < len(p5src):
                p5dma[i + 4]()
            stg = p5slots[i % 4]
            dstq = wdtq if i < DC else wfq
            nm = "wdt" if i < DC else "wf"
            c = p5src[i][1]
            for pc in range(2):
                ternarize(dstq, c * 1024 + pc * 512, stg, pc * 512, 512,
                          scl[nm]["r"], scl[nm]["negr"], p5sa[(i * 2 + pc) % 4], tbuf2)

        assert pump_gen[0] is None, "L1 abs pump must drain before P6 uses xf1 region"
        def x1_src(c, g, par):
            oa = ro_r(c, g)
            return (oa.ap(), oa.key(), oa.ap(), oa.key())

        stats_from(x1_src, nslots=8)
        for g in range(G):
            aap, ak, bap, bk = bcast_AB(g)
            fk = fold.key(g * GW, (g + 1) * GW)
            for c in range(DC):
                oa = ro_r(c, g)
                t0 = (c % 4) * 512
                xk = xf1.key(t0, t0 + GW)
                xnk = xn.key(c * S + g * GW, c * S + (g + 1) * GW)
                xnap = xn.ap(c * S + g * GW, c * S + (g + 1) * GW)
                vop(lambda e, oa=oa, t0=t0, a=aap: e.tensor_mul(xf1.ap(t0, t0 + GW), a, oa.ap()),
                    [oa.key(), ak], [xk])
                vop(lambda e, t0=t0, b=bap, xnap=xnap: e.tensor_add(xnap, b, xf1.ap(t0, t0 + GW)),
                    [xk, bk], [xnk])
                if c == 0:
                    vop(lambda e, g=g, xnap=xnap: e.scalar_tensor_tensor(
                        fold.ap(g * GW, (g + 1) * GW), xnap, -1.0, xnap, Alu.mult, Alu.max),
                        [xnk], [fk])
                else:
                    vop(lambda e, g=g, xnap=xnap: e.tensor_max(fold.ap(g * GW, (g + 1) * GW),
                                                               fold.ap(g * GW, (g + 1) * GW), xnap),
                        [xnk, fk], [fk])
                    vop(lambda e, g=g, xnap=xnap: e.scalar_tensor_tensor(
                        fold.ap(g * GW, (g + 1) * GW), xnap, -1.0,
                        fold.ap(g * GW, (g + 1) * GW), Alu.mult, Alu.max),
                        [xnk, fk], [fk])
        for bI in range(16):
            top_(lambda e, b_=bI: e.transpose(psT_t[:, 0:128],
                                              fold.ap(b_ * 128, (b_ + 1) * 128), idreg.ap()),
                 [fold.key(bI * 128, (bI + 1) * 128), idreg.key()], [kT])
            vop(lambda e, b_=bI: e.tensor_reduce(amaxt.ap(b_, b_ + 1), psT_t[:, 0:128], AxX, Alu.max),
                [kT], [amaxt.key(bI, bI + 1)])
        vop(lambda e: e.tensor_scalar(amaxt.ap(), amaxt.ap(), 1e-5, None, Alu.max),
            [amaxt.key()], [amaxt.key()])
        rop(sptk.ap(), amaxt.ap(), [amaxt.key()], [sptk.key()])
        vop(lambda e: e.tensor_scalar(sptk.ap(), sptk.ap(), 127.0, None, Alu.mult),
            [sptk.key()], [sptk.key()])
        vop(lambda e: e.tensor_scalar(rhot.ap(), amaxt.ap(), 1.0 / 127.0, None, Alu.mult),
            [amaxt.key()], [rhot.key()])
        if any(d[0] == "fold" for d in debug):
            dma_out(outs["dbg_fold"], fold.ap(), fold.key())
            dma_out(outs["dbg_amax"], amaxt.ap(), amaxt.key())
        dma_in(onehreg.ap(), c_oneh, onehreg.key())
        top_(lambda e: e.transpose(psT_t[0:16, 0:128], sptk.ap(), idreg.ap()),
             [sptk.key(), idreg.key()], [kT])
        vop(lambda e: e.tensor_copy(v16sp.ap(), psT_t[0:16, 0:128]), [kT], [v16sp.key()])
        top_(lambda e: e.transpose(psT_t[0:16, 0:128], rhot.ap(), idreg.ap()),
             [rhot.key(), idreg.key()], [kT])
        vop(lambda e: e.tensor_copy(v16rh.ap(), psT_t[0:16, 0:128]), [kT], [v16rh.key()])
        for bI in range(16):
            pop_(lambda e, b_=bI: e.matmul(psA_t[:, b_ * 128:(b_ + 1) * 128],
                                           onehreg.ap(b_ * 128, (b_ + 1) * 128), v16sp.ap(),
                                           start=True, stop=True),
                 [onehreg.key(), v16sp.key()], [kA(bI * 128, (bI + 1) * 128)])
        aop(lambda e: e.activation(spbc.ap(), psA_t[:, :], Act.Copy), [kA(0, 2048)], [spbc.key()])
        for bI in range(16):
            pop_(lambda e, b_=bI: e.matmul(psA_t[:, b_ * 128:(b_ + 1) * 128],
                                           onehreg.ap(b_ * 128, (b_ + 1) * 128), v16rh.ap(),
                                           start=True, stop=True),
                 [onehreg.key(), v16rh.key()], [kA(bI * 128, (bI + 1) * 128)])
        vop(lambda e: e.tensor_scalar(wsxr.ap(), psA_t[:, :], scl["wx"]["scale"].ap(), None, Alu.mult),
            [kA(0, 2048), scl["wx"]["scale"].key()], [wsxr.key()])
        if any(d[0] == "spbc" for d in debug):
            dma_out(outs["dbg_spbc"], pf_t[:, spbc.lo:spbc.lo + S], spbc.key())
        vop(lambda e: e.tensor_copy(sprow.ap(p1=1), r0(spbc, 0, S)), [spbc.key()], [sprow.key()])
        dma_in(r0(tbuf, 0, 2 * D), v_bx.unsqueeze(0), tbuf.key())
        vop(lambda e: e.tensor_scalar(bxrow.ap(p1=1), r0(tbuf, 0, 2 * D),
                                      ppv(scl["wx"]["r"], 0, 0, 1), None, Alu.mult),
            [tbuf.key(), scl["wx"]["r"].key()], [bxrow.key()])
        dma_in(r0(tbuf, 0, D), v_bdt.unsqueeze(0), tbuf.key(0, D))
        vop(lambda e: e.tensor_scalar(bdrow.ap(p1=1), r0(tbuf, 0, D),
                                      ppv(scl["wdt"]["r"], 0, 0, 1), None, Alu.mult),
            [tbuf.key(0, D), scl["wdt"]["r"].key()], [bdrow.key()])
        for g in range(G):
            for c in range(DC):
                oa = ro_r(c, g)
                t0 = (c % 4) * 512
                xk = xf1.key(t0, t0 + GW)
                xnk = xn.key(c * S + g * GW, c * S + (g + 1) * GW)
                xnap = xn.ap(c * S + g * GW, c * S + (g + 1) * GW)
                vop(lambda e, g=g, t0=t0, xnap=xnap: e.tensor_mul(
                    xf1.ap(t0, t0 + GW), xnap, spbc.ap(g * GW, (g + 1) * GW)),
                    [xnk, spbc.key()], [xk])
                vop(lambda e, t0=t0: e.tensor_scalar(xf1.ap(t0, t0 + GW), xf1.ap(t0, t0 + GW),
                                                     Mb_v.ap(), nMb_v.ap(), Alu.add, Alu.add),
                    [xk, Mb_v.key(), nMb_v.key()], [xk])
                vop(lambda e, oa=oa, t0=t0: e.tensor_scalar(oa.ap(), xf1.ap(t0, t0 + GW),
                                                            -128.0, 127.0, Alu.max, Alu.min),
                    [xk], [oa.key()])
        if any(d[0] == "rq" for d in debug):
            for c in range(DC):
                for g in range(G):
                    oa = ro_r(c, g)
                    dma_out(outs["dbg_rq"][c * 128:(c + 1) * 128, g * GW:(g + 1) * GW], oa.ap(), oa.key())
        for bI in range(16):
            pop_(lambda e, b_=bI: e.matmul(psA_t[:, b_ * 128:(b_ + 1) * 128],
                                           onehreg.ap(b_ * 128, (b_ + 1) * 128), v16rh.ap(),
                                           start=True, stop=True),
                 [onehreg.key(), v16rh.key()], [kA(bI * 128, (bI + 1) * 128)])
        vop(lambda e: e.tensor_scalar(wsdr.ap(), psA_t[:, :], scl["wdt"]["scale"].ap(), None, Alu.mult),
            [kA(0, 2048), scl["wdt"]["scale"].key()], [wsdr.key()])

        vop(lambda e: e.memset(zerot.ap(), 0.0), writes=[zerot.key()])
        b1r, b2r, b3r, b4r, b5r = Bs

        def wx_dma(oc, par):
            stg = wstage.sub(par * 1024, 1024)
            for q in range(4):
                dma_in(stg.ap(q * 256, (q + 1) * 256),
                       wxt[oc * 128:(oc + 1) * 128, q * 256:(q + 1) * 256],
                       stg.key(q * 256, (q + 1) * 256))

        def wx_tern(oc, par):
            stg = wstage.sub(par * 1024, 1024)
            wq = wxs.sub(par * 1024, 1024)
            for pc in range(2):
                ternarize(wq, pc * 512, stg, pc * 512, 512,
                          scl["wx"]["r"], scl["wx"]["negr"], (tern7a, tern7b)[pc], tbuf2)

        def l1_mm(c):
            for g in range(G):
                for dc2 in range(DC):
                    pop_(lambda e, c=c, g=g, dc2=dc2: e.matmul(
                        psA_t[:, g * 512:(g + 1) * 512],
                        wdtq.ap(dc2 * 1024 + c * 128, dc2 * 1024 + (c + 1) * 128),
                        ro_r(dc2, g).ap(), start=(dc2 == 0), stop=False),
                        [wdtq.key(dc2 * 1024 + c * 128, dc2 * 1024 + (c + 1) * 128),
                         ro_r(dc2, g).key()], [kA(g * 512, (g + 1) * 512)])
                pop_(lambda e, c=c, g=g: e.matmul(
                    psA_t[:, g * 512:(g + 1) * 512], bdrow.ap(c * 128, (c + 1) * 128, 0, 1),
                    sprow.ap(g * GW, (g + 1) * GW, 0, 1), start=False, stop=True),
                    [bdrow.key(c * 128, (c + 1) * 128), sprow.key(g * GW, (g + 1) * GW)],
                    [kA(g * 512, (g + 1) * 512)])
                vop(lambda e, g=g: e.tensor_mul(b1r.ap(g * 512, (g + 1) * 512),
                                                psA_t[:, g * 512:(g + 1) * 512],
                                                wsdr.ap(g * 512, (g + 1) * 512)),
                    [kA(g * 512, (g + 1) * 512), wsdr.key()], [b1r.key(g * 512, (g + 1) * 512)])
            for oc, dst, par in ((c, b2r, 0), (c + 8, b3r, 1)):
                wq = wxs.sub(par * 1024, 1024)
                for g in range(G):
                    for dc2 in range(DC):
                        pop_(lambda e, g=g, dc2=dc2, wq=wq: e.matmul(
                            psA_t[:, g * 512:(g + 1) * 512], wq.ap(dc2 * 128, (dc2 + 1) * 128),
                            ro_r(dc2, g).ap(), start=(dc2 == 0), stop=False),
                            [wq.key(dc2 * 128, (dc2 + 1) * 128), ro_r(dc2, g).key()],
                            [kA(g * 512, (g + 1) * 512)])
                    pop_(lambda e, oc=oc, g=g: e.matmul(
                        psA_t[:, g * 512:(g + 1) * 512], bxrow.ap(oc * 128, (oc + 1) * 128, 0, 1),
                        sprow.ap(g * GW, (g + 1) * GW, 0, 1), start=False, stop=True),
                        [bxrow.key(oc * 128, (oc + 1) * 128), sprow.key(g * GW, (g + 1) * GW)],
                        [kA(g * 512, (g + 1) * 512)])
                    vop(lambda e, dst=dst, g=g: e.tensor_mul(dst.ap(g * 512, (g + 1) * 512),
                                                             psA_t[:, g * 512:(g + 1) * 512],
                                                             wsxr.ap(g * 512, (g + 1) * 512)),
                        [kA(g * 512, (g + 1) * 512), wsxr.key()],
                        [dst.key(g * 512, (g + 1) * 512)])
                    vop(lambda e, dst=dst, g=g: e.tensor_scalar(dst.ap(g * 512, (g + 1) * 512),
                                                                dst.ap(g * 512, (g + 1) * 512),
                                                                5.0, -5.0, Alu.min, Alu.max),
                        [dst.key(g * 512, (g + 1) * 512)], [dst.key(g * 512, (g + 1) * 512)])
            if any(d[0] == "l1t" for d in debug) and c == 0:
                dma_out(outs["dbg_l1t"][:, 0:S], b1r.ap(), b1r.key())
                dma_out(outs["dbg_l1t"][:, S:2 * S], b2r.ap(), b2r.key())
                dma_out(outs["dbg_l1t"][:, 2 * S:3 * S], b3r.ap(), b3r.key())
        def l1_chain(c, tern_thunk=None):
            for fn, scale, bias in ((Act.Abs, None, None), (Act.Exp, -1.0, None),
                                    (Act.Ln, None, 1.0)):
                for g in range(G):
                    s_, src_ = ((b1r, b5r) if fn == Act.Abs else (b5r, b5r))
                    aop(lambda e, fn=fn, scale=scale, bias=bias, g=g, s_=s_:
                        e.activation(b5r.ap(g * 512, (g + 1) * 512),
                                     s_.ap(g * 512, (g + 1) * 512) if fn != Act.Abs
                                     else b1r.ap(g * 512, (g + 1) * 512), fn,
                                     **({} if scale is None else {"scale": scale}),
                                     **({} if bias is None else {"bias": bias})),
                        [(b1r if fn == Act.Abs else b5r).key(g * 512, (g + 1) * 512)],
                        [b5r.key(g * 512, (g + 1) * 512)])
            if tern_thunk is not None:
                tern_thunk()
            vop(lambda e: e.scalar_tensor_tensor(b1r.ap(), b1r.ap(), 0.0, b5r.ap(),
                                                 Alu.max, Alu.add), [b1r.key(), b5r.key()], [b1r.key()])
            vop(lambda e: e.tensor_scalar(b1r.ap(), b1r.ap(), 0.001, 1.0, Alu.max, Alu.min),
                [b1r.key()], [b1r.key()])
            vop(lambda e, c=c: e.tensor_scalar(b4r.ap(), b1r.ap(), ppv(aneg, c), -10.0,
                                               Alu.mult, Alu.max), [b1r.key(), aneg.key()], [b4r.key()])
            aop(lambda e: e.activation(b4r.ap(), b4r.ap(), Act.Exp), [b4r.key()], [b4r.key()])
            aop(lambda e: e.activation(b3r.ap(), b3r.ap(), Act.Tanh), [b3r.key()], [b3r.key()])
            vop(lambda e: e.tensor_mul(b2r.ap(), b1r.ap(), b2r.ap()), [b1r.key(), b2r.key()], [b2r.key()])
            vop(lambda e: e.tensor_tensor_scan(b4r.ap(), b4r.ap(), zerot.ap(), 1.0,
                                               Alu.mult, Alu.add), [b4r.key(), zerot.key()], [b4r.key()])
            if any(d[0] == "l1t" for d in debug) and c == 0:
                dma_out(outs["dbg_l1t"][:, 3 * S:4 * S], b4r.ap(), b4r.key())
            vop(lambda e: e.tensor_copy(b5r.ap(0, 1), S_v.ap()), [S_v.key()], [b5r.key(0, 1)])
            vop(lambda e: e.tensor_scalar(b5r.ap(1, S), b4r.ap(0, S - 1), S_v.ap(), F_v.ap(),
                                          Alu.mult, Alu.max),
                [b4r.key(), S_v.key(), F_v.key()], [b5r.key(1, S)])
            rop(b5r.ap(), b5r.ap(), [b5r.key()], [b5r.key()])
            vop(lambda e: e.tensor_mul(b2r.ap(), b2r.ap(), b5r.ap()), [b2r.key(), b5r.key()], [b2r.key()])
            vop(lambda e: e.tensor_scalar(b5r.ap(), b2r.ap(), T_v.ap(), nT_v.ap(), Alu.min, Alu.max),
                [b2r.key(), T_v.key(), nT_v.key()], [b5r.key()])
            vop(lambda e: e.tensor_tensor(b1r.ap(), b5r.ap(), b2r.ap(), Alu.is_equal),
                [b5r.key(), b2r.key()], [b1r.key()])
            vop(lambda e: e.tensor_scalar(b2r.ap(), b5r.ap(), G_v.ap(), None, Alu.mult),
                [b5r.key(), G_v.key()], [b2r.key()])
            vop(lambda e: e.tensor_tensor_scan(b2r.ap(), b2r.ap(), zerot.ap(), 0.0,
                                               Alu.add, Alu.add), [b2r.key(), zerot.key()], [b2r.key()])
            vop(lambda e: e.scalar_tensor_tensor(b5r.ap(), b2r.ap(), -1.0, b2r.ap(),
                                                 Alu.mult, Alu.max), [b2r.key()], [b5r.key()])
            vop(lambda e: e.tensor_scalar(b5r.ap(), b5r.ap(), U_v.ap(), None, Alu.is_le),
                [b5r.key(), U_v.key()], [b5r.key()])
            vop(lambda e: e.tensor_tensor_scan(b1r.ap(), b1r.ap(), b5r.ap(), 4096.0,
                                               Alu.mult, Alu.mult), [b1r.key(), b5r.key()], [b1r.key()])
            vop(lambda e: e.tensor_mul(b2r.ap(), b4r.ap(), b2r.ap()), [b4r.key(), b2r.key()], [b2r.key()])
            vop(lambda e: e.tensor_scalar(b2r.ap(), b2r.ap(), U_v.ap(), nU_v.ap(), Alu.min, Alu.max),
                [b2r.key(), U_v.key(), nU_v.key()], [b2r.key()])
            vop(lambda e: e.tensor_mul(b2r.ap(), b2r.ap(), b3r.ap()), [b2r.key(), b3r.key()], [b2r.key()])
            vop(lambda e: e.tensor_mul(b2r.ap(), b2r.ap(), b1r.ap()), [b2r.key(), b1r.key()], [b2r.key()])
            if any(d[0] == "l1t" for d in debug) and c == 0:
                dma_out(outs["dbg_l1t"][:, 4 * S:5 * S], b2r.ap(), b2r.key())
            g1sl = [(psC_t[:, :], kC), (psT_t[:, :], kT),
                    (free8192.ap(0, GW), free8192.key(0, GW)),
                    (free8192.ap(GW, 2 * GW), free8192.key(GW, 2 * GW))]
            for g in range(G):
                pbk = (g % 2) * 512
                for dc2 in range(DC):
                    pop_(lambda e, c=c, g=g, dc2=dc2, pbk=pbk: e.matmul(
                        psB_t[:, pbk:pbk + 512],
                        wfq.ap(dc2 * 1024 + c * 128, dc2 * 1024 + (c + 1) * 128),
                        xn.ap(dc2 * S + g * GW, dc2 * S + (g + 1) * GW),
                        start=(dc2 == 0), stop=(dc2 == DC - 1)),
                        [wfq.key(dc2 * 1024 + c * 128, dc2 * 1024 + (c + 1) * 128),
                         xn.key(dc2 * S + g * GW, dc2 * S + (g + 1) * GW)], [kB(pbk, pbk + 512)])
                gap, gk = g1sl[g]
                aop(lambda e, g=g, pbk=pbk, gap=gap: e.activation(
                    gap, psB_t[:, pbk:pbk + 512], Act.Silu,
                    scale=scl["wf"]["scale"].ap()),
                    [kB(pbk, pbk + 512), scl["wf"]["scale"].key()], [gk])
            for g in range(G):
                gap, gk = g1sl[g]
                vop(lambda e, g=g, gap=gap: e.tensor_mul(b2r.ap(g * GW, (g + 1) * GW),
                                                         b2r.ap(g * GW, (g + 1) * GW), gap),
                    [b2r.key(g * GW, (g + 1) * GW), gk], [b2r.key(g * GW, (g + 1) * GW)])
            if any(d[0] == "l1t" for d in debug) and c == 0:
                for g in range(G):
                    gap, gk = g1sl[g]
                    vop(lambda e, g=g, gap=gap: e.tensor_copy(b4r.ap(g * GW, (g + 1) * GW), gap),
                        [gk], [b4r.key(g * GW, (g + 1) * GW)])
                dma_out(outs["dbg_l1t"][:, 5 * S:6 * S], b4r.ap(), b4r.key())
            dma_rd(b5r.ap(), x1sp[c * 128:(c + 1) * 128, :], b5r.key(),
                   hbmkey=("x1sp", c * S, (c + 1) * S))
            vop(lambda e: e.tensor_add(b2r.ap(), b2r.ap(), b5r.ap()), [b2r.key(), b5r.key()], [b2r.key()])
            if any(d[0] == "y0" for d in debug) and c == 0:
                dma_out(outs["dbg_y0"], b2r.ap(), b2r.key())
            dma_out(outT[c * 128:(c + 1) * 128, :], b2r.ap(), b2r.key())

        wx_dma(0, 0)
        wx_dma(8, 1)
        wx_tern(0, 0)
        wx_tern(8, 1)
        wx_dma(1, 0)
        wx_dma(9, 1)
        for c in range(DC):
            l1_mm(c)
            def _tern(c=c):
                if c + 1 < DC:
                    wx_tern(c + 1, 0)
                    wx_tern(c + 9, 1)
                    if c + 2 < DC:
                        wx_dma(c + 2, 0)
                        wx_dma(c + 10, 1)
            l1_chain(c, tern_thunk=_tern)

        if stop is not None:
            st.truncate(stop)
        if any(d[0] == "vsync" for d in debug):
            allw = [(s_, st.cnt[s_]) for s_ in Sched.STREAMS]
            st.op("din0", (lambda e, sem: e.dma_start(out=outs["dbg_vsync"], in_=pf_t[:, 0:640])
                           .then_inc(sem, 16)), inc=16, extra=allw)
            st.op("dout0", (lambda e, sem: e.dma_start(out=outs["dbg_vgp"], in_=pf_t[:, 0:640])
                            .then_inc(sem, 16)), inc=16, extra=allw)
        st.emit(blk, sems)
    return nc, ins, outs


def _host_layout(inputs):
    f32 = np.float32
    common = dict(
        w1t=np.ascontiguousarray(np.asarray(inputs["L0_ffn_W1"], f32).T),
        w2t=np.ascontiguousarray(np.asarray(inputs["L0_ffn_W2"], f32).T),
        wxt=np.ascontiguousarray(np.asarray(inputs["L1_attn_W_x"], f32).T.reshape(8, 128, 16, 128).transpose(2, 1, 0, 3).reshape(2048, 1024)),
        wdtt=np.ascontiguousarray(np.asarray(inputs["L1_attn_W_dt"], f32).T),
        wft=np.ascontiguousarray(np.asarray(inputs["L1_ffn_weight"], f32).T),
        kwt=np.ascontiguousarray(np.asarray(inputs["L0_attn_key_w"], f32).T
                                 .reshape(8, 128, -1).transpose(1, 0, 2).reshape(128, -1)),
        vwt=np.ascontiguousarray(np.asarray(inputs["L0_attn_value_w"], f32).T
                                 .reshape(8, 128, -1).transpose(1, 0, 2).reshape(128, -1)),
        owt=np.ascontiguousarray(np.asarray(inputs["L0_attn_output_w"], f32).T),
        v_td=np.asarray(inputs["L0_attn_time_decay"], f32),
        v_tf=np.asarray(inputs["L0_attn_time_first"], f32),
        v_tsg=np.asarray(inputs["L0_attn_time_shift_gates"], f32),
        v_lnw=np.asarray(inputs["L0_attn_ln_w"], f32),
        v_lnb=np.asarray(inputs["L0_attn_ln_b"], f32),
        v_b1=np.asarray(inputs["L0_ffn_b1"], f32),
        v_b2=np.asarray(inputs["L0_ffn_b2"], f32),
        v_bx=np.asarray(inputs["L1_attn_b_x"], f32),
        v_bdt=np.asarray(inputs["L1_attn_b_dt"], f32),
        v_alog=np.asarray(inputs["L1_attn_A_log"], f32),
        vpack=np.concatenate(
            [np.asarray(inputs[k], f32).reshape(-1, 128).T for k in
             ("L0_attn_time_shift_gates", "L0_attn_ln_w", "L0_attn_ln_b",
              "L1_attn_A_log", "L0_ffn_b2", "L0_ffn_b1")] +
            [np.pad(np.asarray(inputs[k], f32).reshape(-1, 64).T, ((0, 64), (0, 0)))
             for k in ("L0_attn_time_decay", "L0_attn_time_first")], axis=1),
        c_ident=np.eye(128, dtype=f32),
        c_ones_bf=np.ones((128, 1), NPBF),
        c_ones_f=np.ones((128, 1), f32),
        c_onesbc=np.ones((1, 128), f32),
    )
    oneh = np.zeros((16, S), f32)
    for i in range(16):
        oneh[i, i * 128:(i + 1) * 128] = 1.0
    common["c_oneh"] = oneh
    x = np.asarray(inputs["L0_attn_x"], f32)
    in_maps = []
    for c in range(B):
        m = dict(common)
        m["xT"] = np.ascontiguousarray(x[c].T)
        in_maps.append(m)
    return in_maps


def kernel(**inputs):
    in_maps = _host_layout(inputs)
    nc, ins, outs = build_core()
    res = run_bass_kernel_spmd(nc, in_maps, core_ids=list(range(B)))
    out = np.stack([np.asarray(r["outT"]).T for r in res.results], axis=0)
    return np.ascontiguousarray(out, dtype=np.float32)
